# version 12
# baseline (speedup 1.0000x reference)
import numpy as np
import concourse.bass as bass
import concourse.mybir as mybir
from concourse.bass_utils import run_bass_kernel_spmd

F32 = mybir.dt.float32
BF16 = mybir.dt.bfloat16
U32 = mybir.dt.uint32
I32 = mybir.dt.int32
AF = mybir.ActivationFunctionType
ALU = mybir.AluOpType

HID = 4096
INTER = 11008
TOK = 1024
NCORES = 8
HC = HID // 128
ICN = INTER // 128
NG = 32
ND = 86
GU_COLS = 2 * INTER
EPS = 1e-5
GW = 256

PERM = (8 * (np.arange(128) % 16) + np.arange(128) // 16).astype(np.int64)


class Sem:
    def __init__(self, nc, name, dma=False):
        self.name = name
        self.h = nc.semaphore(name).__enter__()
        self.step = 16 if dma else 1
        self.n = 0

    def inc(self):
        self.n += self.step
        return self.n


class Plan:
    ENGINES = ("sync", "scalar", "vector", "tensor", "gpsimd")

    def __init__(self):
        self.ops = {e: [] for e in self.ENGINES}

    def op(self, engine, fn, waits=(), inc=None):
        ws = []
        for wv in waits:
            if wv is None:
                continue
            s, v = wv
            assert v <= s.n, f"wait on {s.name} for {v} but only {s.n} planned"
            ws.append((s, v))
        self.ops[engine].append((ws, fn, inc))
        if inc is not None:
            inc.inc()

    def emit(self, engine_name, eng):
        waited = {}
        for ws, fn, inc in self.ops[engine_name]:
            for (s, v) in ws:
                if waited.get(s.name, 0) < v:
                    eng.wait_ge(s.h, v)
                    waited[s.name] = v
            inst = fn(eng)
            if inc is not None:
                inst.then_inc(inc.h, inc.step)


class Rot:

    def __init__(self, nslots, nc=None, prefix=None):
        self.n = nslots
        self.tok = [[] for _ in range(nslots)]
        self.i = 0
        self.sems = ([Sem(nc, f"{prefix}{i}", dma=True) for i in range(nslots)]
                     if nc is not None else None)

    def acquire(self):
        s = self.i % self.n
        self.i += 1
        toks = self.tok[s]
        self.tok[s] = []
        return s, toks

    def release(self, slot, tok):
        if tok is not None:
            self.tok[slot].append(tok)


def build_program():
    nc = bass.Bass("TRN2", target_bir_lowering=False,
                   detect_race_conditions=False)

    xT = nc.dram_tensor("xT", [HID, TOK], F32, kind="ExternalInput")
    qw_gu = nc.dram_tensor("qw_gu", [HID // 8, GU_COLS], U32, kind="ExternalInput")
    qz_gu = nc.dram_tensor("qz_gu", [NG, GU_COLS // 8], U32, kind="ExternalInput")
    sc_gu = nc.dram_tensor("sc_gu", [NG, GU_COLS], F32, kind="ExternalInput")
    qw_d = nc.dram_tensor("qw_d", [INTER // 8, HID], U32, kind="ExternalInput")
    qz_d = nc.dram_tensor("qz_d", [ND, HID // 8], U32, kind="ExternalInput")
    sc_d = nc.dram_tensor("sc_d", [ND, HID], F32, kind="ExternalInput")
    gidx_gu_p = nc.dram_tensor("gidx_gu_p", [HID], I32, kind="ExternalInput")
    gidx_d_p = nc.dram_tensor("gidx_d_p", [INTER], I32, kind="ExternalInput")
    gidx_gu_pt = nc.dram_tensor("gidx_gu_pt", [128, HC], I32, kind="ExternalInput")
    gidx_d_pt = nc.dram_tensor("gidx_d_pt", [128, ICN], I32, kind="ExternalInput")
    c_shift = nc.dram_tensor("c_shift", [128], U32, kind="ExternalInput")
    c_pw2 = nc.dram_tensor("c_pw2", [128], F32, kind="ExternalInput")
    c_iota32 = nc.dram_tensor("c_iota32", [NG], F32, kind="ExternalInput")
    c_iota86 = nc.dram_tensor("c_iota86", [ND], F32, kind="ExternalInput")
    c_ones = nc.dram_tensor("c_ones", [128], F32, kind="ExternalInput")
    xoutT = nc.dram_tensor("xoutT", [HID, TOK], F32, kind="ExternalOutput")

    wg_s = nc.dram_tensor("wg_s", [HID, GU_COLS], BF16)
    wd_s = nc.dram_tensor("wd_s", [INTER, HID], BF16)
    h_s = nc.dram_tensor("h_s", [INTER, TOK], BF16)
    tg_s = nc.dram_tensor("tg_s", [NG, GU_COLS], BF16)
    tgl_s = nc.dram_tensor("tgl_s", [NG, GU_COLS], BF16)
    td_s = nc.dram_tensor("td_s", [ND, HID], BF16)
    tdl_s = nc.dram_tensor("tdl_s", [ND, HID], BF16)
    x_s = nc.dram_tensor("x_s", [HID, TOK], F32)
    mrs_s = nc.dram_tensor("mrs_s", [2, TOK], F32)

    import contextlib
    ctx = contextlib.ExitStack()
    sb = lambda name, shape, dt: ctx.enter_context(nc.sbuf_tensor(name, shape, dt))

    y_sb = sb("y_sb", [128, HC * TOK], BF16)
    wbuf = sb("wbuf", [128, 2 * HC * GW], BF16)
    stash = sb("stash", [128, 2 * TOK], F32)
    silu_b = sb("silu_b", [128, 2 * TOK], F32)
    hbuf = sb("hbuf", [128, 2 * TOK], BF16)
    qbuf = sb("qbuf", [128, 4 * 2 * GW], U32)
    wdtile = sb("wdtile", [128, 4 * GW], BF16)
    scb = sb("scb", [NG, 2 * GW], BF16)
    tgb = sb("tgb", [NG, 2 * 2 * GW], BF16)
    scd_h = sb("scd_h", [ND, HID], BF16)
    tdb = sb("tdb", [ND, 2 * 2 * 512], BF16)
    gt_gu = sb("gt_gu", [NG, HC * 128], BF16)
    g_gu = sb("g_gu", [128, HC * NG], BF16)
    gtd_b = sb("gtd_b", [ND, 2 * 128], BF16)
    gd_b = sb("gd_b", [128, 4 * ND], BF16)
    gta_i = sb("gta_i", [128, HC], I32)
    gta_f = sb("gta_f", [128, HC], F32)
    gtd_i = sb("gtd_i", [128, ICN], I32)
    gtd_f = sb("gtd_f", [128, ICN], F32)
    grep_i = sb("grep_i", [ND, 128], I32)
    grep_f = sb("grep_f", [ND, 128], F32)
    iota32c = sb("iota32c", [NG, 1], F32)
    iota86c = sb("iota86c", [ND, 1], F32)
    iota86r = sb("iota86r", [128, ND], F32)
    iota32r = sb("iota32r", [128, NG], F32)
    shift_c = sb("shift_c", [128, 1], U32)
    pw2_c = sb("pw2_c", [128, 1], F32)
    ones_c = sb("ones_c", [128, 1], F32)
    ys_sb = sb("ys_sb", [NG, TOK], BF16)
    ysl_sb = sb("ysl_sb", [NG, TOK], BF16)
    ysd_sb = sb("ysd_sb", [ND, TOK], BF16)
    ysdl_sb = sb("ysdl_sb", [ND, TOK], BF16)
    xbuf = sb("xbuf", [128, 3 * TOK], F32)
    sqb = sb("sqb", [128, 1 * TOK], F32)
    tmpb = sb("tmpb", [128, 2 * TOK], F32)
    hinb = sb("hinb", [128, 4 * TOK], BF16)
    wdb = sb("wdb", [128, 3 * 512], BF16)
    esb = sb("esb", [128, 3 * GW], BF16)
    mB = sb("mB", [128, TOK], F32)
    rsB = sb("rsB", [128, TOK], F32)
    silu_bf = silu_b[:, :].bitcast(BF16)
    tchunk_b = silu_bf[0:ND, 0:512]
    tchunk_l = silu_bf[0:ND, 512:1024]
    m_sb = mB[0:1, :]
    rs_sb = rsB[0:1, :]
    mm_sb = stash[0:1, 0:TOK]
    xbuf_u = xbuf[:, :].bitcast(U32)
    zq_u = xbuf_u[0:ND, 0:1376]
    zq_t = xbuf_u[0:ND, 1376:1440]
    scch = xbuf[0:ND, 1440:2464]
    tchunk = xbuf[0:ND, 2464:2976]

    PS = ctx.enter_context(nc.psum_tensor("PS", [128, 4096], F32))
    pg = [PS[:, 0:1024], PS[:, 1024:2048]]
    pE = [PS[:, 2048:2560], PS[:, 2560:3072]]
    pys = PS[0:NG, 3072:4096]
    pysd = PS[0:ND, 3072:4096]
    pd = [PS[:, i * 1024:(i + 1) * 1024] for i in range(4)]
    pstat_s = PS[0:1, 0:1024]
    pstat_q = PS[0:1, 1024:2048]

    bank_tok = {i: [] for i in range(8)}

    def take_banks(banks):
        out = []
        for b in banks:
            out += bank_tok[b]
            bank_tok[b] = []
        return out

    def give_banks(banks, tok):
        if tok is None:
            return
        for b in banks:
            bank_tok[b].append(tok)

    sT = Sem(nc, "sT")
    sV = Sem(nc, "sV")
    sA = Sem(nc, "sA")
    sC = Sem(nc, "sC", dma=True)
    sSCD = Sem(nc, "sSCD", dma=True)
    sQZ = Sem(nc, "sQZ", dma=True)
    sSCC = Sem(nc, "sSCC", dma=True)
    sGt = Sem(nc, "sGt", dma=True)
    sGm = Sem(nc, "sGm", dma=True)
    sMB = Sem(nc, "sMB", dma=True)
    sRSB = Sem(nc, "sRSB", dma=True)
    sGrep = Sem(nc, "sGrep", dma=True)
    sGtab = Sem(nc, "sGtab", dma=True)
    sWG = [Sem(nc, f"sWG{i}", dma=True) for i in range(2)]
    sWD = [Sem(nc, f"sWD{i}", dma=True) for i in range(2)]
    sH = [Sem(nc, f"sH{i}", dma=True) for i in range(3)]
    sX = [Sem(nc, f"sX{i}", dma=True) for i in range(3)]

    P = Plan()

    def dma(engine, sem, dst, src, waits=()):
        def fn(eng, dst=dst, src=src):
            return eng.dma_start(dst, src)
        P.op(engine, fn, waits=waits, inc=sem)
        return (sem, sem.n)

    xrot = Rot(3, nc, "sXB")
    qrot = Rot(4, nc, "sQB")
    wrot = Rot(2, nc, "sWB")
    scrot = Rot(2, nc, "sSB")
    tgrot = Rot(2, nc, "sTGb")
    tdrot = Rot(2, nc, "sTDb")
    hin_rot = Rot(4, nc, "sHI")
    wdin_rot = Rot(3, nc, "sWI")
    sqrot = Rot(1)
    tmprot = Rot(2)
    erot = Rot(2)
    esrot = Rot(3)
    hrot = Rot(2)
    gdrot = Rot(4)
    gtdrot = Rot(2)
    wdorot = Rot(4)
    mrs_free = []

    for dst, src_t in [
        (shift_c[:, :], bass.AP(c_shift, 0, [[1, 128], [1, 1]])),
        (pw2_c[:, :], bass.AP(c_pw2, 0, [[1, 128], [1, 1]])),
        (iota32c[:, :], bass.AP(c_iota32, 0, [[1, NG], [1, 1]])),
        (iota86c[:, :], bass.AP(c_iota86, 0, [[1, ND], [1, 1]])),
        (ones_c[:, :], bass.AP(c_ones, 0, [[1, 128], [1, 1]])),
        (iota86r[:, :], bass.AP(c_iota86, 0, [[0, 128], [1, ND]])),
        (iota32r[:, :], bass.AP(c_iota32, 0, [[0, 128], [1, NG]])),
    ]:
        dma("sync", sC, dst, src_t)
    t_const = (sC, sC.n)

    v_scdh = None
    for c0 in range(0, HID, 1024):
        t_scd = dma("sync", sSCD, scch[:, 0:1024],
                    bass.AP(sc_d, c0, [[HID, ND], [1, 1024]]),
                    waits=[v_scdh])
        P.op("vector", lambda e, c0=c0: e.tensor_scalar_mul(
                 scd_h[:, c0:c0 + 1024], scch[:, 0:1024], 0.5),
             waits=[t_scd], inc=sV)
        v_scdh = (sV, sV.n)

    t_gta = dma("scalar", sGtab, gta_i[:, :], gidx_gu_pt[:, :])
    t_gtd2 = dma("scalar", sGtab, gtd_i[:, :], gidx_d_pt[:, :])
    tab_gate = (sGtab, sGtab.n)
    P.op("vector", lambda e: e.tensor_copy(gta_f[:, :], gta_i[:, :]),
         waits=[tab_gate], inc=sV)
    P.op("vector", lambda e: e.tensor_copy(gtd_f[:, :], gtd_i[:, :]),
         waits=[tab_gate], inc=sV)
    v_gtab = (sV, sV.n)
    grep_free = None
    v_ggu_done = None
    for hc in range(HC):
        P.op("vector", lambda e, hc=hc: e.tensor_scalar(
                 g_gu[:, hc * NG:(hc + 1) * NG], iota32r[:, :],
                 gta_f[:, hc:hc + 1], None, ALU.is_equal),
             waits=[t_const, v_gtab], inc=sV)
        tg2 = dma("scalar", sGrep, grep_i[0:NG, 0:128],
                  bass.AP(gidx_gu_p, hc * 128, [[0, NG], [1, 128]]),
                  waits=[grep_free])
        P.op("vector", lambda e: e.tensor_copy(grep_f[0:NG, 0:128], grep_i[0:NG, 0:128]),
             waits=[tg2], inc=sV)
        v_cp_g = (sV, sV.n)
        P.op("vector", lambda e, hc=hc: e.tensor_scalar(
                 gt_gu[:, hc * 128:(hc + 1) * 128], grep_f[0:NG, 0:128], iota32c[:, :],
                 None, ALU.is_equal),
             waits=[t_const, v_cp_g], inc=sV)
        grep_free = (sV, sV.n)
        v_ggu_done = (sV, sV.n)

    def build_T(qz, sc_dram, scratch, scratch_l, ngrp, ncp_packed, out_cols, t_prev):
        t_last = t_prev
        sc_guard = t_prev
        tb_guard = None
        for half in range(2):
            p0 = (ncp_packed // 2) * half
            pw_half = ncp_packed - p0 if half == 1 else ncp_packed // 2
            if pw_half == 0:
                continue
            t_qz = dma("sync", sQZ, zq_u[0:ngrp, 0:pw_half],
                       bass.AP(qz, p0, [[ncp_packed, ngrp], [1, pw_half]]),
                       waits=[t_last])
            col0 = p0 * 8
            colw = pw_half * 8
            for c0 in range(col0, col0 + colw, 1024):
                cw = min(1024, col0 + colw - c0)
                t_sc = dma("sync", sSCC, scch[0:ngrp, 0:cw],
                           bass.AP(sc_dram, c0, [[out_cols, ngrp], [1, cw]]),
                           waits=[sc_guard])
                for s0 in range(0, cw, 512):
                    sw = min(512, cw - s0)
                    pw = sw // 8
                    bp = (c0 + s0) // 8 - p0
                    for k in range(8):
                        P.op("vector", lambda e, k=k, pw=pw, bp=bp, ngrp=ngrp:
                             e.tensor_scalar(zq_t[0:ngrp, 0:pw],
                                             zq_u[0:ngrp, bp:bp + pw],
                                             4 * k, 15,
                                             ALU.logical_shift_right, ALU.bitwise_and),
                             waits=[t_qz], inc=sV)
                        v_zq = (sV, sV.n)
                        P.op("vector", lambda e, k=k, pw=pw, s0=s0, ngrp=ngrp:
                             e.scalar_tensor_tensor(
                                 bass.AP(tchunk.tensor, tchunk.offset + k,
                                         [[3 * TOK, ngrp], [8, pw]]),
                                 zq_t[0:ngrp, 0:pw], 1.0,
                                 bass.AP(scch.tensor, scch.offset + s0 + k,
                                         [[3 * TOK, ngrp], [8, pw]]),
                                 ALU.add, ALU.mult),
                             waits=[t_sc, v_zq], inc=sV)
                    v_stt = (sV, sV.n)
                    P.op("vector", lambda e, ngrp=ngrp, sw=sw: e.tensor_copy(
                             tchunk_b[0:ngrp, 0:sw], tchunk[0:ngrp, 0:sw]),
                         waits=[tb_guard, v_stt], inc=sV)
                    vv = (sV, sV.n)
                    dma("gpsimd", sGt,
                        bass.AP(scratch, c0 + s0, [[out_cols, ngrp], [1, sw]]),
                        tchunk_b[0:ngrp, 0:sw], waits=[vv])
                    P.op("vector", lambda e, ngrp=ngrp, sw=sw: e.tensor_sub(
                             tchunk_l[0:ngrp, 0:sw], tchunk[0:ngrp, 0:sw],
                             tchunk_b[0:ngrp, 0:sw]),
                         waits=[tb_guard, vv], inc=sV)
                    vv2 = (sV, sV.n)
                    tb_guard = dma("gpsimd", sGt,
                                   bass.AP(scratch_l, c0 + s0,
                                           [[out_cols, ngrp], [1, sw]]),
                                   tchunk_l[0:ngrp, 0:sw], waits=[vv2])
                sc_guard = (sV, sV.n)
            t_last = (sV, sV.n)
        return t_last

    v_t1 = build_T(qz_gu, sc_gu, tg_s, tgl_s, NG, GU_COLS // 8, GU_COLS, v_scdh)
    v_t2 = build_T(qz_d, sc_d, td_s, tdl_s, ND, HID // 8, HID, v_t1)
    t_T = (sGt, sGt.n)
    for i in range(xrot.n):
        xrot.tok[i].append(v_t2)

    def x_chunk_ap(src, hc, permuted):
        if permuted:
            return bass.AP(src, hc * 128 * TOK, [[TOK, 8], [8 * TOK, 16], [1, TOK]])
        return bass.AP(src, hc * 128 * TOK, [[TOK, 128], [1, TOK]])

    def ln_stats(src, extra_waits=()):
        tk_s = take_banks([0, 1])
        tk_q = take_banks([2, 3])
        v_mm = None
        for hc in range(HC):
            slot, toks = xrot.acquire()
            w = list(toks) + [t_const] + list(extra_waits)
            t_x = dma("sync", xrot.sems[slot], xbuf[:, slot * TOK:(slot + 1) * TOK],
                      x_chunk_ap(src, hc, False), waits=w)
            qslot, qtoks = sqrot.acquire()
            P.op("scalar", lambda e, slot=slot, qslot=qslot: e.activation(
                     sqb[:, qslot * TOK:(qslot + 1) * TOK],
                     xbuf[:, slot * TOK:(slot + 1) * TOK], AF.Square),
                 waits=list(qtoks) + [t_x], inc=sA)
            v_sq = (sA, sA.n)
            for tk in range(2):
                P.op("tensor", lambda e, slot=slot, tk=tk, hc=hc: e.matmul(
                         pstat_s[:, tk * 512:(tk + 1) * 512], ones_c[:, :],
                         xbuf[:, slot * TOK + tk * 512: slot * TOK + (tk + 1) * 512],
                         start=(hc == 0), stop=(hc == HC - 1)),
                     waits=(tk_s if hc == 0 and tk == 0 else []) + [t_x], inc=None)
                P.op("tensor", lambda e, qslot=qslot, tk=tk, hc=hc: e.matmul(
                         pstat_q[:, tk * 512:(tk + 1) * 512], ones_c[:, :],
                         sqb[:, qslot * TOK + tk * 512: qslot * TOK + (tk + 1) * 512],
                         start=(hc == 0), stop=(hc == HC - 1)),
                     waits=(tk_q if hc == 0 and tk == 0 else []) + [v_sq],
                     inc=(sT if tk == 1 else None))
            v_mm = (sT, sT.n)
            xrot.release(slot, v_mm)
            sqrot.release(qslot, v_mm)
        w_m = list(mrs_free)
        mrs_free.clear()
        P.op("vector", lambda e: e.tensor_scalar_mul(m_sb, pstat_s[:, :], 1.0 / HID),
             waits=[v_mm] + w_m, inc=sV)
        v_m0 = (sV, sV.n)
        give_banks([0, 1], v_m0)
        P.op("vector", lambda e: e.tensor_scalar_mul(rs_sb, pstat_q[:, :], 1.0 / HID),
             inc=sV)
        v_r0 = (sV, sV.n)
        give_banks([2, 3], v_r0)
        P.op("vector", lambda e: e.tensor_mul(mm_sb, m_sb, m_sb),
             waits=[v_m0], inc=sV)
        v_mm2 = (sV, sV.n)
        P.op("vector", lambda e: e.scalar_tensor_tensor(
                 rs_sb, rs_sb, EPS, mm_sb, ALU.add, ALU.subtract),
             waits=[v_r0, v_mm2], inc=sV)
        v_var = (sV, sV.n)
        P.op("scalar", lambda e: e.activation(rs_sb, rs_sb, AF.Ln),
             waits=[v_var], inc=sA)
        v_ln = (sA, sA.n)
        P.op("scalar", lambda e: e.activation(rs_sb, rs_sb, AF.Exp, scale=-0.5),
             waits=[v_ln], inc=sA)
        v_rs = (sA, sA.n)
        t1 = dma("gpsimd", sGm, bass.AP(mrs_s, 0, [[TOK, 1], [1, TOK]]), m_sb,
                 waits=[v_var])
        dma("gpsimd", sGm, bass.AP(mrs_s, TOK, [[TOK, 1], [1, TOK]]), rs_sb,
            waits=[v_rs])
        pair_done = (sGm, sGm.n)
        t_m = dma("sync", sMB, mB[:, :], bass.AP(mrs_s, 0, [[0, 128], [1, TOK]]),
                  waits=[pair_done] + w_m)
        t_rs = dma("sync", sRSB, rsB[:, :], bass.AP(mrs_s, TOK, [[0, 128], [1, TOK]]),
                   waits=[pair_done])
        return t_m, t_rs

    def y_build(src, t_m, t_rs):
        last = None
        for hc in range(HC):
            slot, toks = xrot.acquire()
            t_x = dma("sync", xrot.sems[slot], xbuf[:, slot * TOK:(slot + 1) * TOK],
                      x_chunk_ap(src, hc, True), waits=toks)
            tslot, ttoks = tmprot.acquire()
            P.op("vector", lambda e, slot=slot, tslot=tslot: e.tensor_sub(
                     tmpb[:, tslot * TOK:(tslot + 1) * TOK],
                     xbuf[:, slot * TOK:(slot + 1) * TOK], mB[:, :]),
                 waits=list(ttoks) + [t_x, t_m], inc=sV)
            xrot.release(slot, (sV, sV.n))
            P.op("vector", lambda e, hc=hc, tslot=tslot: e.tensor_mul(
                     y_sb[:, hc * TOK:(hc + 1) * TOK],
                     tmpb[:, tslot * TOK:(tslot + 1) * TOK], rsB[:, :]),
                 waits=[t_rs], inc=sV)
            tmprot.release(tslot, (sV, sV.n))
            last = (sV, sV.n)
        mrs_free.append(last)
        return last

    def attn_apply(t_m, t_rs):
        for hc in range(HC):
            slot, toks = xrot.acquire()
            t_x = dma("sync", xrot.sems[slot], xbuf[:, slot * TOK:(slot + 1) * TOK],
                      x_chunk_ap(x_s, hc, False), waits=toks)
            tslot, ttoks = tmprot.acquire()
            xs = xbuf[:, slot * TOK:(slot + 1) * TOK]
            ts = tmpb[:, tslot * TOK:(tslot + 1) * TOK]
            P.op("vector", lambda e, xs=xs, ts=ts: e.tensor_sub(ts, xs, mB[:, :]),
                 waits=list(ttoks) + [t_x, t_m], inc=None)
            P.op("vector", lambda e, xs=xs, ts=ts: e.tensor_mul(ts, ts, rsB[:, :]),
                 waits=[t_rs], inc=None)
            P.op("vector", lambda e, xs=xs, ts=ts: e.tensor_add(xs, xs, ts), inc=sV)
            tmprot.release(tslot, (sV, sV.n))
            v_x2 = (sV, sV.n)
            t_st = dma("gpsimd", sX[slot], x_chunk_ap(x_s, hc, False), xs,
                       waits=[v_x2])
            xrot.release(slot, t_st)
        mrs_free.append((sV, sV.n))
        return [(s, s.n) for s in sX]

    def gu_phase(dequant, t_y_done, t_w_gate):
        pg_banks = [[0, 1], [2, 3]]
        stash_ready = [None, None]
        silu_free = [t_T, t_T]
        cp_counter = [0]
        grep_gate = [grep_free]

        tk67 = take_banks([6, 7])
        for hc in range(HC):
            for tk in range(2):
                P.op("tensor", lambda e, hc=hc, tk=tk: e.matmul(
                         pys[:, tk * 512:(tk + 1) * 512],
                         g_gu[:, hc * NG:(hc + 1) * NG],
                         y_sb[:, hc * TOK + tk * 512: hc * TOK + (tk + 1) * 512],
                         start=(hc == 0), stop=(hc == HC - 1)),
                     waits=(tk67 if hc == 0 and tk == 0 else []) + [t_y_done, v_ggu_done],
                     inc=(sT if hc == HC - 1 and tk == 1 else None))
        v_ys = (sT, sT.n)
        P.op("scalar", lambda e: e.activation(ys_sb[:, :], pys[:, :], AF.Identity,
                                              scale=-1.0),
             waits=[v_ys], inc=sA)
        v_ysb = (sA, sA.n)
        P.op("vector", lambda e: e.scalar_tensor_tensor(
                 ysl_sb[:, :], pys[:, :], -1.0, ys_sb[:, :],
                 ALU.mult, ALU.subtract),
             waits=[v_ysb], inc=sV)
        v_ysb_l = (sV, sV.n)
        ysd_gate = v_ysb_l

        def wd_begin(ic):
            gslot, gtoks = gtdrot.acquire()
            tgi = dma("scalar", sGrep, grep_i[0:ND, 0:128],
                      bass.AP(gidx_d_p, ic * 128, [[0, ND], [1, 128]]),
                      waits=gtoks + [grep_gate[0]])
            P.op("vector", lambda e: e.tensor_copy(grep_f[0:ND, 0:128],
                                                   grep_i[0:ND, 0:128]),
                 waits=[tgi], inc=sV)
            v_cp_d = (sV, sV.n)
            P.op("vector", lambda e, gslot=gslot: e.tensor_scalar(
                     gtd_b[:, gslot * 128:(gslot + 1) * 128], grep_f[0:ND, 0:128],
                     iota86c[:, :], None, ALU.is_equal),
                 waits=[t_const, v_cp_d], inc=sV)
            v_gtd = (sV, sV.n)
            grep_gate[0] = v_gtd
            return {"gslot": gslot, "v_gtd": v_gtd, "ic": ic, "last_e": None}

        def wd_unit(st, jt):
            gslot = st["gslot"]
            v_gtd = st["v_gtd"]
            ic = st["ic"]
            if True:
                if jt % 2 == 0:
                    qslot, qtoks = qrot.acquire()
                    t_q = dma("sync" if qslot % 2 == 0 else "gpsimd",
                              qrot.sems[qslot],
                              qbuf[:, qslot * 2 * GW:(qslot + 1) * 2 * GW],
                              bass.AP(qw_d, ic * (16 * 16 * GW) + jt * GW,
                                      [[0, 8], [16 * GW, 16], [1, 2 * GW]]),
                              waits=qtoks)
                    P.op("vector", lambda e, qslot=qslot: e.tensor_scalar(
                             qbuf[:, qslot * 2 * GW:(qslot + 1) * 2 * GW],
                             qbuf[:, qslot * 2 * GW:(qslot + 1) * 2 * GW],
                             shift_c[:, 0:1], None, ALU.bitwise_and),
                         waits=[t_q, t_const], inc=sV)
                    st["q4"] = qslot
                    st["v_and"] = (sV, sV.n)
                qslot = st["q4"]
                eslot, etoks = erot.acquire()
                P.op("tensor", lambda e, eslot=eslot, gslot=gslot, jt=jt: e.matmul(
                         pE[eslot][0:128, 0:GW],
                         gtd_b[:, gslot * 128:(gslot + 1) * 128],
                         scd_h[:, jt * GW:(jt + 1) * GW],
                         start=True, stop=True),
                     waits=etoks + [v_gtd, v_scdh], inc=sT)
                v_e = (sT, sT.n)
                st["last_e"] = v_e
                esslot, estoks = esrot.acquire()
                P.op("scalar", lambda e, eslot=eslot, esslot=esslot: e.activation(
                         esb[:, esslot * GW:(esslot + 1) * GW], pE[eslot][0:128, 0:GW],
                         AF.Identity, scale=pw2_c[:, 0:1]),
                     waits=estoks + [v_e, t_const], inc=sA)
                v_es = (sA, sA.n)
                erot.release(eslot, v_es)
                oslot, otoks = wdorot.acquire()
                P.op("vector", lambda e, qslot=qslot, jt=jt, esslot=esslot,
                              oslot=oslot:
                     e.tensor_tensor(
                         wdtile[:, oslot * GW:(oslot + 1) * GW],
                         qbuf[:, qslot * 2 * GW + (jt % 2) * GW:
                              qslot * 2 * GW + (jt % 2 + 1) * GW],
                         esb[:, esslot * GW:(esslot + 1) * GW], ALU.mult),
                     waits=otoks + [v_es, st["v_and"]], inc=sV)
                v_w = (sV, sV.n)
                esrot.release(esslot, v_w)
                if jt % 2 == 1:
                    qrot.release(qslot, v_w)
                if jt % 2 == 0:
                    st["pend_o"] = oslot
                else:
                    o0 = st.pop("pend_o")
                    assert o0 % 2 == 0 and oslot == o0 + 1
                    t_wb = dma("gpsimd", sWD[(jt // 2) % 2],
                               bass.AP(wd_s, ic * (128 * HID) + (jt - 1) * GW,
                                       [[HID, 128], [1, 2 * GW]]),
                               wdtile[:, o0 * GW:(o0 + 2) * GW], waits=[v_w])
                    wdorot.release(o0, t_wb)
                    wdorot.release(oslot, t_wb)

        def wd_end(st):
            gtdrot.release(st["gslot"], st["last_e"])

        grps = []
        for j in range(INTER // GW):
            grps.append(("g", j * GW))
            grps.append(("u", INTER + j * GW))

        v_ysd_stop = None
        n_up = 0
        npairs = INTER // GW

        def dq_begin(wslot, wtoks, c0):
            sslot, stoks = scrot.acquire()
            t_sc = dma("gpsimd", scrot.sems[sslot],
                       scb[:, sslot * GW: (sslot + 1) * GW],
                       bass.AP(sc_gu, c0, [[GU_COLS, NG], [1, GW]]), waits=stoks)
            return {"wslot": wslot, "wtoks": wtoks, "c0": c0, "sslot": sslot,
                    "t_sc": t_sc, "wready": [], "last_e": None}

        def dq_unit(st, hc):
            wslot = st["wslot"]
            wtoks = st["wtoks"]
            c0 = st["c0"]
            sslot = st["sslot"]
            t_sc = st["t_sc"]
            if hc % 2 == 0:
                qslot, qtoks = qrot.acquire()
                t_q = dma("sync" if qslot % 2 == 0 else "gpsimd",
                          qrot.sems[qslot],
                          qbuf[:, qslot * 2 * GW:(qslot + 1) * 2 * GW],
                          bass.AP(qw_gu, (c0 // GW) * (16 * HC * GW) + hc * GW,
                                  [[0, 8], [HC * GW, 16], [1, 2 * GW]]),
                          waits=qtoks + (wtoks if hc == 0 else []))
                P.op("vector", lambda e, qslot=qslot: e.tensor_scalar(
                         qbuf[:, qslot * 2 * GW:(qslot + 1) * 2 * GW],
                         qbuf[:, qslot * 2 * GW:(qslot + 1) * 2 * GW],
                         shift_c[:, 0:1], None, ALU.bitwise_and),
                     waits=[t_q, t_const], inc=sV)
                st["q4"] = qslot
                st["v_and"] = (sV, sV.n)
            qslot = st["q4"]
            eslot, etoks = erot.acquire()
            P.op("tensor", lambda e, eslot=eslot, hc=hc, sslot=sslot:
                 e.matmul(pE[eslot][0:128, 0:GW],
                          gt_gu[:, hc * 128:(hc + 1) * 128],
                          scb[:, sslot * GW: (sslot + 1) * GW],
                          start=True, stop=True),
                 waits=etoks + [t_sc, v_ggu_done], inc=sT)
            v_e = (sT, sT.n)
            st["last_e"] = v_e
            esslot, estoks = esrot.acquire()
            P.op("scalar", lambda e, eslot=eslot, esslot=esslot: e.activation(
                     esb[:, esslot * GW:(esslot + 1) * GW],
                     pE[eslot][0:128, 0:GW],
                     AF.Identity, scale=pw2_c[:, 0:1]),
                 waits=estoks + [v_e, t_const], inc=sA)
            v_es = (sA, sA.n)
            erot.release(eslot, v_es)
            P.op("vector", lambda e, qslot=qslot, esslot=esslot, wslot=wslot,
                          hc=hc:
                 e.tensor_tensor(
                     wbuf[:, wslot * (HC * GW) + hc * GW:
                          wslot * (HC * GW) + (hc + 1) * GW],
                     qbuf[:, qslot * 2 * GW + (hc % 2) * GW:
                          qslot * 2 * GW + (hc % 2 + 1) * GW],
                     esb[:, esslot * GW:(esslot + 1) * GW], ALU.mult),
                 waits=[v_es, st["v_and"]], inc=sV)
            v_w = (sV, sV.n)
            esrot.release(esslot, v_w)
            if hc % 2 == 1:
                qrot.release(qslot, v_w)
            st["wready"].append(v_w)

        def dq_end(st):
            scrot.release(st["sslot"], st["last_e"])
            wslot = st["wslot"]
            for hb in (0, HC // 2):
                dma("gpsimd", sWG[wslot],
                    bass.AP(wg_s, (st["c0"] // GW) * (128 * HC * GW) + hb * GW,
                            [[HC * GW, 128], [1, (HC // 2) * GW]]),
                    wbuf[:, wslot * (HC * GW) + hb * GW:
                         wslot * (HC * GW) + (hb + HC // 2) * GW],
                    waits=[st["wready"][-1]])
            return st["wready"], (sWG[st["wslot"]], sWG[st["wslot"]].n)

        wd_queue = []
        wd_active = []

        def drain_wd(budget):
            while budget > 0:
                if not wd_active:
                    if not wd_queue:
                        return
                    wd_active.append([wd_begin(wd_queue.pop(0)), 0])
                st, jt = wd_active[0]
                for _ in range(4):
                    wd_unit(st, jt)
                    jt += 1
                budget -= 4
                if jt == HID // GW:
                    wd_end(st)
                    wd_active.pop(0)
                else:
                    wd_active[0][1] = jt

        pend = {}
        pend_state = None
        pend_hc = 0
        for gi, (side, c0) in enumerate(grps):
            ncp = GW // 128
            if dequant:
                if gi == 0:
                    ws0, wt0 = wrot.acquire()
                    st0 = dq_begin(ws0, wt0, grps[0][1])
                    for hc0 in range(HC):
                        dq_unit(st0, hc0)
                    pend[0] = (ws0,) + dq_end(st0)
                if gi + 1 < len(grps):
                    wsN, wtN = wrot.acquire()
                    pend_state = dq_begin(wsN, wtN, grps[gi + 1][1])
                    pend_hc = 0
                else:
                    pend_state = None
                wslot, wready, wb_use_done = pend.pop(gi)
            else:
                wslot, wtoks = wrot.acquire()
                wb_use_done = None
                for hb in range(0, HC, 8):
                    dma("sync", wrot.sems[wslot],
                        wbuf[:, wslot * (HC * GW) + hb * GW:
                             wslot * (HC * GW) + (hb + 8) * GW],
                        bass.AP(wg_s, (c0 // GW) * (128 * HC * GW) + hb * GW,
                                [[HC * GW, 128], [1, 8 * GW]]),
                        waits=(wtoks + list(t_w_gate) if hb == 0 else []))
                wready = [(wrot.sems[wslot], wrot.sems[wslot].n)]
            tslot, ttoks = tgrot.acquire()
            dma("scalar", tgrot.sems[tslot],
                tgb[:, tslot * 2 * GW: tslot * 2 * GW + GW],
                bass.AP(tg_s, c0, [[GU_COLS, NG], [1, GW]]),
                waits=ttoks + [t_T])
            t_tg = dma("scalar", tgrot.sems[tslot],
                       tgb[:, tslot * 2 * GW + GW: (tslot + 1) * 2 * GW],
                       bass.AP(tgl_s, c0, [[GU_COLS, NG], [1, GW]]), waits=[])
            grp_last = None
            for cp in range(ncp):
                if dequant:
                    target = (HC * (cp + 1)) // ncp
                    while pend_state is not None and pend_hc < target:
                        dq_unit(pend_state, pend_hc)
                        pend_hc += 1
                    drain_wd(10)
                pgi = cp_counter[0] % 2
                cp_counter[0] += 1
                tk_banks = take_banks(pg_banks[pgi])
                for hc in range(HC):
                    for tk in range(2):
                        w = []
                        if hc == 0 and tk == 0:
                            w += tk_banks
                            w.append(t_y_done)
                        w.append(wready[hc] if dequant else wready[0])
                        P.op("tensor", lambda e, pgi=pgi, tk=tk, hc=hc, wslot=wslot,
                                      cp=cp:
                             e.matmul(pg[pgi][:, tk * 512:(tk + 1) * 512],
                                      wbuf[:, wslot * (HC * GW) + hc * GW + cp * 128:
                                           wslot * (HC * GW) + hc * GW + (cp + 1) * 128],
                                      y_sb[:, hc * TOK + tk * 512:
                                           hc * TOK + (tk + 1) * 512],
                                      start=(hc == 0), stop=False),
                             waits=w, inc=None)
                for tk in range(2):
                    P.op("tensor", lambda e, pgi=pgi, tk=tk, tslot=tslot, cp=cp:
                         e.matmul(pg[pgi][:, tk * 512:(tk + 1) * 512],
                                  tgb[:, tslot * 2 * GW + cp * 128:
                                       tslot * 2 * GW + (cp + 1) * 128],
                                  ys_sb[:, tk * 512:(tk + 1) * 512],
                                  start=False, stop=False),
                         waits=[t_tg, v_ysb], inc=None)
                    P.op("tensor", lambda e, pgi=pgi, tk=tk, tslot=tslot, cp=cp:
                         e.matmul(pg[pgi][:, tk * 512:(tk + 1) * 512],
                                  tgb[:, tslot * 2 * GW + cp * 128:
                                       tslot * 2 * GW + (cp + 1) * 128],
                                  ysl_sb[:, tk * 512:(tk + 1) * 512],
                                  start=False, stop=False),
                         waits=[v_ysb_l], inc=None)
                    P.op("tensor", lambda e, pgi=pgi, tk=tk, tslot=tslot, cp=cp:
                         e.matmul(pg[pgi][:, tk * 512:(tk + 1) * 512],
                                  tgb[:, tslot * 2 * GW + GW + cp * 128:
                                       tslot * 2 * GW + GW + (cp + 1) * 128],
                                  ys_sb[:, tk * 512:(tk + 1) * 512],
                                  start=False, stop=True),
                         waits=[], inc=sT)
                v_cp = (sT, sT.n)
                grp_last = v_cp
                if side == "g":
                    P.op("scalar", lambda e, pgi=pgi, cp=cp: e.copy(
                             stash[:, cp * TOK:(cp + 1) * TOK], pg[pgi][:, :]),
                         waits=[v_cp], inc=sA)
                    give_banks(pg_banks[pgi], (sA, sA.n))
                    stash_ready[cp] = (sA, sA.n)
                else:
                    ic = (c0 - INTER) // 128 + cp
                    sl = n_up % 2
                    n_up += 1
                    P.op("scalar", lambda e, cp=cp, sl=sl: e.activation(
                             silu_b[:, sl * TOK:(sl + 1) * TOK],
                             stash[:, cp * TOK:(cp + 1) * TOK], AF.Silu),
                         waits=[stash_ready[cp], silu_free[sl]], inc=sA)
                    v_silu = (sA, sA.n)
                    hslot, htoks = hrot.acquire()
                    P.op("vector", lambda e, pgi=pgi, hslot=hslot, sl=sl:
                         e.tensor_tensor(
                             hbuf[:, hslot * TOK:(hslot + 1) * TOK],
                             silu_b[:, sl * TOK:(sl + 1) * TOK],
                             pg[pgi][:, :], ALU.mult),
                         waits=htoks + [v_silu, v_cp], inc=sV)
                    v_h = (sV, sV.n)
                    silu_free[sl] = v_h
                    give_banks(pg_banks[pgi], v_h)
                    gdslot, gdtoks = gdrot.acquire()
                    P.op("vector", lambda e, gdslot=gdslot, ic=ic: e.tensor_scalar(
                             gd_b[:, gdslot * ND:(gdslot + 1) * ND], iota86r[:, :],
                             gtd_f[:, ic:ic + 1], None, ALU.is_equal),
                         waits=gdtoks + [t_const, v_gtab], inc=sV)
                    v_gd = (sV, sV.n)
                    for tk in range(2):
                        P.op("tensor", lambda e, gdslot=gdslot, hslot=hslot, tk=tk,
                                      ic=ic:
                             e.matmul(pysd[:, tk * 512:(tk + 1) * 512],
                                      gd_b[:, gdslot * ND:(gdslot + 1) * ND],
                                      hbuf[:, hslot * TOK + tk * 512:
                                           hslot * TOK + (tk + 1) * 512],
                                      start=(ic == 0), stop=(ic == ICN - 1)),
                             waits=[v_gd, v_h] + ([ysd_gate] if ic == 0 and tk == 0
                                                  else []),
                             inc=(sT if tk == 1 else None))
                    v_ysd_mm = (sT, sT.n)
                    gdrot.release(gdslot, v_ysd_mm)
                    hrot.release(hslot, v_ysd_mm)
                    inter0 = c0 - INTER + cp * 128
                    t_h = dma("gpsimd", sH[hslot],
                              bass.AP(h_s, inter0 * TOK,
                                      [[TOK, 16], [16 * TOK, 8], [1, TOK]]),
                              hbuf[:, hslot * TOK:(hslot + 1) * TOK], waits=[v_h])
                    hrot.release(hslot, t_h)
                    if ic == ICN - 1:
                        v_ysd_stop = v_ysd_mm
            tgrot.release(tslot, grp_last)
            wrot.release(wslot, grp_last)
            wrot.release(wslot, wb_use_done)
            if dequant and pend_state is not None:
                while pend_hc < HC:
                    dq_unit(pend_state, pend_hc)
                    pend_hc += 1
                pend[gi + 1] = (pend_state["wslot"],) + dq_end(pend_state)
                pend_state = None
            if dequant and side == "u":
                k = gi // 2
                lo = (k * ICN) // npairs
                hi = ((k + 1) * ICN) // npairs
                wd_queue.extend(range(lo, hi))
        if dequant:
            drain_wd(10 ** 9)
        P.op("scalar", lambda e: e.activation(ysd_sb[:, :], pysd[:, :], AF.Identity,
                                              scale=-0.5),
             waits=[v_ysd_stop], inc=sA)
        v_ysd_h = (sA, sA.n)
        P.op("vector", lambda e: e.scalar_tensor_tensor(
                 ysdl_sb[:, :], pysd[:, :], -0.5, ysd_sb[:, :],
                 ALU.mult, ALU.subtract),
             waits=[v_ysd_h], inc=sV)
        v_ysd_sb = (sV, sV.n)
        give_banks([6, 7], v_ysd_sb)
        for i, toks in enumerate(erot.tok):
            for t in toks:
                bank_tok[4 + i].append(t)
            erot.tok[i] = []
        erot.i = 0
        h_gate = [(s, s.n) for s in sH]
        w_gate = [(s, s.n) for s in sWG] + [(s, s.n) for s in sWD]
        return v_ysd_sb, h_gate, w_gate

    def down_phase(v_ysd_sb, h_gate, w_gate, src, dst):
        pd_banks = [[0, 1], [2, 3], [4, 5], [6, 7]]
        for hg in range(8):
            tslot, ttoks = tdrot.acquire()
            dma("scalar", tdrot.sems[tslot], tdb[:, tslot * 1024: tslot * 1024 + 512],
                bass.AP(td_s, hg * 512, [[HID, ND], [1, 512]]),
                waits=ttoks + [t_T])
            t_tdsl = dma("scalar", tdrot.sems[tslot],
                         tdb[:, tslot * 1024 + 512: (tslot + 1) * 1024],
                         bass.AP(tdl_s, hg * 512, [[HID, ND], [1, 512]]), waits=[])
            for ic in range(ICN):
                hslot, htoks = hin_rot.acquire()
                t_hin = dma("sync", hin_rot.sems[hslot],
                            hinb[:, hslot * TOK:(hslot + 1) * TOK],
                            bass.AP(h_s, ic * 128 * TOK,
                                    [[TOK, 128], [1, TOK]]),
                            waits=htoks + (h_gate if hg == 0 and ic == 0 else []))
                wslot, wtoks = wdin_rot.acquire()
                t_wd = dma("sync", wdin_rot.sems[wslot],
                           wdb[:, wslot * 512:(wslot + 1) * 512],
                           bass.AP(wd_s, (ic * 128) * HID + hg * 512,
                                   [[HID, 128], [1, 512]]),
                           waits=wtoks + (w_gate if hg == 0 and ic == 0 else []))
                for c in range(4):
                    for tk in range(2):
                        w = [t_hin, t_wd]
                        if ic == 0 and tk == 0:
                            w += take_banks(pd_banks[c])
                        P.op("tensor", lambda e, c=c, tk=tk, wslot=wslot, hslot=hslot,
                                      ic=ic:
                             e.matmul(pd[c][:, tk * 512:(tk + 1) * 512],
                                      wdb[:, wslot * 512 + c * 128:
                                           wslot * 512 + (c + 1) * 128],
                                      hinb[:, hslot * TOK + tk * 512:
                                           hslot * TOK + (tk + 1) * 512],
                                      start=(ic == 0), stop=False),
                             waits=w, inc=(sT if c == 3 and tk == 1 else None))
                v_ic = (sT, sT.n)
                hin_rot.release(hslot, v_ic)
                wdin_rot.release(wslot, v_ic)
            v_c = None
            for c in range(4):
                for tk in range(2):
                    P.op("tensor", lambda e, c=c, tk=tk, tslot=tslot: e.matmul(
                             pd[c][:, tk * 512:(tk + 1) * 512],
                             tdb[:, tslot * 1024 + c * 128:
                                  tslot * 1024 + (c + 1) * 128],
                             ysd_sb[:, tk * 512:(tk + 1) * 512],
                             start=False, stop=False),
                         waits=[t_tdsl, v_ysd_sb], inc=None)
                    P.op("tensor", lambda e, c=c, tk=tk, tslot=tslot: e.matmul(
                             pd[c][:, tk * 512:(tk + 1) * 512],
                             tdb[:, tslot * 1024 + c * 128:
                                  tslot * 1024 + (c + 1) * 128],
                             ysdl_sb[:, tk * 512:(tk + 1) * 512],
                             start=False, stop=False),
                         waits=[], inc=None)
                    P.op("tensor", lambda e, c=c, tk=tk, tslot=tslot: e.matmul(
                             pd[c][:, tk * 512:(tk + 1) * 512],
                             tdb[:, tslot * 1024 + 512 + c * 128:
                                  tslot * 1024 + 512 + (c + 1) * 128],
                             ysd_sb[:, tk * 512:(tk + 1) * 512],
                             start=False, stop=True),
                         waits=[], inc=(sT if tk == 1 else None))
                v_c = (sT, sT.n)
                hc = hg * 4 + c
                slot, toks = xrot.acquire()
                t_x = dma("sync", xrot.sems[slot], xbuf[:, slot * TOK:(slot + 1) * TOK],
                          x_chunk_ap(src, hc, False), waits=toks)
                P.op("vector", lambda e, c=c, slot=slot: e.tensor_tensor(
                         xbuf[:, slot * TOK:(slot + 1) * TOK],
                         pd[c][:, :], xbuf[:, slot * TOK:(slot + 1) * TOK], ALU.add),
                     waits=[v_c, t_x], inc=sV)
                v_add = (sV, sV.n)
                give_banks(pd_banks[c], v_add)
                t_st = dma("gpsimd", sX[slot], x_chunk_ap(dst, hc, False),
                           xbuf[:, slot * TOK:(slot + 1) * TOK], waits=[v_add])
                xrot.release(slot, t_st)
            tdrot.release(tslot, v_c)
        return [(s, s.n) for s in sX]

    t_m1, t_rs1 = ln_stats(xT)
    t_y1 = y_build(xT, t_m1, t_rs1)
    v_ysd1, h_gate1, w_gate1 = gu_phase(True, t_y1, None)
    x_gate1 = down_phase(v_ysd1, h_gate1, w_gate1, xT, x_s)

    t_m2, t_rs2 = ln_stats(x_s, extra_waits=x_gate1)
    x_gate2 = attn_apply(t_m2, t_rs2)

    t_m3, t_rs3 = ln_stats(x_s, extra_waits=x_gate2)
    t_y3 = y_build(x_s, t_m3, t_rs3)
    v_ysd2, h_gate2, _ = gu_phase(False, t_y3, w_gate1)
    down_phase(v_ysd2, h_gate2, w_gate1, x_s, xoutT)

    final_waits = ([(sGt, sGt.n), (sGm, sGm.n)] +
                   [(s, s.n) for s in sX] + [(s, s.n) for s in sH] +
                   [(s, s.n) for s in sWG] + [(s, s.n) for s in sWD])
    P.op("gpsimd", lambda e: e.nop(), waits=final_waits, inc=None)

    with nc.Block() as block:
        @block.sync
        def _(eng):
            P.emit("sync", eng)

        @block.scalar
        def _(eng):
            P.emit("scalar", eng)

        @block.vector
        def _(eng):
            P.emit("vector", eng)

        @block.tensor
        def _(eng):
            P.emit("tensor", eng)

        @block.gpsimd
        def _(eng):
            P.emit("gpsimd", eng)

    return nc


def kernel(L0_attn_x, L0_ffn_qweight_gu, L0_ffn_qzeros_gu, L0_ffn_scales_gu,
           L0_ffn_g_idx_gu, L0_ffn_qweight_down, L0_ffn_qzeros_down,
           L0_ffn_scales_down, L0_ffn_g_idx_down):
    x = np.asarray(L0_attn_x, dtype=np.float32).reshape(NCORES * TOK, HID)
    qw_gu_a = np.ascontiguousarray(
        np.asarray(L0_ffn_qweight_gu, dtype=np.uint32)
        .reshape(32, 16, 86, 256).transpose(2, 1, 0, 3)).reshape(512, 22016)
    qz_gu_a = np.ascontiguousarray(np.asarray(L0_ffn_qzeros_gu, dtype=np.uint32))
    sc_gu_a = np.ascontiguousarray(np.asarray(L0_ffn_scales_gu, dtype=np.float32))
    qw_d_a = np.ascontiguousarray(np.asarray(L0_ffn_qweight_down, dtype=np.uint32))
    qz_d_a = np.ascontiguousarray(np.asarray(L0_ffn_qzeros_down, dtype=np.uint32))
    sc_d_a = np.ascontiguousarray(np.asarray(L0_ffn_scales_down, dtype=np.float32))
    gidx_gu = np.asarray(L0_ffn_g_idx_gu, dtype=np.int32)
    gidx_d = np.asarray(L0_ffn_g_idx_down, dtype=np.int32)

    gidx_gu_p = np.ascontiguousarray(gidx_gu.reshape(HC, 128)[:, PERM].reshape(-1))
    gidx_d_p = np.ascontiguousarray(gidx_d.reshape(ICN, 128)[:, PERM].reshape(-1))
    gidx_gu_pt = np.ascontiguousarray(gidx_gu_p.reshape(HC, 128).T)
    gidx_d_pt = np.ascontiguousarray(gidx_d_p.reshape(ICN, 128).T)

    _sh = (4 * (np.arange(128, dtype=np.uint32) // 16)).astype(np.uint32)
    c_shift = (np.uint32(15) << _sh).astype(np.uint32)
    c_pw2 = np.ldexp(np.float32(1.0), -_sh.astype(np.int32)).astype(np.float32)
    c_iota32 = np.arange(NG, dtype=np.float32)
    c_iota86 = np.arange(ND, dtype=np.float32)
    c_ones = np.ones(128, dtype=np.float32)

    nc = build_program()
    in_maps = []
    for k in range(NCORES):
        xk = np.ascontiguousarray(x[k * TOK:(k + 1) * TOK, :].T)
        in_maps.append({
            "xT": xk, "qw_gu": qw_gu_a, "qz_gu": qz_gu_a, "sc_gu": sc_gu_a,
            "qw_d": qw_d_a, "qz_d": qz_d_a, "sc_d": sc_d_a,
            "gidx_gu_p": gidx_gu_p, "gidx_d_p": gidx_d_p,
            "gidx_gu_pt": gidx_gu_pt, "gidx_d_pt": gidx_d_pt,
            "c_shift": c_shift, "c_pw2": c_pw2,
            "c_iota32": c_iota32, "c_iota86": c_iota86,
            "c_ones": c_ones,
        })
    res = run_bass_kernel_spmd(nc, in_maps, core_ids=list(range(NCORES)))
    out = np.empty((NCORES * TOK, HID), dtype=np.float32)
    for k in range(NCORES):
        out[k * TOK:(k + 1) * TOK, :] = np.asarray(res.results[k]["xoutT"]).T
    return out.reshape(2, HID, HID)


# version 13
# speedup vs baseline: 1.1257x; 1.1257x over previous
import numpy as np
import concourse.bass as bass
import concourse.mybir as mybir
from concourse.bass_utils import run_bass_kernel_spmd

F32 = mybir.dt.float32
BF16 = mybir.dt.bfloat16
U32 = mybir.dt.uint32
U8 = mybir.dt.uint8
I32 = mybir.dt.int32
AF = mybir.ActivationFunctionType
ALU = mybir.AluOpType

HID = 4096
INTER = 11008
TOK = 1024
NCORES = 8
HC = HID // 128
ICN = INTER // 128
NG = 32
ND = 86
GU_COLS = 2 * INTER
EPS = 1e-5
GW = 256

PERM = (8 * (np.arange(128) % 16) + np.arange(128) // 16).astype(np.int64)


class Sem:
    def __init__(self, nc, name, dma=False):
        self.name = name
        self.h = nc.semaphore(name).__enter__()
        self.step = 16 if dma else 1
        self.n = 0

    def inc(self):
        self.n += self.step
        return self.n


class Plan:
    ENGINES = ("sync", "scalar", "vector", "tensor", "gpsimd")

    def __init__(self):
        self.ops = {e: [] for e in self.ENGINES}

    def op(self, engine, fn, waits=(), inc=None):
        ws = []
        for wv in waits:
            if wv is None:
                continue
            s, v = wv
            assert v <= s.n, f"wait on {s.name} for {v} but only {s.n} planned"
            ws.append((s, v))
        self.ops[engine].append((ws, fn, inc))
        if inc is not None:
            inc.inc()

    def emit(self, engine_name, eng):
        waited = {}
        for ws, fn, inc in self.ops[engine_name]:
            for (s, v) in ws:
                if waited.get(s.name, 0) < v:
                    eng.wait_ge(s.h, v)
                    waited[s.name] = v
            inst = fn(eng)
            if inc is not None:
                inst.then_inc(inc.h, inc.step)


class Rot:

    def __init__(self, nslots, nc=None, prefix=None):
        self.n = nslots
        self.tok = [[] for _ in range(nslots)]
        self.i = 0
        self.sems = ([Sem(nc, f"{prefix}{i}", dma=True) for i in range(nslots)]
                     if nc is not None else None)

    def acquire(self):
        s = self.i % self.n
        self.i += 1
        toks = self.tok[s]
        self.tok[s] = []
        return s, toks

    def release(self, slot, tok):
        if tok is not None:
            self.tok[slot].append(tok)


def build_program():
    nc = bass.Bass("TRN2", target_bir_lowering=False,
                   detect_race_conditions=False)

    xT = nc.dram_tensor("xT", [HID, TOK], F32, kind="ExternalInput")
    qw_gu = nc.dram_tensor("qw_gu", [(GU_COLS // GW) * 128 * HC * GW], U8,
                           kind="ExternalInput")
    qz_gu = nc.dram_tensor("qz_gu", [NG, GU_COLS // 8], U32, kind="ExternalInput")
    sc_gu = nc.dram_tensor("sc_gu", [NG, GU_COLS], F32, kind="ExternalInput")
    qw_d = nc.dram_tensor("qw_d", [ICN * 128 * 16 * GW], U8,
                          kind="ExternalInput")
    qz_d = nc.dram_tensor("qz_d", [ND, HID // 8], U32, kind="ExternalInput")
    sc_d = nc.dram_tensor("sc_d", [ND, HID], F32, kind="ExternalInput")
    gidx_gu_p = nc.dram_tensor("gidx_gu_p", [HID], I32, kind="ExternalInput")
    gidx_d_p = nc.dram_tensor("gidx_d_p", [INTER], I32, kind="ExternalInput")
    gidx_gu_pt = nc.dram_tensor("gidx_gu_pt", [128, HC], I32, kind="ExternalInput")
    gidx_d_pt = nc.dram_tensor("gidx_d_pt", [128, ICN], I32, kind="ExternalInput")
    c_shift = nc.dram_tensor("c_shift", [128], U8, kind="ExternalInput")
    c_pw2 = nc.dram_tensor("c_pw2", [128], F32, kind="ExternalInput")
    c_iota32 = nc.dram_tensor("c_iota32", [NG], F32, kind="ExternalInput")
    c_iota86 = nc.dram_tensor("c_iota86", [ND], F32, kind="ExternalInput")
    c_ones = nc.dram_tensor("c_ones", [128], F32, kind="ExternalInput")
    xoutT = nc.dram_tensor("xoutT", [HID, TOK], F32, kind="ExternalOutput")

    wg_s = nc.dram_tensor("wg_s", [HID, GU_COLS], BF16)
    wd_s = nc.dram_tensor("wd_s", [INTER, HID], BF16)
    h_s = nc.dram_tensor("h_s", [INTER, TOK], BF16)
    tg_s = nc.dram_tensor("tg_s", [NG, GU_COLS], BF16)
    tgl_s = nc.dram_tensor("tgl_s", [NG, GU_COLS], BF16)
    td_s = nc.dram_tensor("td_s", [ND, HID], BF16)
    tdl_s = nc.dram_tensor("tdl_s", [ND, HID], BF16)
    x_s = nc.dram_tensor("x_s", [HID, TOK], F32)
    mrs_s = nc.dram_tensor("mrs_s", [2, TOK], F32)

    import contextlib
    ctx = contextlib.ExitStack()
    sb = lambda name, shape, dt: ctx.enter_context(nc.sbuf_tensor(name, shape, dt))

    y_sb = sb("y_sb", [128, HC * TOK], BF16)
    wbuf = sb("wbuf", [128, 2 * HC * GW], BF16)
    stash = sb("stash", [128, 2 * TOK], F32)
    silu_b = sb("silu_b", [128, 2 * TOK], F32)
    hbuf = sb("hbuf", [128, 2 * TOK], BF16)
    qbuf = sb("qbuf", [128, 6 * 2 * GW], U8)
    wdtile = sb("wdtile", [128, 4 * GW], BF16)
    scb = sb("scb", [NG, 2 * GW], BF16)
    tgb = sb("tgb", [NG, 2 * 2 * GW], BF16)
    scd_h = sb("scd_h", [ND, HID], BF16)
    tdb = sb("tdb", [ND, 2 * 2 * 512], BF16)
    gt_gu = sb("gt_gu", [NG, HC * 128], BF16)
    g_gu = sb("g_gu", [128, HC * NG], BF16)
    gtd_b = sb("gtd_b", [ND, 2 * 128], BF16)
    gd_b = sb("gd_b", [128, 4 * ND], BF16)
    gta_i = sb("gta_i", [128, HC], I32)
    gta_f = sb("gta_f", [128, HC], F32)
    gtd_i = sb("gtd_i", [128, ICN], I32)
    gtd_f = sb("gtd_f", [128, ICN], F32)
    grep_i = sb("grep_i", [ND, 128], I32)
    grep_f = sb("grep_f", [ND, 128], F32)
    iota32c = sb("iota32c", [NG, 1], F32)
    iota86c = sb("iota86c", [ND, 1], F32)
    iota86r = sb("iota86r", [128, ND], F32)
    iota32r = sb("iota32r", [128, NG], F32)
    shift_c = sb("shift_c", [128, 1], U8)
    pw2_c = sb("pw2_c", [128, 1], F32)
    ones_c = sb("ones_c", [128, 1], F32)
    ys_sb = sb("ys_sb", [NG, TOK], BF16)
    ysl_sb = sb("ysl_sb", [NG, TOK], BF16)
    ysd_sb = sb("ysd_sb", [ND, TOK], BF16)
    ysdl_sb = sb("ysdl_sb", [ND, TOK], BF16)
    xbuf = sb("xbuf", [128, 3 * TOK], F32)
    sqb = sb("sqb", [128, 1 * TOK], F32)
    tmpb = sb("tmpb", [128, 2 * TOK], F32)
    hinb = sb("hinb", [128, 4 * TOK], BF16)
    wdb = sb("wdb", [128, 4 * 512], BF16)
    esb = sb("esb", [128, 4 * GW], BF16)
    mB = sb("mB", [128, TOK], F32)
    rsB = sb("rsB", [128, TOK], F32)
    silu_bf = silu_b[:, :].bitcast(BF16)
    tchunk_b = silu_bf[0:ND, 0:512]
    tchunk_l = silu_bf[0:ND, 512:1024]
    m_sb = mB[0:1, :]
    rs_sb = rsB[0:1, :]
    mm_sb = stash[0:1, 0:TOK]
    xbuf_u = xbuf[:, :].bitcast(U32)
    zq_u = xbuf_u[0:ND, 0:1376]
    zq_t = xbuf_u[0:ND, 1376:1440]
    scch = xbuf[0:ND, 1440:2464]
    tchunk = xbuf[0:ND, 2464:2976]

    PS = ctx.enter_context(nc.psum_tensor("PS", [128, 4096], F32))
    pg = [PS[:, 0:1024], PS[:, 1024:2048]]
    pE = [PS[:, 2048:2560], PS[:, 2560:3072]]
    pys = PS[0:NG, 3072:4096]
    pysd = PS[0:ND, 3072:4096]
    pd = [PS[:, i * 1024:(i + 1) * 1024] for i in range(4)]
    pstat_s = PS[0:1, 0:1024]
    pstat_q = PS[0:1, 1024:2048]

    bank_tok = {i: [] for i in range(8)}

    def take_banks(banks):
        out = []
        for b in banks:
            out += bank_tok[b]
            bank_tok[b] = []
        return out

    def give_banks(banks, tok):
        if tok is None:
            return
        for b in banks:
            bank_tok[b].append(tok)

    sT = Sem(nc, "sT")
    sV = Sem(nc, "sV")
    sA = Sem(nc, "sA")
    sC = Sem(nc, "sC", dma=True)
    sSCD = Sem(nc, "sSCD", dma=True)
    sQZ = Sem(nc, "sQZ", dma=True)
    sSCC = Sem(nc, "sSCC", dma=True)
    sGt = Sem(nc, "sGt", dma=True)
    sGm = Sem(nc, "sGm", dma=True)
    sMB = Sem(nc, "sMB", dma=True)
    sRSB = Sem(nc, "sRSB", dma=True)
    sGrep = Sem(nc, "sGrep", dma=True)
    sGtab = Sem(nc, "sGtab", dma=True)
    sWG = [Sem(nc, f"sWG{i}", dma=True) for i in range(2)]
    sWD = [Sem(nc, f"sWD{i}", dma=True) for i in range(2)]
    sH = [Sem(nc, f"sH{i}", dma=True) for i in range(3)]
    sX = [Sem(nc, f"sX{i}", dma=True) for i in range(3)]

    P = Plan()

    def dma(engine, sem, dst, src, waits=()):
        def fn(eng, dst=dst, src=src):
            return eng.dma_start(dst, src)
        P.op(engine, fn, waits=waits, inc=sem)
        return (sem, sem.n)

    xrot = Rot(3, nc, "sXB")
    qrot = Rot(6, nc, "sQB")
    wrot = Rot(2, nc, "sWB")
    scrot = Rot(2, nc, "sSB")
    tgrot = Rot(2, nc, "sTGb")
    tdrot = Rot(2, nc, "sTDb")
    hin_rot = Rot(4, nc, "sHI")
    wdin_rot = Rot(4, nc, "sWI")
    sqrot = Rot(1)
    tmprot = Rot(2)
    erot = Rot(2)
    esrot = Rot(4)
    hrot = Rot(2)
    gdrot = Rot(4)
    gtdrot = Rot(2)
    wdorot = Rot(4)
    mrs_free = []

    for dst, src_t in [
        (shift_c[:, :], bass.AP(c_shift, 0, [[1, 128], [1, 1]])),
        (pw2_c[:, :], bass.AP(c_pw2, 0, [[1, 128], [1, 1]])),
        (iota32c[:, :], bass.AP(c_iota32, 0, [[1, NG], [1, 1]])),
        (iota86c[:, :], bass.AP(c_iota86, 0, [[1, ND], [1, 1]])),
        (ones_c[:, :], bass.AP(c_ones, 0, [[1, 128], [1, 1]])),
        (iota86r[:, :], bass.AP(c_iota86, 0, [[0, 128], [1, ND]])),
        (iota32r[:, :], bass.AP(c_iota32, 0, [[0, 128], [1, NG]])),
    ]:
        dma("sync", sC, dst, src_t)
    t_const = (sC, sC.n)

    v_scdh = None
    for c0 in range(0, HID, 1024):
        t_scd = dma("sync", sSCD, scch[:, 0:1024],
                    bass.AP(sc_d, c0, [[HID, ND], [1, 1024]]),
                    waits=[v_scdh])
        P.op("vector", lambda e, c0=c0: e.tensor_scalar_mul(
                 scd_h[:, c0:c0 + 1024], scch[:, 0:1024], 0.5),
             waits=[t_scd], inc=sV)
        v_scdh = (sV, sV.n)

    t_gta = dma("scalar", sGtab, gta_i[:, :], gidx_gu_pt[:, :])
    t_gtd2 = dma("scalar", sGtab, gtd_i[:, :], gidx_d_pt[:, :])
    tab_gate = (sGtab, sGtab.n)
    P.op("vector", lambda e: e.tensor_copy(gta_f[:, :], gta_i[:, :]),
         waits=[tab_gate], inc=sV)
    P.op("vector", lambda e: e.tensor_copy(gtd_f[:, :], gtd_i[:, :]),
         waits=[tab_gate], inc=sV)
    v_gtab = (sV, sV.n)
    grep_free = None
    v_ggu_done = None
    for hc in range(HC):
        P.op("vector", lambda e, hc=hc: e.tensor_scalar(
                 g_gu[:, hc * NG:(hc + 1) * NG], iota32r[:, :],
                 gta_f[:, hc:hc + 1], None, ALU.is_equal),
             waits=[t_const, v_gtab], inc=sV)
        tg2 = dma("scalar", sGrep, grep_i[0:NG, 0:128],
                  bass.AP(gidx_gu_p, hc * 128, [[0, NG], [1, 128]]),
                  waits=[grep_free])
        P.op("vector", lambda e: e.tensor_copy(grep_f[0:NG, 0:128], grep_i[0:NG, 0:128]),
             waits=[tg2], inc=sV)
        v_cp_g = (sV, sV.n)
        P.op("vector", lambda e, hc=hc: e.tensor_scalar(
                 gt_gu[:, hc * 128:(hc + 1) * 128], grep_f[0:NG, 0:128], iota32c[:, :],
                 None, ALU.is_equal),
             waits=[t_const, v_cp_g], inc=sV)
        grep_free = (sV, sV.n)
        v_ggu_done = (sV, sV.n)

    def build_T(qz, sc_dram, scratch, scratch_l, ngrp, ncp_packed, out_cols, t_prev):
        t_last = t_prev
        sc_guard = t_prev
        tb_guard = None
        for half in range(2):
            p0 = (ncp_packed // 2) * half
            pw_half = ncp_packed - p0 if half == 1 else ncp_packed // 2
            if pw_half == 0:
                continue
            t_qz = dma("sync", sQZ, zq_u[0:ngrp, 0:pw_half],
                       bass.AP(qz, p0, [[ncp_packed, ngrp], [1, pw_half]]),
                       waits=[t_last])
            col0 = p0 * 8
            colw = pw_half * 8
            for c0 in range(col0, col0 + colw, 1024):
                cw = min(1024, col0 + colw - c0)
                t_sc = dma("sync", sSCC, scch[0:ngrp, 0:cw],
                           bass.AP(sc_dram, c0, [[out_cols, ngrp], [1, cw]]),
                           waits=[sc_guard])
                for s0 in range(0, cw, 512):
                    sw = min(512, cw - s0)
                    pw = sw // 8
                    bp = (c0 + s0) // 8 - p0
                    for k in range(8):
                        P.op("vector", lambda e, k=k, pw=pw, bp=bp, ngrp=ngrp:
                             e.tensor_scalar(zq_t[0:ngrp, 0:pw],
                                             zq_u[0:ngrp, bp:bp + pw],
                                             4 * k, 15,
                                             ALU.logical_shift_right, ALU.bitwise_and),
                             waits=[t_qz], inc=sV)
                        v_zq = (sV, sV.n)
                        P.op("vector", lambda e, k=k, pw=pw, s0=s0, ngrp=ngrp:
                             e.scalar_tensor_tensor(
                                 bass.AP(tchunk.tensor, tchunk.offset + k,
                                         [[3 * TOK, ngrp], [8, pw]]),
                                 zq_t[0:ngrp, 0:pw], 1.0,
                                 bass.AP(scch.tensor, scch.offset + s0 + k,
                                         [[3 * TOK, ngrp], [8, pw]]),
                                 ALU.add, ALU.mult),
                             waits=[t_sc, v_zq], inc=sV)
                    v_stt = (sV, sV.n)
                    P.op("vector", lambda e, ngrp=ngrp, sw=sw: e.tensor_copy(
                             tchunk_b[0:ngrp, 0:sw], tchunk[0:ngrp, 0:sw]),
                         waits=[tb_guard, v_stt], inc=sV)
                    vv = (sV, sV.n)
                    dma("gpsimd", sGt,
                        bass.AP(scratch, c0 + s0, [[out_cols, ngrp], [1, sw]]),
                        tchunk_b[0:ngrp, 0:sw], waits=[vv])
                    P.op("vector", lambda e, ngrp=ngrp, sw=sw: e.tensor_sub(
                             tchunk_l[0:ngrp, 0:sw], tchunk[0:ngrp, 0:sw],
                             tchunk_b[0:ngrp, 0:sw]),
                         waits=[tb_guard, vv], inc=sV)
                    vv2 = (sV, sV.n)
                    tb_guard = dma("gpsimd", sGt,
                                   bass.AP(scratch_l, c0 + s0,
                                           [[out_cols, ngrp], [1, sw]]),
                                   tchunk_l[0:ngrp, 0:sw], waits=[vv2])
                sc_guard = (sV, sV.n)
            t_last = (sV, sV.n)
        return t_last

    v_t1 = build_T(qz_gu, sc_gu, tg_s, tgl_s, NG, GU_COLS // 8, GU_COLS, v_scdh)
    v_t2 = build_T(qz_d, sc_d, td_s, tdl_s, ND, HID // 8, HID, v_t1)
    t_T = (sGt, sGt.n)
    for i in range(xrot.n):
        xrot.tok[i].append(v_t2)

    def x_chunk_ap(src, hc, permuted):
        if permuted:
            return bass.AP(src, hc * 128 * TOK, [[TOK, 8], [8 * TOK, 16], [1, TOK]])
        return bass.AP(src, hc * 128 * TOK, [[TOK, 128], [1, TOK]])

    def ln_stats(src, extra_waits=()):
        tk_s = take_banks([0, 1])
        tk_q = take_banks([2, 3])
        v_mm = None
        for hc in range(HC):
            slot, toks = xrot.acquire()
            w = list(toks) + [t_const] + list(extra_waits)
            t_x = dma("sync", xrot.sems[slot], xbuf[:, slot * TOK:(slot + 1) * TOK],
                      x_chunk_ap(src, hc, False), waits=w)
            qslot, qtoks = sqrot.acquire()
            P.op("scalar", lambda e, slot=slot, qslot=qslot: e.activation(
                     sqb[:, qslot * TOK:(qslot + 1) * TOK],
                     xbuf[:, slot * TOK:(slot + 1) * TOK], AF.Square),
                 waits=list(qtoks) + [t_x], inc=sA)
            v_sq = (sA, sA.n)
            for tk in range(2):
                P.op("tensor", lambda e, slot=slot, tk=tk, hc=hc: e.matmul(
                         pstat_s[:, tk * 512:(tk + 1) * 512], ones_c[:, :],
                         xbuf[:, slot * TOK + tk * 512: slot * TOK + (tk + 1) * 512],
                         start=(hc == 0), stop=(hc == HC - 1)),
                     waits=(tk_s if hc == 0 and tk == 0 else []) + [t_x], inc=None)
                P.op("tensor", lambda e, qslot=qslot, tk=tk, hc=hc: e.matmul(
                         pstat_q[:, tk * 512:(tk + 1) * 512], ones_c[:, :],
                         sqb[:, qslot * TOK + tk * 512: qslot * TOK + (tk + 1) * 512],
                         start=(hc == 0), stop=(hc == HC - 1)),
                     waits=(tk_q if hc == 0 and tk == 0 else []) + [v_sq],
                     inc=(sT if tk == 1 else None))
            v_mm = (sT, sT.n)
            xrot.release(slot, v_mm)
            sqrot.release(qslot, v_mm)
        w_m = list(mrs_free)
        mrs_free.clear()
        P.op("vector", lambda e: e.tensor_scalar_mul(m_sb, pstat_s[:, :], 1.0 / HID),
             waits=[v_mm] + w_m, inc=sV)
        v_m0 = (sV, sV.n)
        give_banks([0, 1], v_m0)
        P.op("vector", lambda e: e.tensor_scalar_mul(rs_sb, pstat_q[:, :], 1.0 / HID),
             inc=sV)
        v_r0 = (sV, sV.n)
        give_banks([2, 3], v_r0)
        P.op("vector", lambda e: e.tensor_mul(mm_sb, m_sb, m_sb),
             waits=[v_m0], inc=sV)
        v_mm2 = (sV, sV.n)
        P.op("vector", lambda e: e.scalar_tensor_tensor(
                 rs_sb, rs_sb, EPS, mm_sb, ALU.add, ALU.subtract),
             waits=[v_r0, v_mm2], inc=sV)
        v_var = (sV, sV.n)
        P.op("scalar", lambda e: e.activation(rs_sb, rs_sb, AF.Ln),
             waits=[v_var], inc=sA)
        v_ln = (sA, sA.n)
        P.op("scalar", lambda e: e.activation(rs_sb, rs_sb, AF.Exp, scale=-0.5),
             waits=[v_ln], inc=sA)
        v_rs = (sA, sA.n)
        t1 = dma("gpsimd", sGm, bass.AP(mrs_s, 0, [[TOK, 1], [1, TOK]]), m_sb,
                 waits=[v_var])
        dma("gpsimd", sGm, bass.AP(mrs_s, TOK, [[TOK, 1], [1, TOK]]), rs_sb,
            waits=[v_rs])
        pair_done = (sGm, sGm.n)
        t_m = dma("sync", sMB, mB[:, :], bass.AP(mrs_s, 0, [[0, 128], [1, TOK]]),
                  waits=[pair_done] + w_m)
        t_rs = dma("sync", sRSB, rsB[:, :], bass.AP(mrs_s, TOK, [[0, 128], [1, TOK]]),
                   waits=[pair_done])
        return t_m, t_rs

    def y_build(src, t_m, t_rs):
        last = None
        for hc in range(HC):
            slot, toks = xrot.acquire()
            t_x = dma("sync", xrot.sems[slot], xbuf[:, slot * TOK:(slot + 1) * TOK],
                      x_chunk_ap(src, hc, True), waits=toks)
            tslot, ttoks = tmprot.acquire()
            P.op("vector", lambda e, slot=slot, tslot=tslot: e.tensor_sub(
                     tmpb[:, tslot * TOK:(tslot + 1) * TOK],
                     xbuf[:, slot * TOK:(slot + 1) * TOK], mB[:, :]),
                 waits=list(ttoks) + [t_x, t_m], inc=sV)
            xrot.release(slot, (sV, sV.n))
            P.op("vector", lambda e, hc=hc, tslot=tslot: e.tensor_mul(
                     y_sb[:, hc * TOK:(hc + 1) * TOK],
                     tmpb[:, tslot * TOK:(tslot + 1) * TOK], rsB[:, :]),
                 waits=[t_rs], inc=sV)
            tmprot.release(tslot, (sV, sV.n))
            last = (sV, sV.n)
        mrs_free.append(last)
        return last

    def attn_apply(t_m, t_rs):
        for hc in range(HC):
            slot, toks = xrot.acquire()
            t_x = dma("sync", xrot.sems[slot], xbuf[:, slot * TOK:(slot + 1) * TOK],
                      x_chunk_ap(x_s, hc, False), waits=toks)
            tslot, ttoks = tmprot.acquire()
            xs = xbuf[:, slot * TOK:(slot + 1) * TOK]
            ts = tmpb[:, tslot * TOK:(tslot + 1) * TOK]
            P.op("vector", lambda e, xs=xs, ts=ts: e.tensor_sub(ts, xs, mB[:, :]),
                 waits=list(ttoks) + [t_x, t_m], inc=None)
            P.op("vector", lambda e, xs=xs, ts=ts: e.tensor_mul(ts, ts, rsB[:, :]),
                 waits=[t_rs], inc=None)
            P.op("vector", lambda e, xs=xs, ts=ts: e.tensor_add(xs, xs, ts), inc=sV)
            tmprot.release(tslot, (sV, sV.n))
            v_x2 = (sV, sV.n)
            t_st = dma("gpsimd", sX[slot], x_chunk_ap(x_s, hc, False), xs,
                       waits=[v_x2])
            xrot.release(slot, t_st)
        mrs_free.append((sV, sV.n))
        return [(s, s.n) for s in sX]

    def gu_phase(dequant, t_y_done, t_w_gate):
        pg_banks = [[0, 1], [2, 3]]
        stash_ready = [None, None]
        silu_free = [t_T, t_T]
        cp_counter = [0]
        grep_gate = [grep_free]

        tk67 = take_banks([6, 7])
        for hc in range(HC):
            for tk in range(2):
                P.op("tensor", lambda e, hc=hc, tk=tk: e.matmul(
                         pys[:, tk * 512:(tk + 1) * 512],
                         g_gu[:, hc * NG:(hc + 1) * NG],
                         y_sb[:, hc * TOK + tk * 512: hc * TOK + (tk + 1) * 512],
                         start=(hc == 0), stop=(hc == HC - 1)),
                     waits=(tk67 if hc == 0 and tk == 0 else []) + [t_y_done, v_ggu_done],
                     inc=(sT if hc == HC - 1 and tk == 1 else None))
        v_ys = (sT, sT.n)
        P.op("scalar", lambda e: e.activation(ys_sb[:, :], pys[:, :], AF.Identity,
                                              scale=-1.0),
             waits=[v_ys], inc=sA)
        v_ysb = (sA, sA.n)
        P.op("vector", lambda e: e.scalar_tensor_tensor(
                 ysl_sb[:, :], pys[:, :], -1.0, ys_sb[:, :],
                 ALU.mult, ALU.subtract),
             waits=[v_ysb], inc=sV)
        v_ysb_l = (sV, sV.n)
        ysd_gate = v_ysb_l

        def wd_begin(ic):
            gslot, gtoks = gtdrot.acquire()
            tgi = dma("scalar", sGrep, grep_i[0:ND, 0:128],
                      bass.AP(gidx_d_p, ic * 128, [[0, ND], [1, 128]]),
                      waits=gtoks + [grep_gate[0]])
            P.op("vector", lambda e: e.tensor_copy(grep_f[0:ND, 0:128],
                                                   grep_i[0:ND, 0:128]),
                 waits=[tgi], inc=sV)
            v_cp_d = (sV, sV.n)
            P.op("vector", lambda e, gslot=gslot: e.tensor_scalar(
                     gtd_b[:, gslot * 128:(gslot + 1) * 128], grep_f[0:ND, 0:128],
                     iota86c[:, :], None, ALU.is_equal),
                 waits=[t_const, v_cp_d], inc=sV)
            v_gtd = (sV, sV.n)
            grep_gate[0] = v_gtd
            return {"gslot": gslot, "v_gtd": v_gtd, "ic": ic, "last_e": None}

        def wd_unit(st, jt):
            gslot = st["gslot"]
            v_gtd = st["v_gtd"]
            ic = st["ic"]
            if True:
                if jt % 2 == 0:
                    qslot, qtoks = qrot.acquire()
                    t_q = dma("sync" if qslot % 2 == 0 else "gpsimd",
                              qrot.sems[qslot],
                              qbuf[:, qslot * 2 * GW:(qslot + 1) * 2 * GW],
                              bass.AP(qw_d, ic * (128 * 16 * GW) + jt * GW,
                                      [[16 * GW, 128], [1, 2 * GW]]),
                              waits=qtoks)
                    P.op("vector", lambda e, qslot=qslot: e.tensor_tensor(
                             qbuf[:, qslot * 2 * GW:(qslot + 1) * 2 * GW],
                             qbuf[:, qslot * 2 * GW:(qslot + 1) * 2 * GW],
                             shift_c[:, 0:1].broadcast_to([128, 2 * GW]),
                             ALU.bitwise_and),
                         waits=[t_q, t_const], inc=sV)
                    st["q4"] = qslot
                    st["v_and"] = (sV, sV.n)
                qslot = st["q4"]
                eslot, etoks = erot.acquire()
                P.op("tensor", lambda e, eslot=eslot, gslot=gslot, jt=jt: e.matmul(
                         pE[eslot][0:128, 0:GW],
                         gtd_b[:, gslot * 128:(gslot + 1) * 128],
                         scd_h[:, jt * GW:(jt + 1) * GW],
                         start=True, stop=True),
                     waits=etoks + [v_gtd, v_scdh], inc=sT)
                v_e = (sT, sT.n)
                st["last_e"] = v_e
                esslot, estoks = esrot.acquire()
                P.op("scalar", lambda e, eslot=eslot, esslot=esslot: e.activation(
                         esb[:, esslot * GW:(esslot + 1) * GW], pE[eslot][0:128, 0:GW],
                         AF.Identity, scale=pw2_c[:, 0:1]),
                     waits=estoks + [v_e, t_const], inc=sA)
                v_es = (sA, sA.n)
                erot.release(eslot, v_es)
                oslot, otoks = wdorot.acquire()
                P.op("vector", lambda e, qslot=qslot, jt=jt, esslot=esslot,
                              oslot=oslot:
                     e.tensor_tensor(
                         wdtile[:, oslot * GW:(oslot + 1) * GW],
                         qbuf[:, qslot * 2 * GW + (jt % 2) * GW:
                              qslot * 2 * GW + (jt % 2 + 1) * GW],
                         esb[:, esslot * GW:(esslot + 1) * GW], ALU.mult),
                     waits=otoks + [v_es, st["v_and"]], inc=sV)
                v_w = (sV, sV.n)
                esrot.release(esslot, v_w)
                if jt % 2 == 1:
                    qrot.release(qslot, v_w)
                if jt % 2 == 0:
                    st["pend_o"] = oslot
                else:
                    o0 = st.pop("pend_o")
                    assert o0 % 2 == 0 and oslot == o0 + 1
                    t_wb = dma("gpsimd", sWD[(jt // 2) % 2],
                               bass.AP(wd_s, ic * (128 * HID) + (jt - 1) * GW,
                                       [[HID, 128], [1, 2 * GW]]),
                               wdtile[:, o0 * GW:(o0 + 2) * GW], waits=[v_w])
                    wdorot.release(o0, t_wb)
                    wdorot.release(oslot, t_wb)

        def wd_end(st):
            gtdrot.release(st["gslot"], st["last_e"])

        grps = []
        for j in range(INTER // GW):
            grps.append(("g", j * GW))
            grps.append(("u", INTER + j * GW))

        v_ysd_stop = None
        n_up = 0
        npairs = INTER // GW

        def dq_begin(wslot, wtoks, c0):
            sslot, stoks = scrot.acquire()
            t_sc = dma("gpsimd", scrot.sems[sslot],
                       scb[:, sslot * GW: (sslot + 1) * GW],
                       bass.AP(sc_gu, c0, [[GU_COLS, NG], [1, GW]]), waits=stoks)
            return {"wslot": wslot, "wtoks": wtoks, "c0": c0, "sslot": sslot,
                    "t_sc": t_sc, "wready": [], "last_e": None}

        def dq_unit(st, hc):
            wslot = st["wslot"]
            wtoks = st["wtoks"]
            c0 = st["c0"]
            sslot = st["sslot"]
            t_sc = st["t_sc"]
            if hc % 2 == 0:
                qslot, qtoks = qrot.acquire()
                t_q = dma("sync" if qslot % 2 == 0 else "gpsimd",
                          qrot.sems[qslot],
                          qbuf[:, qslot * 2 * GW:(qslot + 1) * 2 * GW],
                          bass.AP(qw_gu, (c0 // GW) * (128 * HC * GW) + hc * GW,
                                  [[HC * GW, 128], [1, 2 * GW]]),
                          waits=qtoks + (wtoks if hc == 0 else []))
                P.op("vector", lambda e, qslot=qslot: e.tensor_tensor(
                         qbuf[:, qslot * 2 * GW:(qslot + 1) * 2 * GW],
                         qbuf[:, qslot * 2 * GW:(qslot + 1) * 2 * GW],
                         shift_c[:, 0:1].broadcast_to([128, 2 * GW]),
                         ALU.bitwise_and),
                     waits=[t_q, t_const], inc=sV)
                st["q4"] = qslot
                st["v_and"] = (sV, sV.n)
            qslot = st["q4"]
            eslot, etoks = erot.acquire()
            P.op("tensor", lambda e, eslot=eslot, hc=hc, sslot=sslot:
                 e.matmul(pE[eslot][0:128, 0:GW],
                          gt_gu[:, hc * 128:(hc + 1) * 128],
                          scb[:, sslot * GW: (sslot + 1) * GW],
                          start=True, stop=True),
                 waits=etoks + [t_sc, v_ggu_done], inc=sT)
            v_e = (sT, sT.n)
            st["last_e"] = v_e
            esslot, estoks = esrot.acquire()
            P.op("scalar", lambda e, eslot=eslot, esslot=esslot: e.activation(
                     esb[:, esslot * GW:(esslot + 1) * GW],
                     pE[eslot][0:128, 0:GW],
                     AF.Identity, scale=pw2_c[:, 0:1]),
                 waits=estoks + [v_e, t_const], inc=sA)
            v_es = (sA, sA.n)
            erot.release(eslot, v_es)
            P.op("vector", lambda e, qslot=qslot, esslot=esslot, wslot=wslot,
                          hc=hc:
                 e.tensor_tensor(
                     wbuf[:, wslot * (HC * GW) + hc * GW:
                          wslot * (HC * GW) + (hc + 1) * GW],
                     qbuf[:, qslot * 2 * GW + (hc % 2) * GW:
                          qslot * 2 * GW + (hc % 2 + 1) * GW],
                     esb[:, esslot * GW:(esslot + 1) * GW], ALU.mult),
                 waits=[v_es, st["v_and"]], inc=sV)
            v_w = (sV, sV.n)
            esrot.release(esslot, v_w)
            if hc % 2 == 1:
                qrot.release(qslot, v_w)
            st["wready"].append(v_w)

        def dq_end(st):
            scrot.release(st["sslot"], st["last_e"])
            wslot = st["wslot"]
            for hb in (0, HC // 2):
                dma("gpsimd", sWG[wslot],
                    bass.AP(wg_s, (st["c0"] // GW) * (128 * HC * GW) + hb * GW,
                            [[HC * GW, 128], [1, (HC // 2) * GW]]),
                    wbuf[:, wslot * (HC * GW) + hb * GW:
                         wslot * (HC * GW) + (hb + HC // 2) * GW],
                    waits=[st["wready"][-1]])
            return st["wready"], (sWG[st["wslot"]], sWG[st["wslot"]].n)

        wd_queue = []
        wd_active = []

        def drain_wd(budget):
            while budget > 0:
                if not wd_active:
                    if not wd_queue:
                        return
                    wd_active.append([wd_begin(wd_queue.pop(0)), 0])
                st, jt = wd_active[0]
                for _ in range(4):
                    wd_unit(st, jt)
                    jt += 1
                budget -= 4
                if jt == HID // GW:
                    wd_end(st)
                    wd_active.pop(0)
                else:
                    wd_active[0][1] = jt

        pend = {}
        pend_state = None
        pend_hc = 0
        for gi, (side, c0) in enumerate(grps):
            ncp = GW // 128
            if dequant:
                if gi == 0:
                    ws0, wt0 = wrot.acquire()
                    st0 = dq_begin(ws0, wt0, grps[0][1])
                    for hc0 in range(HC):
                        dq_unit(st0, hc0)
                    pend[0] = (ws0,) + dq_end(st0)
                if gi + 1 < len(grps):
                    wsN, wtN = wrot.acquire()
                    pend_state = dq_begin(wsN, wtN, grps[gi + 1][1])
                    pend_hc = 0
                else:
                    pend_state = None
                wslot, wready, wb_use_done = pend.pop(gi)
            else:
                wslot, wtoks = wrot.acquire()
                wb_use_done = None
                for hb in range(0, HC, 8):
                    dma("sync", wrot.sems[wslot],
                        wbuf[:, wslot * (HC * GW) + hb * GW:
                             wslot * (HC * GW) + (hb + 8) * GW],
                        bass.AP(wg_s, (c0 // GW) * (128 * HC * GW) + hb * GW,
                                [[HC * GW, 128], [1, 8 * GW]]),
                        waits=(wtoks + list(t_w_gate) if hb == 0 else []))
                wready = [(wrot.sems[wslot], wrot.sems[wslot].n)]
            tslot, ttoks = tgrot.acquire()
            dma("scalar", tgrot.sems[tslot],
                tgb[:, tslot * 2 * GW: tslot * 2 * GW + GW],
                bass.AP(tg_s, c0, [[GU_COLS, NG], [1, GW]]),
                waits=ttoks + [t_T])
            t_tg = dma("scalar", tgrot.sems[tslot],
                       tgb[:, tslot * 2 * GW + GW: (tslot + 1) * 2 * GW],
                       bass.AP(tgl_s, c0, [[GU_COLS, NG], [1, GW]]), waits=[])
            grp_last = None
            for cp in range(ncp):
                if dequant:
                    target = (HC * (cp + 1)) // ncp
                    while pend_state is not None and pend_hc < target:
                        dq_unit(pend_state, pend_hc)
                        pend_hc += 1
                    drain_wd(10)
                pgi = cp_counter[0] % 2
                cp_counter[0] += 1
                tk_banks = take_banks(pg_banks[pgi])
                for hc in range(HC):
                    for tk in range(2):
                        w = []
                        if hc == 0 and tk == 0:
                            w += tk_banks
                            w.append(t_y_done)
                        w.append(wready[hc] if dequant else wready[0])
                        P.op("tensor", lambda e, pgi=pgi, tk=tk, hc=hc, wslot=wslot,
                                      cp=cp:
                             e.matmul(pg[pgi][:, tk * 512:(tk + 1) * 512],
                                      wbuf[:, wslot * (HC * GW) + hc * GW + cp * 128:
                                           wslot * (HC * GW) + hc * GW + (cp + 1) * 128],
                                      y_sb[:, hc * TOK + tk * 512:
                                           hc * TOK + (tk + 1) * 512],
                                      start=(hc == 0), stop=False),
                             waits=w, inc=None)
                for tk in range(2):
                    P.op("tensor", lambda e, pgi=pgi, tk=tk, tslot=tslot, cp=cp:
                         e.matmul(pg[pgi][:, tk * 512:(tk + 1) * 512],
                                  tgb[:, tslot * 2 * GW + cp * 128:
                                       tslot * 2 * GW + (cp + 1) * 128],
                                  ys_sb[:, tk * 512:(tk + 1) * 512],
                                  start=False, stop=False),
                         waits=[t_tg, v_ysb], inc=None)
                    P.op("tensor", lambda e, pgi=pgi, tk=tk, tslot=tslot, cp=cp:
                         e.matmul(pg[pgi][:, tk * 512:(tk + 1) * 512],
                                  tgb[:, tslot * 2 * GW + cp * 128:
                                       tslot * 2 * GW + (cp + 1) * 128],
                                  ysl_sb[:, tk * 512:(tk + 1) * 512],
                                  start=False, stop=False),
                         waits=[v_ysb_l], inc=None)
                    P.op("tensor", lambda e, pgi=pgi, tk=tk, tslot=tslot, cp=cp:
                         e.matmul(pg[pgi][:, tk * 512:(tk + 1) * 512],
                                  tgb[:, tslot * 2 * GW + GW + cp * 128:
                                       tslot * 2 * GW + GW + (cp + 1) * 128],
                                  ys_sb[:, tk * 512:(tk + 1) * 512],
                                  start=False, stop=True),
                         waits=[], inc=sT)
                v_cp = (sT, sT.n)
                grp_last = v_cp
                if side == "g":
                    P.op("scalar", lambda e, pgi=pgi, cp=cp: e.copy(
                             stash[:, cp * TOK:(cp + 1) * TOK], pg[pgi][:, :]),
                         waits=[v_cp], inc=sA)
                    give_banks(pg_banks[pgi], (sA, sA.n))
                    stash_ready[cp] = (sA, sA.n)
                else:
                    ic = (c0 - INTER) // 128 + cp
                    sl = n_up % 2
                    n_up += 1
                    P.op("scalar", lambda e, cp=cp, sl=sl: e.activation(
                             silu_b[:, sl * TOK:(sl + 1) * TOK],
                             stash[:, cp * TOK:(cp + 1) * TOK], AF.Silu),
                         waits=[stash_ready[cp], silu_free[sl]], inc=sA)
                    v_silu = (sA, sA.n)
                    hslot, htoks = hrot.acquire()
                    P.op("vector", lambda e, pgi=pgi, hslot=hslot, sl=sl:
                         e.tensor_tensor(
                             hbuf[:, hslot * TOK:(hslot + 1) * TOK],
                             silu_b[:, sl * TOK:(sl + 1) * TOK],
                             pg[pgi][:, :], ALU.mult),
                         waits=htoks + [v_silu, v_cp], inc=sV)
                    v_h = (sV, sV.n)
                    silu_free[sl] = v_h
                    give_banks(pg_banks[pgi], v_h)
                    gdslot, gdtoks = gdrot.acquire()
                    P.op("vector", lambda e, gdslot=gdslot, ic=ic: e.tensor_scalar(
                             gd_b[:, gdslot * ND:(gdslot + 1) * ND], iota86r[:, :],
                             gtd_f[:, ic:ic + 1], None, ALU.is_equal),
                         waits=gdtoks + [t_const, v_gtab], inc=sV)
                    v_gd = (sV, sV.n)
                    for tk in range(2):
                        P.op("tensor", lambda e, gdslot=gdslot, hslot=hslot, tk=tk,
                                      ic=ic:
                             e.matmul(pysd[:, tk * 512:(tk + 1) * 512],
                                      gd_b[:, gdslot * ND:(gdslot + 1) * ND],
                                      hbuf[:, hslot * TOK + tk * 512:
                                           hslot * TOK + (tk + 1) * 512],
                                      start=(ic == 0), stop=(ic == ICN - 1)),
                             waits=[v_gd, v_h] + ([ysd_gate] if ic == 0 and tk == 0
                                                  else []),
                             inc=(sT if tk == 1 else None))
                    v_ysd_mm = (sT, sT.n)
                    gdrot.release(gdslot, v_ysd_mm)
                    hrot.release(hslot, v_ysd_mm)
                    inter0 = c0 - INTER + cp * 128
                    t_h = dma("gpsimd", sH[hslot],
                              bass.AP(h_s, inter0 * TOK,
                                      [[TOK, 16], [16 * TOK, 8], [1, TOK]]),
                              hbuf[:, hslot * TOK:(hslot + 1) * TOK], waits=[v_h])
                    hrot.release(hslot, t_h)
                    if ic == ICN - 1:
                        v_ysd_stop = v_ysd_mm
            tgrot.release(tslot, grp_last)
            wrot.release(wslot, grp_last)
            wrot.release(wslot, wb_use_done)
            if dequant and pend_state is not None:
                while pend_hc < HC:
                    dq_unit(pend_state, pend_hc)
                    pend_hc += 1
                pend[gi + 1] = (pend_state["wslot"],) + dq_end(pend_state)
                pend_state = None
            if dequant and side == "u":
                k = gi // 2
                lo = (k * ICN) // npairs
                hi = ((k + 1) * ICN) // npairs
                wd_queue.extend(range(lo, hi))
        if dequant:
            drain_wd(10 ** 9)
        P.op("scalar", lambda e: e.activation(ysd_sb[:, :], pysd[:, :], AF.Identity,
                                              scale=-0.5),
             waits=[v_ysd_stop], inc=sA)
        v_ysd_h = (sA, sA.n)
        P.op("vector", lambda e: e.scalar_tensor_tensor(
                 ysdl_sb[:, :], pysd[:, :], -0.5, ysd_sb[:, :],
                 ALU.mult, ALU.subtract),
             waits=[v_ysd_h], inc=sV)
        v_ysd_sb = (sV, sV.n)
        give_banks([6, 7], v_ysd_sb)
        for i, toks in enumerate(erot.tok):
            for t in toks:
                bank_tok[4 + i].append(t)
            erot.tok[i] = []
        erot.i = 0
        h_gate = [(s, s.n) for s in sH]
        w_gate = [(s, s.n) for s in sWG] + [(s, s.n) for s in sWD]
        return v_ysd_sb, h_gate, w_gate

    def down_phase(v_ysd_sb, h_gate, w_gate, src, dst):
        pd_banks = [[0, 1], [2, 3], [4, 5], [6, 7]]
        for hg in range(8):
            tslot, ttoks = tdrot.acquire()
            dma("scalar", tdrot.sems[tslot], tdb[:, tslot * 1024: tslot * 1024 + 512],
                bass.AP(td_s, hg * 512, [[HID, ND], [1, 512]]),
                waits=ttoks + [t_T])
            t_tdsl = dma("scalar", tdrot.sems[tslot],
                         tdb[:, tslot * 1024 + 512: (tslot + 1) * 1024],
                         bass.AP(tdl_s, hg * 512, [[HID, ND], [1, 512]]), waits=[])
            for ic in range(ICN):
                hslot, htoks = hin_rot.acquire()
                t_hin = dma("sync", hin_rot.sems[hslot],
                            hinb[:, hslot * TOK:(hslot + 1) * TOK],
                            bass.AP(h_s, ic * 128 * TOK,
                                    [[TOK, 128], [1, TOK]]),
                            waits=htoks + (h_gate if hg == 0 and ic == 0 else []))
                wslot, wtoks = wdin_rot.acquire()
                t_wd = dma("sync", wdin_rot.sems[wslot],
                           wdb[:, wslot * 512:(wslot + 1) * 512],
                           bass.AP(wd_s, (ic * 128) * HID + hg * 512,
                                   [[HID, 128], [1, 512]]),
                           waits=wtoks + (w_gate if hg == 0 and ic == 0 else []))
                for c in range(4):
                    for tk in range(2):
                        w = [t_hin, t_wd]
                        if ic == 0 and tk == 0:
                            w += take_banks(pd_banks[c])
                        P.op("tensor", lambda e, c=c, tk=tk, wslot=wslot, hslot=hslot,
                                      ic=ic:
                             e.matmul(pd[c][:, tk * 512:(tk + 1) * 512],
                                      wdb[:, wslot * 512 + c * 128:
                                           wslot * 512 + (c + 1) * 128],
                                      hinb[:, hslot * TOK + tk * 512:
                                           hslot * TOK + (tk + 1) * 512],
                                      start=(ic == 0), stop=False),
                             waits=w, inc=(sT if c == 3 and tk == 1 else None))
                v_ic = (sT, sT.n)
                hin_rot.release(hslot, v_ic)
                wdin_rot.release(wslot, v_ic)
            v_c = None
            for c in range(4):
                for tk in range(2):
                    P.op("tensor", lambda e, c=c, tk=tk, tslot=tslot: e.matmul(
                             pd[c][:, tk * 512:(tk + 1) * 512],
                             tdb[:, tslot * 1024 + c * 128:
                                  tslot * 1024 + (c + 1) * 128],
                             ysd_sb[:, tk * 512:(tk + 1) * 512],
                             start=False, stop=False),
                         waits=[t_tdsl, v_ysd_sb], inc=None)
                    P.op("tensor", lambda e, c=c, tk=tk, tslot=tslot: e.matmul(
                             pd[c][:, tk * 512:(tk + 1) * 512],
                             tdb[:, tslot * 1024 + c * 128:
                                  tslot * 1024 + (c + 1) * 128],
                             ysdl_sb[:, tk * 512:(tk + 1) * 512],
                             start=False, stop=False),
                         waits=[], inc=None)
                    P.op("tensor", lambda e, c=c, tk=tk, tslot=tslot: e.matmul(
                             pd[c][:, tk * 512:(tk + 1) * 512],
                             tdb[:, tslot * 1024 + 512 + c * 128:
                                  tslot * 1024 + 512 + (c + 1) * 128],
                             ysd_sb[:, tk * 512:(tk + 1) * 512],
                             start=False, stop=True),
                         waits=[], inc=(sT if tk == 1 else None))
                v_c = (sT, sT.n)
                hc = hg * 4 + c
                slot, toks = xrot.acquire()
                t_x = dma("sync", xrot.sems[slot], xbuf[:, slot * TOK:(slot + 1) * TOK],
                          x_chunk_ap(src, hc, False), waits=toks)
                P.op("vector", lambda e, c=c, slot=slot: e.tensor_tensor(
                         xbuf[:, slot * TOK:(slot + 1) * TOK],
                         pd[c][:, :], xbuf[:, slot * TOK:(slot + 1) * TOK], ALU.add),
                     waits=[v_c, t_x], inc=sV)
                v_add = (sV, sV.n)
                give_banks(pd_banks[c], v_add)
                t_st = dma("gpsimd", sX[slot], x_chunk_ap(dst, hc, False),
                           xbuf[:, slot * TOK:(slot + 1) * TOK], waits=[v_add])
                xrot.release(slot, t_st)
            tdrot.release(tslot, v_c)
        return [(s, s.n) for s in sX]

    t_m1, t_rs1 = ln_stats(xT)
    t_y1 = y_build(xT, t_m1, t_rs1)
    v_ysd1, h_gate1, w_gate1 = gu_phase(True, t_y1, None)
    x_gate1 = down_phase(v_ysd1, h_gate1, w_gate1, xT, x_s)

    t_m2, t_rs2 = ln_stats(x_s, extra_waits=x_gate1)
    x_gate2 = attn_apply(t_m2, t_rs2)

    t_m3, t_rs3 = ln_stats(x_s, extra_waits=x_gate2)
    t_y3 = y_build(x_s, t_m3, t_rs3)
    v_ysd2, h_gate2, _ = gu_phase(False, t_y3, w_gate1)
    down_phase(v_ysd2, h_gate2, w_gate1, x_s, xoutT)

    final_waits = ([(sGt, sGt.n), (sGm, sGm.n)] +
                   [(s, s.n) for s in sX] + [(s, s.n) for s in sH] +
                   [(s, s.n) for s in sWG] + [(s, s.n) for s in sWD])
    P.op("gpsimd", lambda e: e.nop(), waits=final_waits, inc=None)

    with nc.Block() as block:
        @block.sync
        def _(eng):
            P.emit("sync", eng)

        @block.scalar
        def _(eng):
            P.emit("scalar", eng)

        @block.vector
        def _(eng):
            P.emit("vector", eng)

        @block.tensor
        def _(eng):
            P.emit("tensor", eng)

        @block.gpsimd
        def _(eng):
            P.emit("gpsimd", eng)

    return nc


def kernel(L0_attn_x, L0_ffn_qweight_gu, L0_ffn_qzeros_gu, L0_ffn_scales_gu,
           L0_ffn_g_idx_gu, L0_ffn_qweight_down, L0_ffn_qzeros_down,
           L0_ffn_scales_down, L0_ffn_g_idx_down):
    x = np.asarray(L0_attn_x, dtype=np.float32).reshape(NCORES * TOK, HID)
    _v = np.asarray(L0_ffn_qweight_gu, dtype=np.uint32).reshape(32, 16, 86, 256)
    _v8 = np.repeat(_v.view(np.uint8).reshape(32, 16, 86, 256, 4), 2, axis=4)
    qw_gu_a = np.ascontiguousarray(_v8.transpose(2, 4, 1, 0, 3)).reshape(-1)
    qz_gu_a = np.ascontiguousarray(np.asarray(L0_ffn_qzeros_gu, dtype=np.uint32))
    sc_gu_a = np.ascontiguousarray(np.asarray(L0_ffn_scales_gu, dtype=np.float32))
    _w = np.asarray(L0_ffn_qweight_down, dtype=np.uint32).reshape(86, 16, 16, 256)
    _w8 = np.repeat(_w.view(np.uint8).reshape(86, 16, 16, 256, 4), 2, axis=4)
    qw_d_a = np.ascontiguousarray(_w8.transpose(0, 4, 1, 2, 3)).reshape(-1)
    qz_d_a = np.ascontiguousarray(np.asarray(L0_ffn_qzeros_down, dtype=np.uint32))
    sc_d_a = np.ascontiguousarray(np.asarray(L0_ffn_scales_down, dtype=np.float32))
    gidx_gu = np.asarray(L0_ffn_g_idx_gu, dtype=np.int32)
    gidx_d = np.asarray(L0_ffn_g_idx_down, dtype=np.int32)

    gidx_gu_p = np.ascontiguousarray(gidx_gu.reshape(HC, 128)[:, PERM].reshape(-1))
    gidx_d_p = np.ascontiguousarray(gidx_d.reshape(ICN, 128)[:, PERM].reshape(-1))
    gidx_gu_pt = np.ascontiguousarray(gidx_gu_p.reshape(HC, 128).T)
    gidx_d_pt = np.ascontiguousarray(gidx_d_p.reshape(ICN, 128).T)

    _nib = (np.arange(128) // 16) % 2
    c_shift = np.where(_nib == 0, 0x0F, 0xF0).astype(np.uint8)
    c_pw2 = np.ldexp(np.float32(1.0), -4 * _nib.astype(np.int32)).astype(np.float32)
    c_iota32 = np.arange(NG, dtype=np.float32)
    c_iota86 = np.arange(ND, dtype=np.float32)
    c_ones = np.ones(128, dtype=np.float32)

    nc = build_program()
    in_maps = []
    for k in range(NCORES):
        xk = np.ascontiguousarray(x[k * TOK:(k + 1) * TOK, :].T)
        in_maps.append({
            "xT": xk, "qw_gu": qw_gu_a, "qz_gu": qz_gu_a, "sc_gu": sc_gu_a,
            "qw_d": qw_d_a, "qz_d": qz_d_a, "sc_d": sc_d_a,
            "gidx_gu_p": gidx_gu_p, "gidx_d_p": gidx_d_p,
            "gidx_gu_pt": gidx_gu_pt, "gidx_d_pt": gidx_d_pt,
            "c_shift": c_shift, "c_pw2": c_pw2,
            "c_iota32": c_iota32, "c_iota86": c_iota86,
            "c_ones": c_ones,
        })
    res = run_bass_kernel_spmd(nc, in_maps, core_ids=list(range(NCORES)))
    out = np.empty((NCORES * TOK, HID), dtype=np.float32)
    for k in range(NCORES):
        out[k * TOK:(k + 1) * TOK, :] = np.asarray(res.results[k]["xoutT"]).T
    return out.reshape(2, HID, HID)


# version 14
# speedup vs baseline: 1.1746x; 1.0434x over previous
import numpy as np
import concourse.bass as bass
import concourse.mybir as mybir
from concourse.bass_utils import run_bass_kernel_spmd

F32 = mybir.dt.float32
BF16 = mybir.dt.bfloat16
U32 = mybir.dt.uint32
U8 = mybir.dt.uint8
I32 = mybir.dt.int32
AF = mybir.ActivationFunctionType
ALU = mybir.AluOpType

HID = 4096
INTER = 11008
TOK = 1024
NCORES = 8
HC = HID // 128
ICN = INTER // 128
NG = 32
ND = 86
GU_COLS = 2 * INTER
EPS = 1e-5
GW = 256

PERM = (8 * (np.arange(128) % 16) + np.arange(128) // 16).astype(np.int64)


class Sem:
    def __init__(self, nc, name, dma=False):
        self.name = name
        self.h = nc.semaphore(name).__enter__()
        self.step = 16 if dma else 1
        self.n = 0

    def inc(self):
        self.n += self.step
        return self.n


class Plan:
    ENGINES = ("sync", "scalar", "vector", "tensor", "gpsimd")

    def __init__(self):
        self.ops = {e: [] for e in self.ENGINES}

    def op(self, engine, fn, waits=(), inc=None):
        ws = []
        for wv in waits:
            if wv is None:
                continue
            s, v = wv
            assert v <= s.n, f"wait on {s.name} for {v} but only {s.n} planned"
            ws.append((s, v))
        self.ops[engine].append((ws, fn, inc))
        if inc is not None:
            inc.inc()

    def emit(self, engine_name, eng):
        waited = {}
        for ws, fn, inc in self.ops[engine_name]:
            for (s, v) in ws:
                if waited.get(s.name, 0) < v:
                    eng.wait_ge(s.h, v)
                    waited[s.name] = v
            inst = fn(eng)
            if inc is not None:
                inst.then_inc(inc.h, inc.step)


class Rot:

    def __init__(self, nslots, nc=None, prefix=None):
        self.n = nslots
        self.tok = [[] for _ in range(nslots)]
        self.i = 0
        self.sems = ([Sem(nc, f"{prefix}{i}", dma=True) for i in range(nslots)]
                     if nc is not None else None)

    def acquire(self):
        s = self.i % self.n
        self.i += 1
        toks = self.tok[s]
        self.tok[s] = []
        return s, toks

    def release(self, slot, tok):
        if tok is not None:
            self.tok[slot].append(tok)


def build_program():
    nc = bass.Bass("TRN2", target_bir_lowering=False,
                   detect_race_conditions=False)

    xT = nc.dram_tensor("xT", [HID, TOK], F32, kind="ExternalInput")
    qw_gu = nc.dram_tensor("qw_gu", [(GU_COLS // GW) * 128 * HC * GW], U8,
                           kind="ExternalInput")
    qz_gu = nc.dram_tensor("qz_gu", [NG, GU_COLS // 8], U32, kind="ExternalInput")
    sc_gu = nc.dram_tensor("sc_gu", [NG, GU_COLS], F32, kind="ExternalInput")
    qw_d = nc.dram_tensor("qw_d", [ICN * 128 * 16 * GW], U8,
                          kind="ExternalInput")
    qz_d = nc.dram_tensor("qz_d", [ND, HID // 8], U32, kind="ExternalInput")
    sc_d = nc.dram_tensor("sc_d", [ND, HID], F32, kind="ExternalInput")
    gidx_gu_p = nc.dram_tensor("gidx_gu_p", [HID], I32, kind="ExternalInput")
    gidx_d_p = nc.dram_tensor("gidx_d_p", [INTER], I32, kind="ExternalInput")
    gidx_gu_pt = nc.dram_tensor("gidx_gu_pt", [128, HC], I32, kind="ExternalInput")
    gidx_d_pt = nc.dram_tensor("gidx_d_pt", [128, ICN], I32, kind="ExternalInput")
    c_shift = nc.dram_tensor("c_shift", [128], U8, kind="ExternalInput")
    c_pw2 = nc.dram_tensor("c_pw2", [128], F32, kind="ExternalInput")
    c_iota32 = nc.dram_tensor("c_iota32", [NG], F32, kind="ExternalInput")
    c_iota86 = nc.dram_tensor("c_iota86", [ND], F32, kind="ExternalInput")
    c_ones = nc.dram_tensor("c_ones", [128], F32, kind="ExternalInput")
    xoutT = nc.dram_tensor("xoutT", [HID, TOK], F32, kind="ExternalOutput")

    wg_s = nc.dram_tensor("wg_s", [HID, GU_COLS], BF16)
    wd_s = nc.dram_tensor("wd_s", [INTER, HID], BF16)
    h_s = nc.dram_tensor("h_s", [INTER, TOK], BF16)
    tg_s = nc.dram_tensor("tg_s", [NG, GU_COLS], BF16)
    tgl_s = nc.dram_tensor("tgl_s", [NG, GU_COLS], BF16)
    td_s = nc.dram_tensor("td_s", [ND, HID], BF16)
    tdl_s = nc.dram_tensor("tdl_s", [ND, HID], BF16)
    x_s = nc.dram_tensor("x_s", [HID, TOK], F32)
    mrs_s = nc.dram_tensor("mrs_s", [2, TOK], F32)

    import contextlib
    ctx = contextlib.ExitStack()
    sb = lambda name, shape, dt: ctx.enter_context(nc.sbuf_tensor(name, shape, dt))

    y_sb = sb("y_sb", [128, HC * TOK], BF16)
    wbuf = sb("wbuf", [128, 2 * HC * GW], BF16)
    stash = sb("stash", [128, 2 * TOK], F32)
    silu_b = sb("silu_b", [128, 2 * TOK], F32)
    hbuf = sb("hbuf", [128, 2 * TOK], BF16)
    qbuf = sb("qbuf", [128, 8 * 2 * GW], U8)
    wdtile = sb("wdtile", [128, 4 * GW], BF16)
    scb = sb("scb", [NG, 2 * GW], BF16)
    tgb = sb("tgb", [NG, 2 * 2 * GW], BF16)
    scd_h = sb("scd_h", [ND, HID], BF16)
    tdb = sb("tdb", [ND, 2 * 2 * 512], BF16)
    gt_gu = sb("gt_gu", [NG, HC * 128], BF16)
    g_gu = sb("g_gu", [128, HC * NG], BF16)
    gtd_b = sb("gtd_b", [ND, 2 * 128], BF16)
    gd_b = sb("gd_b", [128, 4 * ND], BF16)
    gta_i = sb("gta_i", [128, HC], I32)
    gta_f = sb("gta_f", [128, HC], F32)
    gtd_i = sb("gtd_i", [128, ICN], I32)
    gtd_f = sb("gtd_f", [128, ICN], F32)
    grep_i = sb("grep_i", [ND, 128], I32)
    grep_f = sb("grep_f", [ND, 128], F32)
    iota32c = sb("iota32c", [NG, 1], F32)
    iota86c = sb("iota86c", [ND, 1], F32)
    iota86r = sb("iota86r", [128, ND], F32)
    iota32r = sb("iota32r", [128, NG], F32)
    shift_c = sb("shift_c", [128, 1], U8)
    pw2_c = sb("pw2_c", [128, 1], F32)
    ones_c = sb("ones_c", [128, 1], F32)
    ys_sb = sb("ys_sb", [NG, TOK], BF16)
    ysl_sb = sb("ysl_sb", [NG, TOK], BF16)
    ysd_sb = sb("ysd_sb", [ND, TOK], BF16)
    ysdl_sb = sb("ysdl_sb", [ND, TOK], BF16)
    xbuf = sb("xbuf", [128, 3 * TOK], F32)
    sqb = sb("sqb", [128, 1 * TOK], F32)
    tmpb = sb("tmpb", [128, 2 * TOK], F32)
    hinb = sb("hinb", [128, 4 * TOK], BF16)
    wdb = sb("wdb", [128, 4 * 512], BF16)
    esb = sb("esb", [128, 4 * GW], BF16)
    mB = sb("mB", [128, TOK], F32)
    rsB = sb("rsB", [128, TOK], F32)
    silu_bf = silu_b[:, :].bitcast(BF16)
    tchunk_b = silu_bf[0:ND, 0:512]
    tchunk_l = silu_bf[0:ND, 512:1024]
    m_sb = mB[0:1, :]
    rs_sb = rsB[0:1, :]
    mm_sb = stash[0:1, 0:TOK]
    xbuf_u = xbuf[:, :].bitcast(U32)
    zq_u = xbuf_u[0:ND, 0:1376]
    zq_t = xbuf_u[0:ND, 1376:1440]
    scch = xbuf[0:ND, 1440:2464]
    tchunk = xbuf[0:ND, 2464:2976]

    PS = ctx.enter_context(nc.psum_tensor("PS", [128, 4096], F32))
    pg = [PS[:, 0:1024], PS[:, 1024:2048]]
    pE = [PS[:, 2048:2560], PS[:, 2560:3072]]
    pys = PS[0:NG, 3072:4096]
    pysd = PS[0:ND, 3072:4096]
    pd = [PS[:, i * 1024:(i + 1) * 1024] for i in range(4)]
    pstat_s = PS[0:1, 0:1024]
    pstat_q = PS[0:1, 1024:2048]

    bank_tok = {i: [] for i in range(8)}

    def take_banks(banks):
        out = []
        for b in banks:
            out += bank_tok[b]
            bank_tok[b] = []
        return out

    def give_banks(banks, tok):
        if tok is None:
            return
        for b in banks:
            bank_tok[b].append(tok)

    sT = Sem(nc, "sT")
    sV = Sem(nc, "sV")
    sA = Sem(nc, "sA")
    sC = Sem(nc, "sC", dma=True)
    sSCD = Sem(nc, "sSCD", dma=True)
    sQZ = Sem(nc, "sQZ", dma=True)
    sSCC = Sem(nc, "sSCC", dma=True)
    sGt = Sem(nc, "sGt", dma=True)
    sGm = Sem(nc, "sGm", dma=True)
    sMB = Sem(nc, "sMB", dma=True)
    sRSB = Sem(nc, "sRSB", dma=True)
    sGrep = Sem(nc, "sGrep", dma=True)
    sGtab = Sem(nc, "sGtab", dma=True)
    sWG = [Sem(nc, f"sWG{i}", dma=True) for i in range(2)]
    sWD = [Sem(nc, f"sWD{i}", dma=True) for i in range(2)]
    sH = [Sem(nc, f"sH{i}", dma=True) for i in range(3)]
    sX = [Sem(nc, f"sX{i}", dma=True) for i in range(3)]

    P = Plan()

    def dma(engine, sem, dst, src, waits=()):
        def fn(eng, dst=dst, src=src):
            return eng.dma_start(dst, src)
        P.op(engine, fn, waits=waits, inc=sem)
        return (sem, sem.n)

    xrot = Rot(3, nc, "sXB")
    qrot = Rot(6, nc, "sQB")
    wdqrot = Rot(2, nc, "sQD")
    wrot = Rot(2, nc, "sWB")
    scrot = Rot(2, nc, "sSB")
    tgrot = Rot(2, nc, "sTGb")
    tdrot = Rot(2, nc, "sTDb")
    hin_rot = Rot(4, nc, "sHI")
    wdin_rot = Rot(4, nc, "sWI")
    sqrot = Rot(1)
    tmprot = Rot(2)
    erot = Rot(2)
    esrot = Rot(4)
    hrot = Rot(2)
    gdrot = Rot(4)
    gtdrot = Rot(2)
    wdorot = Rot(4)
    mrs_free = []

    for dst, src_t in [
        (shift_c[:, :], bass.AP(c_shift, 0, [[1, 128], [1, 1]])),
        (pw2_c[:, :], bass.AP(c_pw2, 0, [[1, 128], [1, 1]])),
        (iota32c[:, :], bass.AP(c_iota32, 0, [[1, NG], [1, 1]])),
        (iota86c[:, :], bass.AP(c_iota86, 0, [[1, ND], [1, 1]])),
        (ones_c[:, :], bass.AP(c_ones, 0, [[1, 128], [1, 1]])),
        (iota86r[:, :], bass.AP(c_iota86, 0, [[0, 128], [1, ND]])),
        (iota32r[:, :], bass.AP(c_iota32, 0, [[0, 128], [1, NG]])),
    ]:
        dma("sync", sC, dst, src_t)
    t_const = (sC, sC.n)

    v_scdh = None
    for c0 in range(0, HID, 1024):
        t_scd = dma("sync", sSCD, scch[:, 0:1024],
                    bass.AP(sc_d, c0, [[HID, ND], [1, 1024]]),
                    waits=[v_scdh])
        P.op("vector", lambda e, c0=c0: e.tensor_scalar_mul(
                 scd_h[:, c0:c0 + 1024], scch[:, 0:1024], 0.5),
             waits=[t_scd], inc=sV)
        v_scdh = (sV, sV.n)

    t_gta = dma("scalar", sGtab, gta_i[:, :], gidx_gu_pt[:, :])
    t_gtd2 = dma("scalar", sGtab, gtd_i[:, :], gidx_d_pt[:, :])
    tab_gate = (sGtab, sGtab.n)
    P.op("vector", lambda e: e.tensor_copy(gta_f[:, :], gta_i[:, :]),
         waits=[tab_gate], inc=sV)
    P.op("vector", lambda e: e.tensor_copy(gtd_f[:, :], gtd_i[:, :]),
         waits=[tab_gate], inc=sV)
    v_gtab = (sV, sV.n)
    grep_free = None
    v_ggu_done = None
    for hc in range(HC):
        P.op("vector", lambda e, hc=hc: e.tensor_scalar(
                 g_gu[:, hc * NG:(hc + 1) * NG], iota32r[:, :],
                 gta_f[:, hc:hc + 1], None, ALU.is_equal),
             waits=[t_const, v_gtab], inc=sV)
        tg2 = dma("scalar", sGrep, grep_i[0:NG, 0:128],
                  bass.AP(gidx_gu_p, hc * 128, [[0, NG], [1, 128]]),
                  waits=[grep_free])
        P.op("vector", lambda e: e.tensor_copy(grep_f[0:NG, 0:128], grep_i[0:NG, 0:128]),
             waits=[tg2], inc=sV)
        v_cp_g = (sV, sV.n)
        P.op("vector", lambda e, hc=hc: e.tensor_scalar(
                 gt_gu[:, hc * 128:(hc + 1) * 128], grep_f[0:NG, 0:128], iota32c[:, :],
                 None, ALU.is_equal),
             waits=[t_const, v_cp_g], inc=sV)
        grep_free = (sV, sV.n)
        v_ggu_done = (sV, sV.n)

    def build_T(qz, sc_dram, scratch, scratch_l, ngrp, ncp_packed, out_cols, t_prev):
        t_last = t_prev
        sc_guard = t_prev
        tb_guard = None
        for half in range(2):
            p0 = (ncp_packed // 2) * half
            pw_half = ncp_packed - p0 if half == 1 else ncp_packed // 2
            if pw_half == 0:
                continue
            t_qz = dma("sync", sQZ, zq_u[0:ngrp, 0:pw_half],
                       bass.AP(qz, p0, [[ncp_packed, ngrp], [1, pw_half]]),
                       waits=[t_last])
            col0 = p0 * 8
            colw = pw_half * 8
            for c0 in range(col0, col0 + colw, 1024):
                cw = min(1024, col0 + colw - c0)
                t_sc = dma("sync", sSCC, scch[0:ngrp, 0:cw],
                           bass.AP(sc_dram, c0, [[out_cols, ngrp], [1, cw]]),
                           waits=[sc_guard])
                for s0 in range(0, cw, 512):
                    sw = min(512, cw - s0)
                    pw = sw // 8
                    bp = (c0 + s0) // 8 - p0
                    for k in range(8):
                        P.op("vector", lambda e, k=k, pw=pw, bp=bp, ngrp=ngrp:
                             e.tensor_scalar(zq_t[0:ngrp, 0:pw],
                                             zq_u[0:ngrp, bp:bp + pw],
                                             4 * k, 15,
                                             ALU.logical_shift_right, ALU.bitwise_and),
                             waits=[t_qz], inc=sV)
                        v_zq = (sV, sV.n)
                        P.op("vector", lambda e, k=k, pw=pw, s0=s0, ngrp=ngrp:
                             e.scalar_tensor_tensor(
                                 bass.AP(tchunk.tensor, tchunk.offset + k,
                                         [[3 * TOK, ngrp], [8, pw]]),
                                 zq_t[0:ngrp, 0:pw], 1.0,
                                 bass.AP(scch.tensor, scch.offset + s0 + k,
                                         [[3 * TOK, ngrp], [8, pw]]),
                                 ALU.add, ALU.mult),
                             waits=[t_sc, v_zq], inc=sV)
                    v_stt = (sV, sV.n)
                    P.op("vector", lambda e, ngrp=ngrp, sw=sw: e.tensor_copy(
                             tchunk_b[0:ngrp, 0:sw], tchunk[0:ngrp, 0:sw]),
                         waits=[tb_guard, v_stt], inc=sV)
                    vv = (sV, sV.n)
                    dma("gpsimd", sGt,
                        bass.AP(scratch, c0 + s0, [[out_cols, ngrp], [1, sw]]),
                        tchunk_b[0:ngrp, 0:sw], waits=[vv])
                    P.op("vector", lambda e, ngrp=ngrp, sw=sw: e.tensor_sub(
                             tchunk_l[0:ngrp, 0:sw], tchunk[0:ngrp, 0:sw],
                             tchunk_b[0:ngrp, 0:sw]),
                         waits=[tb_guard, vv], inc=sV)
                    vv2 = (sV, sV.n)
                    tb_guard = dma("gpsimd", sGt,
                                   bass.AP(scratch_l, c0 + s0,
                                           [[out_cols, ngrp], [1, sw]]),
                                   tchunk_l[0:ngrp, 0:sw], waits=[vv2])
                sc_guard = (sV, sV.n)
            t_last = (sV, sV.n)
        return t_last

    v_t1 = build_T(qz_gu, sc_gu, tg_s, tgl_s, NG, GU_COLS // 8, GU_COLS, v_scdh)
    v_t2 = build_T(qz_d, sc_d, td_s, tdl_s, ND, HID // 8, HID, v_t1)
    t_T = (sGt, sGt.n)
    for i in range(xrot.n):
        xrot.tok[i].append(v_t2)

    def x_chunk_ap(src, hc, permuted):
        if permuted:
            return bass.AP(src, hc * 128 * TOK, [[TOK, 8], [8 * TOK, 16], [1, TOK]])
        return bass.AP(src, hc * 128 * TOK, [[TOK, 128], [1, TOK]])

    def ln_stats(src, extra_waits=()):
        tk_s = take_banks([0, 1])
        tk_q = take_banks([2, 3])
        v_mm = None
        for hc in range(HC):
            slot, toks = xrot.acquire()
            w = list(toks) + [t_const] + list(extra_waits)
            t_x = dma("sync", xrot.sems[slot], xbuf[:, slot * TOK:(slot + 1) * TOK],
                      x_chunk_ap(src, hc, False), waits=w)
            qslot, qtoks = sqrot.acquire()
            P.op("scalar", lambda e, slot=slot, qslot=qslot: e.activation(
                     sqb[:, qslot * TOK:(qslot + 1) * TOK],
                     xbuf[:, slot * TOK:(slot + 1) * TOK], AF.Square),
                 waits=list(qtoks) + [t_x], inc=sA)
            v_sq = (sA, sA.n)
            for tk in range(2):
                P.op("tensor", lambda e, slot=slot, tk=tk, hc=hc: e.matmul(
                         pstat_s[:, tk * 512:(tk + 1) * 512], ones_c[:, :],
                         xbuf[:, slot * TOK + tk * 512: slot * TOK + (tk + 1) * 512],
                         start=(hc == 0), stop=(hc == HC - 1)),
                     waits=(tk_s if hc == 0 and tk == 0 else []) + [t_x], inc=None)
                P.op("tensor", lambda e, qslot=qslot, tk=tk, hc=hc: e.matmul(
                         pstat_q[:, tk * 512:(tk + 1) * 512], ones_c[:, :],
                         sqb[:, qslot * TOK + tk * 512: qslot * TOK + (tk + 1) * 512],
                         start=(hc == 0), stop=(hc == HC - 1)),
                     waits=(tk_q if hc == 0 and tk == 0 else []) + [v_sq],
                     inc=(sT if tk == 1 else None))
            v_mm = (sT, sT.n)
            xrot.release(slot, v_mm)
            sqrot.release(qslot, v_mm)
        w_m = list(mrs_free)
        mrs_free.clear()
        P.op("vector", lambda e: e.tensor_scalar_mul(m_sb, pstat_s[:, :], 1.0 / HID),
             waits=[v_mm] + w_m, inc=sV)
        v_m0 = (sV, sV.n)
        give_banks([0, 1], v_m0)
        P.op("vector", lambda e: e.tensor_scalar_mul(rs_sb, pstat_q[:, :], 1.0 / HID),
             inc=sV)
        v_r0 = (sV, sV.n)
        give_banks([2, 3], v_r0)
        P.op("vector", lambda e: e.tensor_mul(mm_sb, m_sb, m_sb),
             waits=[v_m0], inc=sV)
        v_mm2 = (sV, sV.n)
        P.op("vector", lambda e: e.scalar_tensor_tensor(
                 rs_sb, rs_sb, EPS, mm_sb, ALU.add, ALU.subtract),
             waits=[v_r0, v_mm2], inc=sV)
        v_var = (sV, sV.n)
        P.op("scalar", lambda e: e.activation(rs_sb, rs_sb, AF.Ln),
             waits=[v_var], inc=sA)
        v_ln = (sA, sA.n)
        P.op("scalar", lambda e: e.activation(rs_sb, rs_sb, AF.Exp, scale=-0.5),
             waits=[v_ln], inc=sA)
        v_rs = (sA, sA.n)
        t1 = dma("gpsimd", sGm, bass.AP(mrs_s, 0, [[TOK, 1], [1, TOK]]), m_sb,
                 waits=[v_var])
        dma("gpsimd", sGm, bass.AP(mrs_s, TOK, [[TOK, 1], [1, TOK]]), rs_sb,
            waits=[v_rs])
        pair_done = (sGm, sGm.n)
        t_m = dma("sync", sMB, mB[:, :], bass.AP(mrs_s, 0, [[0, 128], [1, TOK]]),
                  waits=[pair_done] + w_m)
        t_rs = dma("sync", sRSB, rsB[:, :], bass.AP(mrs_s, TOK, [[0, 128], [1, TOK]]),
                   waits=[pair_done])
        return t_m, t_rs

    def y_build(src, t_m, t_rs):
        last = None
        for hc in range(HC):
            slot, toks = xrot.acquire()
            t_x = dma("sync", xrot.sems[slot], xbuf[:, slot * TOK:(slot + 1) * TOK],
                      x_chunk_ap(src, hc, True), waits=toks)
            tslot, ttoks = tmprot.acquire()
            P.op("vector", lambda e, slot=slot, tslot=tslot: e.tensor_sub(
                     tmpb[:, tslot * TOK:(tslot + 1) * TOK],
                     xbuf[:, slot * TOK:(slot + 1) * TOK], mB[:, :]),
                 waits=list(ttoks) + [t_x, t_m], inc=sV)
            xrot.release(slot, (sV, sV.n))
            P.op("vector", lambda e, hc=hc, tslot=tslot: e.tensor_mul(
                     y_sb[:, hc * TOK:(hc + 1) * TOK],
                     tmpb[:, tslot * TOK:(tslot + 1) * TOK], rsB[:, :]),
                 waits=[t_rs], inc=sV)
            tmprot.release(tslot, (sV, sV.n))
            last = (sV, sV.n)
        mrs_free.append(last)
        return last

    def attn_apply(t_m, t_rs):
        for hc in range(HC):
            slot, toks = xrot.acquire()
            t_x = dma("sync", xrot.sems[slot], xbuf[:, slot * TOK:(slot + 1) * TOK],
                      x_chunk_ap(x_s, hc, False), waits=toks)
            tslot, ttoks = tmprot.acquire()
            xs = xbuf[:, slot * TOK:(slot + 1) * TOK]
            ts = tmpb[:, tslot * TOK:(tslot + 1) * TOK]
            P.op("vector", lambda e, xs=xs, ts=ts: e.tensor_sub(ts, xs, mB[:, :]),
                 waits=list(ttoks) + [t_x, t_m], inc=None)
            P.op("vector", lambda e, xs=xs, ts=ts: e.tensor_mul(ts, ts, rsB[:, :]),
                 waits=[t_rs], inc=None)
            P.op("vector", lambda e, xs=xs, ts=ts: e.tensor_add(xs, xs, ts), inc=sV)
            tmprot.release(tslot, (sV, sV.n))
            v_x2 = (sV, sV.n)
            t_st = dma("gpsimd", sX[slot], x_chunk_ap(x_s, hc, False), xs,
                       waits=[v_x2])
            xrot.release(slot, t_st)
        mrs_free.append((sV, sV.n))
        return [(s, s.n) for s in sX]

    def gu_phase(dequant, t_y_done, t_w_gate):
        pg_banks = [[0, 1], [2, 3]]
        stash_ready = [None, None]
        silu_free = [t_T, t_T]
        cp_counter = [0]
        grep_gate = [grep_free]

        tk67 = take_banks([6, 7])
        for hc in range(HC):
            for tk in range(2):
                P.op("tensor", lambda e, hc=hc, tk=tk: e.matmul(
                         pys[:, tk * 512:(tk + 1) * 512],
                         g_gu[:, hc * NG:(hc + 1) * NG],
                         y_sb[:, hc * TOK + tk * 512: hc * TOK + (tk + 1) * 512],
                         start=(hc == 0), stop=(hc == HC - 1)),
                     waits=(tk67 if hc == 0 and tk == 0 else []) + [t_y_done, v_ggu_done],
                     inc=(sT if hc == HC - 1 and tk == 1 else None))
        v_ys = (sT, sT.n)
        P.op("scalar", lambda e: e.activation(ys_sb[:, :], pys[:, :], AF.Identity,
                                              scale=-1.0),
             waits=[v_ys], inc=sA)
        v_ysb = (sA, sA.n)
        P.op("vector", lambda e: e.scalar_tensor_tensor(
                 ysl_sb[:, :], pys[:, :], -1.0, ys_sb[:, :],
                 ALU.mult, ALU.subtract),
             waits=[v_ysb], inc=sV)
        v_ysb_l = (sV, sV.n)
        ysd_gate = v_ysb_l

        def wd_begin(ic):
            gslot, gtoks = gtdrot.acquire()
            tgi = dma("scalar", sGrep, grep_i[0:ND, 0:128],
                      bass.AP(gidx_d_p, ic * 128, [[0, ND], [1, 128]]),
                      waits=gtoks + [grep_gate[0]])
            P.op("vector", lambda e: e.tensor_copy(grep_f[0:ND, 0:128],
                                                   grep_i[0:ND, 0:128]),
                 waits=[tgi], inc=sV)
            v_cp_d = (sV, sV.n)
            P.op("vector", lambda e, gslot=gslot: e.tensor_scalar(
                     gtd_b[:, gslot * 128:(gslot + 1) * 128], grep_f[0:ND, 0:128],
                     iota86c[:, :], None, ALU.is_equal),
                 waits=[t_const, v_cp_d], inc=sV)
            v_gtd = (sV, sV.n)
            grep_gate[0] = v_gtd
            return {"gslot": gslot, "v_gtd": v_gtd, "ic": ic, "last_e": None}

        def wd_unit(st, jt):
            gslot = st["gslot"]
            v_gtd = st["v_gtd"]
            ic = st["ic"]
            if True:
                if jt % 2 == 0:
                    qslot, qtoks = wdqrot.acquire()
                    qslot += 6
                    t_q = dma("sync" if qslot % 2 == 0 else "gpsimd",
                              wdqrot.sems[qslot - 6],
                              qbuf[:, qslot * 2 * GW:(qslot + 1) * 2 * GW],
                              bass.AP(qw_d, ic * (128 * 16 * GW) + jt * GW,
                                      [[16 * GW, 128], [1, 2 * GW]]),
                              waits=qtoks)
                    P.op("vector", lambda e, qslot=qslot: e.tensor_tensor(
                             qbuf[:, qslot * 2 * GW:(qslot + 1) * 2 * GW],
                             qbuf[:, qslot * 2 * GW:(qslot + 1) * 2 * GW],
                             shift_c[:, 0:1].broadcast_to([128, 2 * GW]),
                             ALU.bitwise_and),
                         waits=[t_q, t_const], inc=sV)
                    st["q4"] = qslot
                    st["v_and"] = (sV, sV.n)
                qslot = st["q4"]
                eslot, etoks = erot.acquire()
                P.op("tensor", lambda e, eslot=eslot, gslot=gslot, jt=jt: e.matmul(
                         pE[eslot][0:128, 0:GW],
                         gtd_b[:, gslot * 128:(gslot + 1) * 128],
                         scd_h[:, jt * GW:(jt + 1) * GW],
                         start=True, stop=True),
                     waits=etoks + [v_gtd, v_scdh], inc=sT)
                v_e = (sT, sT.n)
                st["last_e"] = v_e
                esslot, estoks = esrot.acquire()
                P.op("scalar", lambda e, eslot=eslot, esslot=esslot: e.activation(
                         esb[:, esslot * GW:(esslot + 1) * GW], pE[eslot][0:128, 0:GW],
                         AF.Identity, scale=pw2_c[:, 0:1]),
                     waits=estoks + [v_e, t_const], inc=sA)
                v_es = (sA, sA.n)
                erot.release(eslot, v_es)
                oslot, otoks = wdorot.acquire()
                P.op("vector", lambda e, qslot=qslot, jt=jt, esslot=esslot,
                              oslot=oslot:
                     e.tensor_tensor(
                         wdtile[:, oslot * GW:(oslot + 1) * GW],
                         qbuf[:, qslot * 2 * GW + (jt % 2) * GW:
                              qslot * 2 * GW + (jt % 2 + 1) * GW],
                         esb[:, esslot * GW:(esslot + 1) * GW], ALU.mult),
                     waits=otoks + [v_es, st["v_and"]], inc=sV)
                v_w = (sV, sV.n)
                esrot.release(esslot, v_w)
                if jt % 2 == 1:
                    wdqrot.release(qslot - 6, v_w)
                if jt % 2 == 0:
                    st["pend_o"] = oslot
                else:
                    o0 = st.pop("pend_o")
                    assert o0 % 2 == 0 and oslot == o0 + 1
                    t_wb = dma("gpsimd", sWD[(jt // 2) % 2],
                               bass.AP(wd_s, ic * (128 * HID) + (jt - 1) * GW,
                                       [[HID, 128], [1, 2 * GW]]),
                               wdtile[:, o0 * GW:(o0 + 2) * GW], waits=[v_w])
                    wdorot.release(o0, t_wb)
                    wdorot.release(oslot, t_wb)

        def wd_end(st):
            gtdrot.release(st["gslot"], st["last_e"])

        grps = []
        for j in range(INTER // GW):
            grps.append(("g", j * GW))
            grps.append(("u", INTER + j * GW))

        v_ysd_stop = None
        n_up = 0
        npairs = INTER // GW

        def dq_begin(wslot, wtoks, c0):
            sslot, stoks = scrot.acquire()
            t_sc = dma("gpsimd", scrot.sems[sslot],
                       scb[:, sslot * GW: (sslot + 1) * GW],
                       bass.AP(sc_gu, c0, [[GU_COLS, NG], [1, GW]]), waits=stoks)
            return {"wslot": wslot, "wtoks": wtoks, "c0": c0, "sslot": sslot,
                    "t_sc": t_sc, "wready": [], "last_e": None,
                    "next_load": 0, "batches": {}}

        def dq_load(st):
            b = st["next_load"]
            if b >= HC // 2:
                return
            st["next_load"] = b + 1
            qslot, qtoks = qrot.acquire()
            t_q = dma("sync" if qslot % 2 == 0 else "gpsimd",
                      qrot.sems[qslot],
                      qbuf[:, qslot * 2 * GW:(qslot + 1) * 2 * GW],
                      bass.AP(qw_gu, (st["c0"] // GW) * (128 * HC * GW)
                              + (2 * b) * GW,
                              [[HC * GW, 128], [1, 2 * GW]]),
                      waits=qtoks + (st["wtoks"] if b == 0 else []))
            st["batches"][b] = (qslot, t_q)

        def dq_unit(st, hc):
            wslot = st["wslot"]
            c0 = st["c0"]
            sslot = st["sslot"]
            t_sc = st["t_sc"]
            if hc % 2 == 0:
                qslot, t_q = st["batches"].pop(hc // 2)
                P.op("vector", lambda e, qslot=qslot: e.tensor_tensor(
                         qbuf[:, qslot * 2 * GW:(qslot + 1) * 2 * GW],
                         qbuf[:, qslot * 2 * GW:(qslot + 1) * 2 * GW],
                         shift_c[:, 0:1].broadcast_to([128, 2 * GW]),
                         ALU.bitwise_and),
                     waits=[t_q, t_const], inc=sV)
                st["q4"] = qslot
                st["v_and"] = (sV, sV.n)
            qslot = st["q4"]
            eslot, etoks = erot.acquire()
            P.op("tensor", lambda e, eslot=eslot, hc=hc, sslot=sslot:
                 e.matmul(pE[eslot][0:128, 0:GW],
                          gt_gu[:, hc * 128:(hc + 1) * 128],
                          scb[:, sslot * GW: (sslot + 1) * GW],
                          start=True, stop=True),
                 waits=etoks + [t_sc, v_ggu_done], inc=sT)
            v_e = (sT, sT.n)
            st["last_e"] = v_e
            esslot, estoks = esrot.acquire()
            P.op("scalar", lambda e, eslot=eslot, esslot=esslot: e.activation(
                     esb[:, esslot * GW:(esslot + 1) * GW],
                     pE[eslot][0:128, 0:GW],
                     AF.Identity, scale=pw2_c[:, 0:1]),
                 waits=estoks + [v_e, t_const], inc=sA)
            v_es = (sA, sA.n)
            erot.release(eslot, v_es)
            P.op("vector", lambda e, qslot=qslot, esslot=esslot, wslot=wslot,
                          hc=hc:
                 e.tensor_tensor(
                     wbuf[:, wslot * (HC * GW) + hc * GW:
                          wslot * (HC * GW) + (hc + 1) * GW],
                     qbuf[:, qslot * 2 * GW + (hc % 2) * GW:
                          qslot * 2 * GW + (hc % 2 + 1) * GW],
                     esb[:, esslot * GW:(esslot + 1) * GW], ALU.mult),
                 waits=[v_es, st["v_and"]], inc=sV)
            v_w = (sV, sV.n)
            esrot.release(esslot, v_w)
            if hc % 2 == 1:
                qrot.release(qslot, v_w)
            st["wready"].append(v_w)

        def dq_end(st):
            scrot.release(st["sslot"], st["last_e"])
            wslot = st["wslot"]
            for hb in (0, HC // 2):
                dma("gpsimd", sWG[wslot],
                    bass.AP(wg_s, (st["c0"] // GW) * (128 * HC * GW) + hb * GW,
                            [[HC * GW, 128], [1, (HC // 2) * GW]]),
                    wbuf[:, wslot * (HC * GW) + hb * GW:
                         wslot * (HC * GW) + (hb + HC // 2) * GW],
                    waits=[st["wready"][-1]])
            return st["wready"], (sWG[st["wslot"]], sWG[st["wslot"]].n)

        wd_queue = []
        wd_active = []

        def drain_wd(budget):
            while budget > 0:
                if not wd_active:
                    if not wd_queue:
                        return
                    wd_active.append([wd_begin(wd_queue.pop(0)), 0])
                st, jt = wd_active[0]
                for _ in range(4):
                    wd_unit(st, jt)
                    jt += 1
                budget -= 4
                if jt == HID // GW:
                    wd_end(st)
                    wd_active.pop(0)
                else:
                    wd_active[0][1] = jt

        pend = {}
        pend_state = None
        pend_hc = 0
        for gi, (side, c0) in enumerate(grps):
            ncp = GW // 128
            if dequant:
                if gi == 0:
                    ws0, wt0 = wrot.acquire()
                    st0 = dq_begin(ws0, wt0, grps[0][1])
                    for _ in range(4):
                        dq_load(st0)
                    for hc0 in range(HC):
                        dq_unit(st0, hc0)
                        if hc0 % 2 == 1:
                            dq_load(st0)
                    pend[0] = (ws0,) + dq_end(st0)
                if gi + 1 < len(grps):
                    wsN, wtN = wrot.acquire()
                    pend_state = dq_begin(wsN, wtN, grps[gi + 1][1])
                    pend_hc = 0
                    for _ in range(4):
                        dq_load(pend_state)
                else:
                    pend_state = None
                wslot, wready, wb_use_done = pend.pop(gi)
            else:
                wslot, wtoks = wrot.acquire()
                wb_use_done = None
                for hb in range(0, HC, 8):
                    dma("sync", wrot.sems[wslot],
                        wbuf[:, wslot * (HC * GW) + hb * GW:
                             wslot * (HC * GW) + (hb + 8) * GW],
                        bass.AP(wg_s, (c0 // GW) * (128 * HC * GW) + hb * GW,
                                [[HC * GW, 128], [1, 8 * GW]]),
                        waits=(wtoks + list(t_w_gate) if hb == 0 else []))
                wready = [(wrot.sems[wslot], wrot.sems[wslot].n)]
            tslot, ttoks = tgrot.acquire()
            dma("scalar", tgrot.sems[tslot],
                tgb[:, tslot * 2 * GW: tslot * 2 * GW + GW],
                bass.AP(tg_s, c0, [[GU_COLS, NG], [1, GW]]),
                waits=ttoks + [t_T])
            t_tg = dma("scalar", tgrot.sems[tslot],
                       tgb[:, tslot * 2 * GW + GW: (tslot + 1) * 2 * GW],
                       bass.AP(tgl_s, c0, [[GU_COLS, NG], [1, GW]]), waits=[])
            grp_last = None
            for cp in range(ncp):
                pgi = cp_counter[0] % 2
                cp_counter[0] += 1
                tk_banks = take_banks(pg_banks[pgi])
                for hc in range(HC):
                    for tk in range(2):
                        w = []
                        if hc == 0 and tk == 0:
                            w += tk_banks
                            w.append(t_y_done)
                        w.append(wready[hc] if dequant else wready[0])
                        P.op("tensor", lambda e, pgi=pgi, tk=tk, hc=hc, wslot=wslot,
                                      cp=cp:
                             e.matmul(pg[pgi][:, tk * 512:(tk + 1) * 512],
                                      wbuf[:, wslot * (HC * GW) + hc * GW + cp * 128:
                                           wslot * (HC * GW) + hc * GW + (cp + 1) * 128],
                                      y_sb[:, hc * TOK + tk * 512:
                                           hc * TOK + (tk + 1) * 512],
                                      start=(hc == 0), stop=False),
                             waits=w, inc=None)
                    if dequant:
                        step = (cp * HC + hc + 1) // ncp
                        while pend_state is not None and pend_hc < step:
                            dq_unit(pend_state, pend_hc)
                            if pend_hc % 2 == 1:
                                dq_load(pend_state)
                            pend_hc += 1
                        if hc % 8 == 7:
                            drain_wd(4)
                for tk in range(2):
                    P.op("tensor", lambda e, pgi=pgi, tk=tk, tslot=tslot, cp=cp:
                         e.matmul(pg[pgi][:, tk * 512:(tk + 1) * 512],
                                  tgb[:, tslot * 2 * GW + cp * 128:
                                       tslot * 2 * GW + (cp + 1) * 128],
                                  ys_sb[:, tk * 512:(tk + 1) * 512],
                                  start=False, stop=False),
                         waits=[t_tg, v_ysb], inc=None)
                    P.op("tensor", lambda e, pgi=pgi, tk=tk, tslot=tslot, cp=cp:
                         e.matmul(pg[pgi][:, tk * 512:(tk + 1) * 512],
                                  tgb[:, tslot * 2 * GW + cp * 128:
                                       tslot * 2 * GW + (cp + 1) * 128],
                                  ysl_sb[:, tk * 512:(tk + 1) * 512],
                                  start=False, stop=False),
                         waits=[v_ysb_l], inc=None)
                    P.op("tensor", lambda e, pgi=pgi, tk=tk, tslot=tslot, cp=cp:
                         e.matmul(pg[pgi][:, tk * 512:(tk + 1) * 512],
                                  tgb[:, tslot * 2 * GW + GW + cp * 128:
                                       tslot * 2 * GW + GW + (cp + 1) * 128],
                                  ys_sb[:, tk * 512:(tk + 1) * 512],
                                  start=False, stop=True),
                         waits=[], inc=sT)
                v_cp = (sT, sT.n)
                grp_last = v_cp
                if side == "g":
                    P.op("scalar", lambda e, pgi=pgi, cp=cp: e.copy(
                             stash[:, cp * TOK:(cp + 1) * TOK], pg[pgi][:, :]),
                         waits=[v_cp], inc=sA)
                    give_banks(pg_banks[pgi], (sA, sA.n))
                    stash_ready[cp] = (sA, sA.n)
                else:
                    ic = (c0 - INTER) // 128 + cp
                    sl = n_up % 2
                    n_up += 1
                    P.op("scalar", lambda e, cp=cp, sl=sl: e.activation(
                             silu_b[:, sl * TOK:(sl + 1) * TOK],
                             stash[:, cp * TOK:(cp + 1) * TOK], AF.Silu),
                         waits=[stash_ready[cp], silu_free[sl]], inc=sA)
                    v_silu = (sA, sA.n)
                    hslot, htoks = hrot.acquire()
                    P.op("vector", lambda e, pgi=pgi, hslot=hslot, sl=sl:
                         e.tensor_tensor(
                             hbuf[:, hslot * TOK:(hslot + 1) * TOK],
                             silu_b[:, sl * TOK:(sl + 1) * TOK],
                             pg[pgi][:, :], ALU.mult),
                         waits=htoks + [v_silu, v_cp], inc=sV)
                    v_h = (sV, sV.n)
                    silu_free[sl] = v_h
                    give_banks(pg_banks[pgi], v_h)
                    gdslot, gdtoks = gdrot.acquire()
                    P.op("vector", lambda e, gdslot=gdslot, ic=ic: e.tensor_scalar(
                             gd_b[:, gdslot * ND:(gdslot + 1) * ND], iota86r[:, :],
                             gtd_f[:, ic:ic + 1], None, ALU.is_equal),
                         waits=gdtoks + [t_const, v_gtab], inc=sV)
                    v_gd = (sV, sV.n)
                    for tk in range(2):
                        P.op("tensor", lambda e, gdslot=gdslot, hslot=hslot, tk=tk,
                                      ic=ic:
                             e.matmul(pysd[:, tk * 512:(tk + 1) * 512],
                                      gd_b[:, gdslot * ND:(gdslot + 1) * ND],
                                      hbuf[:, hslot * TOK + tk * 512:
                                           hslot * TOK + (tk + 1) * 512],
                                      start=(ic == 0), stop=(ic == ICN - 1)),
                             waits=[v_gd, v_h] + ([ysd_gate] if ic == 0 and tk == 0
                                                  else []),
                             inc=(sT if tk == 1 else None))
                    v_ysd_mm = (sT, sT.n)
                    gdrot.release(gdslot, v_ysd_mm)
                    hrot.release(hslot, v_ysd_mm)
                    inter0 = c0 - INTER + cp * 128
                    t_h = dma("gpsimd", sH[hslot],
                              bass.AP(h_s, inter0 * TOK,
                                      [[TOK, 16], [16 * TOK, 8], [1, TOK]]),
                              hbuf[:, hslot * TOK:(hslot + 1) * TOK], waits=[v_h])
                    hrot.release(hslot, t_h)
                    if ic == ICN - 1:
                        v_ysd_stop = v_ysd_mm
            tgrot.release(tslot, grp_last)
            wrot.release(wslot, grp_last)
            wrot.release(wslot, wb_use_done)
            if dequant and pend_state is not None:
                while pend_hc < HC:
                    dq_unit(pend_state, pend_hc)
                    if pend_hc % 2 == 1:
                        dq_load(pend_state)
                    pend_hc += 1
                pend[gi + 1] = (pend_state["wslot"],) + dq_end(pend_state)
                pend_state = None
            if dequant and side == "u":
                k = gi // 2
                lo = (k * ICN) // npairs
                hi = ((k + 1) * ICN) // npairs
                wd_queue.extend(range(lo, hi))
        if dequant:
            drain_wd(10 ** 9)
        P.op("scalar", lambda e: e.activation(ysd_sb[:, :], pysd[:, :], AF.Identity,
                                              scale=-0.5),
             waits=[v_ysd_stop], inc=sA)
        v_ysd_h = (sA, sA.n)
        P.op("vector", lambda e: e.scalar_tensor_tensor(
                 ysdl_sb[:, :], pysd[:, :], -0.5, ysd_sb[:, :],
                 ALU.mult, ALU.subtract),
             waits=[v_ysd_h], inc=sV)
        v_ysd_sb = (sV, sV.n)
        give_banks([6, 7], v_ysd_sb)
        for i, toks in enumerate(erot.tok):
            for t in toks:
                bank_tok[4 + i].append(t)
            erot.tok[i] = []
        erot.i = 0
        h_gate = [(s, s.n) for s in sH]
        w_gate = [(s, s.n) for s in sWG] + [(s, s.n) for s in sWD]
        return v_ysd_sb, h_gate, w_gate

    def down_phase(v_ysd_sb, h_gate, w_gate, src, dst):
        pd_banks = [[0, 1], [2, 3], [4, 5], [6, 7]]
        for hg in range(8):
            tslot, ttoks = tdrot.acquire()
            dma("scalar", tdrot.sems[tslot], tdb[:, tslot * 1024: tslot * 1024 + 512],
                bass.AP(td_s, hg * 512, [[HID, ND], [1, 512]]),
                waits=ttoks + [t_T])
            t_tdsl = dma("scalar", tdrot.sems[tslot],
                         tdb[:, tslot * 1024 + 512: (tslot + 1) * 1024],
                         bass.AP(tdl_s, hg * 512, [[HID, ND], [1, 512]]), waits=[])
            for ic in range(ICN):
                hslot, htoks = hin_rot.acquire()
                t_hin = dma("sync", hin_rot.sems[hslot],
                            hinb[:, hslot * TOK:(hslot + 1) * TOK],
                            bass.AP(h_s, ic * 128 * TOK,
                                    [[TOK, 128], [1, TOK]]),
                            waits=htoks + (h_gate if hg == 0 and ic == 0 else []))
                wslot, wtoks = wdin_rot.acquire()
                t_wd = dma("sync", wdin_rot.sems[wslot],
                           wdb[:, wslot * 512:(wslot + 1) * 512],
                           bass.AP(wd_s, (ic * 128) * HID + hg * 512,
                                   [[HID, 128], [1, 512]]),
                           waits=wtoks + (w_gate if hg == 0 and ic == 0 else []))
                for c in range(4):
                    for tk in range(2):
                        w = [t_hin, t_wd]
                        if ic == 0 and tk == 0:
                            w += take_banks(pd_banks[c])
                        P.op("tensor", lambda e, c=c, tk=tk, wslot=wslot, hslot=hslot,
                                      ic=ic:
                             e.matmul(pd[c][:, tk * 512:(tk + 1) * 512],
                                      wdb[:, wslot * 512 + c * 128:
                                           wslot * 512 + (c + 1) * 128],
                                      hinb[:, hslot * TOK + tk * 512:
                                           hslot * TOK + (tk + 1) * 512],
                                      start=(ic == 0), stop=False),
                             waits=w, inc=(sT if c == 3 and tk == 1 else None))
                v_ic = (sT, sT.n)
                hin_rot.release(hslot, v_ic)
                wdin_rot.release(wslot, v_ic)
            v_c = None
            for c in range(4):
                for tk in range(2):
                    P.op("tensor", lambda e, c=c, tk=tk, tslot=tslot: e.matmul(
                             pd[c][:, tk * 512:(tk + 1) * 512],
                             tdb[:, tslot * 1024 + c * 128:
                                  tslot * 1024 + (c + 1) * 128],
                             ysd_sb[:, tk * 512:(tk + 1) * 512],
                             start=False, stop=False),
                         waits=[t_tdsl, v_ysd_sb], inc=None)
                    P.op("tensor", lambda e, c=c, tk=tk, tslot=tslot: e.matmul(
                             pd[c][:, tk * 512:(tk + 1) * 512],
                             tdb[:, tslot * 1024 + c * 128:
                                  tslot * 1024 + (c + 1) * 128],
                             ysdl_sb[:, tk * 512:(tk + 1) * 512],
                             start=False, stop=False),
                         waits=[], inc=None)
                    P.op("tensor", lambda e, c=c, tk=tk, tslot=tslot: e.matmul(
                             pd[c][:, tk * 512:(tk + 1) * 512],
                             tdb[:, tslot * 1024 + 512 + c * 128:
                                  tslot * 1024 + 512 + (c + 1) * 128],
                             ysd_sb[:, tk * 512:(tk + 1) * 512],
                             start=False, stop=True),
                         waits=[], inc=(sT if tk == 1 else None))
                v_c = (sT, sT.n)
                hc = hg * 4 + c
                slot, toks = xrot.acquire()
                t_x = dma("sync", xrot.sems[slot], xbuf[:, slot * TOK:(slot + 1) * TOK],
                          x_chunk_ap(src, hc, False), waits=toks)
                P.op("vector", lambda e, c=c, slot=slot: e.tensor_tensor(
                         xbuf[:, slot * TOK:(slot + 1) * TOK],
                         pd[c][:, :], xbuf[:, slot * TOK:(slot + 1) * TOK], ALU.add),
                     waits=[v_c, t_x], inc=sV)
                v_add = (sV, sV.n)
                give_banks(pd_banks[c], v_add)
                t_st = dma("gpsimd", sX[slot], x_chunk_ap(dst, hc, False),
                           xbuf[:, slot * TOK:(slot + 1) * TOK], waits=[v_add])
                xrot.release(slot, t_st)
            tdrot.release(tslot, v_c)
        return [(s, s.n) for s in sX]

    t_m1, t_rs1 = ln_stats(xT)
    t_y1 = y_build(xT, t_m1, t_rs1)
    v_ysd1, h_gate1, w_gate1 = gu_phase(True, t_y1, None)
    x_gate1 = down_phase(v_ysd1, h_gate1, w_gate1, xT, x_s)

    t_m2, t_rs2 = ln_stats(x_s, extra_waits=x_gate1)
    x_gate2 = attn_apply(t_m2, t_rs2)

    t_m3, t_rs3 = ln_stats(x_s, extra_waits=x_gate2)
    t_y3 = y_build(x_s, t_m3, t_rs3)
    v_ysd2, h_gate2, _ = gu_phase(False, t_y3, w_gate1)
    down_phase(v_ysd2, h_gate2, w_gate1, x_s, xoutT)

    final_waits = ([(sGt, sGt.n), (sGm, sGm.n)] +
                   [(s, s.n) for s in sX] + [(s, s.n) for s in sH] +
                   [(s, s.n) for s in sWG] + [(s, s.n) for s in sWD])
    P.op("gpsimd", lambda e: e.nop(), waits=final_waits, inc=None)

    with nc.Block() as block:
        @block.sync
        def _(eng):
            P.emit("sync", eng)

        @block.scalar
        def _(eng):
            P.emit("scalar", eng)

        @block.vector
        def _(eng):
            P.emit("vector", eng)

        @block.tensor
        def _(eng):
            P.emit("tensor", eng)

        @block.gpsimd
        def _(eng):
            P.emit("gpsimd", eng)

    return nc


def kernel(L0_attn_x, L0_ffn_qweight_gu, L0_ffn_qzeros_gu, L0_ffn_scales_gu,
           L0_ffn_g_idx_gu, L0_ffn_qweight_down, L0_ffn_qzeros_down,
           L0_ffn_scales_down, L0_ffn_g_idx_down):
    x = np.asarray(L0_attn_x, dtype=np.float32).reshape(NCORES * TOK, HID)
    _v = np.asarray(L0_ffn_qweight_gu, dtype=np.uint32).reshape(32, 16, 86, 256)
    _v8 = np.repeat(_v.view(np.uint8).reshape(32, 16, 86, 256, 4), 2, axis=4)
    qw_gu_a = np.ascontiguousarray(_v8.transpose(2, 4, 1, 0, 3)).reshape(-1)
    qz_gu_a = np.ascontiguousarray(np.asarray(L0_ffn_qzeros_gu, dtype=np.uint32))
    sc_gu_a = np.ascontiguousarray(np.asarray(L0_ffn_scales_gu, dtype=np.float32))
    _w = np.asarray(L0_ffn_qweight_down, dtype=np.uint32).reshape(86, 16, 16, 256)
    _w8 = np.repeat(_w.view(np.uint8).reshape(86, 16, 16, 256, 4), 2, axis=4)
    qw_d_a = np.ascontiguousarray(_w8.transpose(0, 4, 1, 2, 3)).reshape(-1)
    qz_d_a = np.ascontiguousarray(np.asarray(L0_ffn_qzeros_down, dtype=np.uint32))
    sc_d_a = np.ascontiguousarray(np.asarray(L0_ffn_scales_down, dtype=np.float32))
    gidx_gu = np.asarray(L0_ffn_g_idx_gu, dtype=np.int32)
    gidx_d = np.asarray(L0_ffn_g_idx_down, dtype=np.int32)

    gidx_gu_p = np.ascontiguousarray(gidx_gu.reshape(HC, 128)[:, PERM].reshape(-1))
    gidx_d_p = np.ascontiguousarray(gidx_d.reshape(ICN, 128)[:, PERM].reshape(-1))
    gidx_gu_pt = np.ascontiguousarray(gidx_gu_p.reshape(HC, 128).T)
    gidx_d_pt = np.ascontiguousarray(gidx_d_p.reshape(ICN, 128).T)

    _nib = (np.arange(128) // 16) % 2
    c_shift = np.where(_nib == 0, 0x0F, 0xF0).astype(np.uint8)
    c_pw2 = np.ldexp(np.float32(1.0), -4 * _nib.astype(np.int32)).astype(np.float32)
    c_iota32 = np.arange(NG, dtype=np.float32)
    c_iota86 = np.arange(ND, dtype=np.float32)
    c_ones = np.ones(128, dtype=np.float32)

    nc = build_program()
    in_maps = []
    for k in range(NCORES):
        xk = np.ascontiguousarray(x[k * TOK:(k + 1) * TOK, :].T)
        in_maps.append({
            "xT": xk, "qw_gu": qw_gu_a, "qz_gu": qz_gu_a, "sc_gu": sc_gu_a,
            "qw_d": qw_d_a, "qz_d": qz_d_a, "sc_d": sc_d_a,
            "gidx_gu_p": gidx_gu_p, "gidx_d_p": gidx_d_p,
            "gidx_gu_pt": gidx_gu_pt, "gidx_d_pt": gidx_d_pt,
            "c_shift": c_shift, "c_pw2": c_pw2,
            "c_iota32": c_iota32, "c_iota86": c_iota86,
            "c_ones": c_ones,
        })
    res = run_bass_kernel_spmd(nc, in_maps, core_ids=list(range(NCORES)))
    out = np.empty((NCORES * TOK, HID), dtype=np.float32)
    for k in range(NCORES):
        out[k * TOK:(k + 1) * TOK, :] = np.asarray(res.results[k]["xoutT"]).T
    return out.reshape(2, HID, HID)


# version 15 (final)
# speedup vs baseline: 1.2893x; 1.0976x over previous
import numpy as np
import concourse.bass as bass
import concourse.mybir as mybir
from concourse.bass_utils import run_bass_kernel_spmd

F32 = mybir.dt.float32
BF16 = mybir.dt.bfloat16
U32 = mybir.dt.uint32
U8 = mybir.dt.uint8
I32 = mybir.dt.int32
AF = mybir.ActivationFunctionType
ALU = mybir.AluOpType

HID = 4096
INTER = 11008
TOK = 1024
NCORES = 8
HC = HID // 128
ICN = INTER // 128
NG = 32
ND = 86
GU_COLS = 2 * INTER
EPS = 1e-5
GW = 256

PERM = (8 * (np.arange(128) % 16) + np.arange(128) // 16).astype(np.int64)


class Sem:
    def __init__(self, nc, name, dma=False):
        self.name = name
        self.h = nc.semaphore(name).__enter__()
        self.step = 16 if dma else 1
        self.n = 0

    def inc(self):
        self.n += self.step
        return self.n


class Plan:
    ENGINES = ("sync", "scalar", "vector", "tensor", "gpsimd")

    def __init__(self):
        self.ops = {e: [] for e in self.ENGINES}

    def op(self, engine, fn, waits=(), inc=None):
        ws = []
        for wv in waits:
            if wv is None:
                continue
            s, v = wv
            assert v <= s.n, f"wait on {s.name} for {v} but only {s.n} planned"
            ws.append((s, v))
        self.ops[engine].append((ws, fn, inc))
        if inc is not None:
            inc.inc()

    def emit(self, engine_name, eng):
        waited = {}
        for ws, fn, inc in self.ops[engine_name]:
            for (s, v) in ws:
                if waited.get(s.name, 0) < v:
                    eng.wait_ge(s.h, v)
                    waited[s.name] = v
            inst = fn(eng)
            if inc is not None:
                inst.then_inc(inc.h, inc.step)


class Rot:

    def __init__(self, nslots, nc=None, prefix=None):
        self.n = nslots
        self.tok = [[] for _ in range(nslots)]
        self.i = 0
        self.sems = ([Sem(nc, f"{prefix}{i}", dma=True) for i in range(nslots)]
                     if nc is not None else None)

    def acquire(self):
        s = self.i % self.n
        self.i += 1
        toks = self.tok[s]
        self.tok[s] = []
        return s, toks

    def release(self, slot, tok):
        if tok is not None:
            self.tok[slot].append(tok)


def build_program():
    nc = bass.Bass("TRN2", target_bir_lowering=False,
                   detect_race_conditions=False)

    xT = nc.dram_tensor("xT", [HID, TOK], F32, kind="ExternalInput")
    qw_gu = nc.dram_tensor("qw_gu", [(GU_COLS // GW) * 128 * HC * GW], U8,
                           kind="ExternalInput")
    qz_gu = nc.dram_tensor("qz_gu", [NG, GU_COLS // 8], U32, kind="ExternalInput")
    sc_gu = nc.dram_tensor("sc_gu", [NG, GU_COLS], F32, kind="ExternalInput")
    qw_d = nc.dram_tensor("qw_d", [ICN * 128 * 16 * GW], U8,
                          kind="ExternalInput")
    qz_d = nc.dram_tensor("qz_d", [ND, HID // 8], U32, kind="ExternalInput")
    sc_d = nc.dram_tensor("sc_d", [ND, HID], F32, kind="ExternalInput")
    gidx_gu_p = nc.dram_tensor("gidx_gu_p", [HID], I32, kind="ExternalInput")
    gidx_d_p = nc.dram_tensor("gidx_d_p", [INTER], I32, kind="ExternalInput")
    gidx_gu_pt = nc.dram_tensor("gidx_gu_pt", [128, HC], I32, kind="ExternalInput")
    gidx_d_pt = nc.dram_tensor("gidx_d_pt", [128, ICN], I32, kind="ExternalInput")
    c_shift = nc.dram_tensor("c_shift", [128], U8, kind="ExternalInput")
    c_pw2 = nc.dram_tensor("c_pw2", [128], F32, kind="ExternalInput")
    c_iota32 = nc.dram_tensor("c_iota32", [NG], F32, kind="ExternalInput")
    c_iota86 = nc.dram_tensor("c_iota86", [ND], F32, kind="ExternalInput")
    c_ones = nc.dram_tensor("c_ones", [128], F32, kind="ExternalInput")
    xoutT = nc.dram_tensor("xoutT", [HID, TOK], F32, kind="ExternalOutput")

    wg_s = nc.dram_tensor("wg_s", [HID, GU_COLS], BF16)
    wd_s = nc.dram_tensor("wd_s", [INTER, HID], BF16)
    h_s = nc.dram_tensor("h_s", [INTER, TOK], BF16)
    tg_s = nc.dram_tensor("tg_s", [NG, GU_COLS], BF16)
    tgl_s = nc.dram_tensor("tgl_s", [NG, GU_COLS], BF16)
    td_s = nc.dram_tensor("td_s", [ND, HID], BF16)
    tdl_s = nc.dram_tensor("tdl_s", [ND, HID], BF16)
    x_s = nc.dram_tensor("x_s", [HID, TOK], F32)
    mrs_s = nc.dram_tensor("mrs_s", [2, TOK], F32)

    import contextlib
    ctx = contextlib.ExitStack()
    sb = lambda name, shape, dt: ctx.enter_context(nc.sbuf_tensor(name, shape, dt))

    y_sb = sb("y_sb", [128, HC * TOK], BF16)
    wbuf = sb("wbuf", [128, 2 * HC * GW], BF16)
    stash = sb("stash", [128, 2 * TOK], F32)
    silu_b = sb("silu_b", [128, 2 * TOK], F32)
    hbuf = sb("hbuf", [128, 2 * TOK], BF16)
    qbuf = sb("qbuf", [128, 8 * 2 * GW], U8)
    wdtile = sb("wdtile", [128, 4 * GW], BF16)
    scb = sb("scb", [NG, 2 * GW], BF16)
    tgb = sb("tgb", [NG, 2 * 2 * GW], BF16)
    scd_h = sb("scd_h", [ND, HID], BF16)
    tdb = sb("tdb", [ND, 2 * 2 * 512], BF16)
    gt_gu = sb("gt_gu", [NG, HC * 128], BF16)
    g_gu = sb("g_gu", [128, HC * NG], BF16)
    gtd_b = sb("gtd_b", [ND, 2 * 128], BF16)
    gd_b = sb("gd_b", [128, 4 * ND], BF16)
    gta_i = sb("gta_i", [128, HC], I32)
    gta_f = sb("gta_f", [128, HC], F32)
    gtd_i = sb("gtd_i", [128, ICN], I32)
    gtd_f = sb("gtd_f", [128, ICN], F32)
    grep_i = sb("grep_i", [ND, 128], I32)
    grep_f = sb("grep_f", [ND, 128], F32)
    iota32c = sb("iota32c", [NG, 1], F32)
    iota86c = sb("iota86c", [ND, 1], F32)
    iota86r = sb("iota86r", [128, ND], F32)
    iota32r = sb("iota32r", [128, NG], F32)
    shift_c = sb("shift_c", [128, 1], U8)
    pw2_c = sb("pw2_c", [128, 1], F32)
    ones_c = sb("ones_c", [128, 1], F32)
    ys_sb = sb("ys_sb", [NG, TOK], BF16)
    ysl_sb = sb("ysl_sb", [NG, TOK], BF16)
    ysd_sb = sb("ysd_sb", [ND, TOK], BF16)
    ysdl_sb = sb("ysdl_sb", [ND, TOK], BF16)
    xbuf = sb("xbuf", [128, 3 * TOK], F32)
    sqb = sb("sqb", [128, 1 * TOK], F32)
    tmpb = sb("tmpb", [128, 2 * TOK], F32)
    hinb = sb("hinb", [128, 4 * TOK], BF16)
    wdb = sb("wdb", [128, 4 * 512], BF16)
    esb = sb("esb", [128, 4 * GW], BF16)
    mB = sb("mB", [128, TOK], F32)
    rsB = sb("rsB", [128, TOK], F32)
    silu_bf = silu_b[:, :].bitcast(BF16)
    tchunk_b = silu_bf[0:ND, 0:512]
    tchunk_l = silu_bf[0:ND, 512:1024]
    m_sb = mB[0:1, :]
    rs_sb = rsB[0:1, :]
    mm_sb = stash[0:1, 0:TOK]
    xbuf_u = xbuf[:, :].bitcast(U32)
    zq_u = xbuf_u[0:ND, 0:1376]
    zq_t = xbuf_u[0:ND, 1376:1440]
    scch = xbuf[0:ND, 1440:2464]
    tchunk = xbuf[0:ND, 2464:2976]

    PS = ctx.enter_context(nc.psum_tensor("PS", [128, 4096], F32))
    pg = [PS[:, 0:1024], PS[:, 1024:2048]]
    pE = [PS[:, 2048:2560], PS[:, 2560:3072]]
    pys = PS[0:NG, 3072:4096]
    pysd = PS[0:ND, 3072:4096]
    pd = [PS[:, i * 1024:(i + 1) * 1024] for i in range(4)]
    pstat_s = PS[0:1, 0:1024]
    pstat_q = PS[0:1, 1024:2048]

    bank_tok = {i: [] for i in range(8)}

    def take_banks(banks):
        out = []
        for b in banks:
            out += bank_tok[b]
            bank_tok[b] = []
        return out

    def give_banks(banks, tok):
        if tok is None:
            return
        for b in banks:
            bank_tok[b].append(tok)

    sT = Sem(nc, "sT")
    sV = Sem(nc, "sV")
    sA = Sem(nc, "sA")
    sC = Sem(nc, "sC", dma=True)
    sSCD = Sem(nc, "sSCD", dma=True)
    sQZ = Sem(nc, "sQZ", dma=True)
    sSCC = Sem(nc, "sSCC", dma=True)
    sGt = Sem(nc, "sGt", dma=True)
    sGm = Sem(nc, "sGm", dma=True)
    sMB = Sem(nc, "sMB", dma=True)
    sRSB = Sem(nc, "sRSB", dma=True)
    sGrep = Sem(nc, "sGrep", dma=True)
    sGtab = Sem(nc, "sGtab", dma=True)
    sWG = [Sem(nc, f"sWG{i}", dma=True) for i in range(2)]
    sWD = [Sem(nc, f"sWD{i}", dma=True) for i in range(2)]
    sH = [Sem(nc, f"sH{i}", dma=True) for i in range(3)]
    sX = [Sem(nc, f"sX{i}", dma=True) for i in range(3)]

    P = Plan()

    def dma(engine, sem, dst, src, waits=()):
        def fn(eng, dst=dst, src=src):
            return eng.dma_start(dst, src)
        P.op(engine, fn, waits=waits, inc=sem)
        return (sem, sem.n)

    xrot = Rot(3, nc, "sXB")
    qrot = Rot(6, nc, "sQB")
    wdqrot = Rot(2, nc, "sQD")
    wrot = Rot(2, nc, "sWB")
    scrot = Rot(2, nc, "sSB")
    tgrot = Rot(2, nc, "sTGb")
    tdrot = Rot(2, nc, "sTDb")
    hin_rot = Rot(4, nc, "sHI")
    wdin_rot = Rot(4, nc, "sWI")
    sqrot = Rot(1)
    tmprot = Rot(2)
    erot = Rot(2)
    esrot = Rot(4)
    hrot = Rot(2)
    gdrot = Rot(4)
    gtdrot = Rot(2)
    wdorot = Rot(4)
    mrs_free = []

    for dst, src_t in [
        (shift_c[:, :], bass.AP(c_shift, 0, [[1, 128], [1, 1]])),
        (pw2_c[:, :], bass.AP(c_pw2, 0, [[1, 128], [1, 1]])),
        (iota32c[:, :], bass.AP(c_iota32, 0, [[1, NG], [1, 1]])),
        (iota86c[:, :], bass.AP(c_iota86, 0, [[1, ND], [1, 1]])),
        (ones_c[:, :], bass.AP(c_ones, 0, [[1, 128], [1, 1]])),
        (iota86r[:, :], bass.AP(c_iota86, 0, [[0, 128], [1, ND]])),
        (iota32r[:, :], bass.AP(c_iota32, 0, [[0, 128], [1, NG]])),
    ]:
        dma("sync", sC, dst, src_t)
    t_const = (sC, sC.n)

    v_scdh = None
    for c0 in range(0, HID, 1024):
        t_scd = dma("sync", sSCD, scch[:, 0:1024],
                    bass.AP(sc_d, c0, [[HID, ND], [1, 1024]]),
                    waits=[v_scdh])
        P.op("vector", lambda e, c0=c0: e.tensor_scalar_mul(
                 scd_h[:, c0:c0 + 1024], scch[:, 0:1024], 0.5),
             waits=[t_scd], inc=sV)
        v_scdh = (sV, sV.n)

    t_gta = dma("scalar", sGtab, gta_i[:, :], gidx_gu_pt[:, :])
    t_gtd2 = dma("scalar", sGtab, gtd_i[:, :], gidx_d_pt[:, :])
    tab_gate = (sGtab, sGtab.n)
    P.op("vector", lambda e: e.tensor_copy(gta_f[:, :], gta_i[:, :]),
         waits=[tab_gate], inc=sV)
    P.op("vector", lambda e: e.tensor_copy(gtd_f[:, :], gtd_i[:, :]),
         waits=[tab_gate], inc=sV)
    v_gtab = (sV, sV.n)
    grep_free = None
    v_ggu_done = None
    for hc in range(HC):
        P.op("vector", lambda e, hc=hc: e.tensor_scalar(
                 g_gu[:, hc * NG:(hc + 1) * NG], iota32r[:, :],
                 gta_f[:, hc:hc + 1], None, ALU.is_equal),
             waits=[t_const, v_gtab], inc=sV)
        tg2 = dma("scalar", sGrep, grep_i[0:NG, 0:128],
                  bass.AP(gidx_gu_p, hc * 128, [[0, NG], [1, 128]]),
                  waits=[grep_free])
        P.op("vector", lambda e: e.tensor_copy(grep_f[0:NG, 0:128], grep_i[0:NG, 0:128]),
             waits=[tg2], inc=sV)
        v_cp_g = (sV, sV.n)
        P.op("vector", lambda e, hc=hc: e.tensor_scalar(
                 gt_gu[:, hc * 128:(hc + 1) * 128], grep_f[0:NG, 0:128], iota32c[:, :],
                 None, ALU.is_equal),
             waits=[t_const, v_cp_g], inc=sV)
        grep_free = (sV, sV.n)
        v_ggu_done = (sV, sV.n)

    def build_T(qz, sc_dram, scratch, scratch_l, ngrp, ncp_packed, out_cols, t_prev):
        t_last = t_prev
        sc_guard = t_prev
        tb_guard = None
        for half in range(2):
            p0 = (ncp_packed // 2) * half
            pw_half = ncp_packed - p0 if half == 1 else ncp_packed // 2
            if pw_half == 0:
                continue
            t_qz = dma("sync", sQZ, zq_u[0:ngrp, 0:pw_half],
                       bass.AP(qz, p0, [[ncp_packed, ngrp], [1, pw_half]]),
                       waits=[t_last])
            col0 = p0 * 8
            colw = pw_half * 8
            for c0 in range(col0, col0 + colw, 1024):
                cw = min(1024, col0 + colw - c0)
                t_sc = dma("sync", sSCC, scch[0:ngrp, 0:cw],
                           bass.AP(sc_dram, c0, [[out_cols, ngrp], [1, cw]]),
                           waits=[sc_guard])
                for s0 in range(0, cw, 512):
                    sw = min(512, cw - s0)
                    pw = sw // 8
                    bp = (c0 + s0) // 8 - p0
                    for k in range(8):
                        P.op("vector", lambda e, k=k, pw=pw, bp=bp, ngrp=ngrp:
                             e.tensor_scalar(zq_t[0:ngrp, 0:pw],
                                             zq_u[0:ngrp, bp:bp + pw],
                                             4 * k, 15,
                                             ALU.logical_shift_right, ALU.bitwise_and),
                             waits=[t_qz], inc=sV)
                        v_zq = (sV, sV.n)
                        P.op("vector", lambda e, k=k, pw=pw, s0=s0, ngrp=ngrp:
                             e.scalar_tensor_tensor(
                                 bass.AP(tchunk.tensor, tchunk.offset + k,
                                         [[3 * TOK, ngrp], [8, pw]]),
                                 zq_t[0:ngrp, 0:pw], 1.0,
                                 bass.AP(scch.tensor, scch.offset + s0 + k,
                                         [[3 * TOK, ngrp], [8, pw]]),
                                 ALU.add, ALU.mult),
                             waits=[t_sc, v_zq], inc=sV)
                    v_stt = (sV, sV.n)
                    P.op("vector", lambda e, ngrp=ngrp, sw=sw: e.tensor_copy(
                             tchunk_b[0:ngrp, 0:sw], tchunk[0:ngrp, 0:sw]),
                         waits=[tb_guard, v_stt], inc=sV)
                    vv = (sV, sV.n)
                    dma("gpsimd", sGt,
                        bass.AP(scratch, c0 + s0, [[out_cols, ngrp], [1, sw]]),
                        tchunk_b[0:ngrp, 0:sw], waits=[vv])
                    P.op("vector", lambda e, ngrp=ngrp, sw=sw: e.tensor_sub(
                             tchunk_l[0:ngrp, 0:sw], tchunk[0:ngrp, 0:sw],
                             tchunk_b[0:ngrp, 0:sw]),
                         waits=[tb_guard, vv], inc=sV)
                    vv2 = (sV, sV.n)
                    tb_guard = dma("gpsimd", sGt,
                                   bass.AP(scratch_l, c0 + s0,
                                           [[out_cols, ngrp], [1, sw]]),
                                   tchunk_l[0:ngrp, 0:sw], waits=[vv2])
                sc_guard = (sV, sV.n)
            t_last = (sV, sV.n)
        return t_last

    v_t1 = build_T(qz_gu, sc_gu, tg_s, tgl_s, NG, GU_COLS // 8, GU_COLS, v_scdh)
    v_t2 = build_T(qz_d, sc_d, td_s, tdl_s, ND, HID // 8, HID, v_t1)
    t_T = (sGt, sGt.n)
    for i in range(xrot.n):
        xrot.tok[i].append(v_t2)

    def x_chunk_ap(src, hc, permuted):
        if permuted:
            return bass.AP(src, hc * 128 * TOK, [[TOK, 8], [8 * TOK, 16], [1, TOK]])
        return bass.AP(src, hc * 128 * TOK, [[TOK, 128], [1, TOK]])

    def ln_stats(src, extra_waits=()):
        tk_s = take_banks([0, 1])
        tk_q = take_banks([2, 3])
        v_mm = None
        for hc in range(HC):
            slot, toks = xrot.acquire()
            w = list(toks) + [t_const] + list(extra_waits)
            t_x = dma("sync", xrot.sems[slot], xbuf[:, slot * TOK:(slot + 1) * TOK],
                      x_chunk_ap(src, hc, False), waits=w)
            qslot, qtoks = sqrot.acquire()
            P.op("scalar", lambda e, slot=slot, qslot=qslot: e.activation(
                     sqb[:, qslot * TOK:(qslot + 1) * TOK],
                     xbuf[:, slot * TOK:(slot + 1) * TOK], AF.Square),
                 waits=list(qtoks) + [t_x], inc=sA)
            v_sq = (sA, sA.n)
            for tk in range(2):
                P.op("tensor", lambda e, slot=slot, tk=tk, hc=hc: e.matmul(
                         pstat_s[:, tk * 512:(tk + 1) * 512], ones_c[:, :],
                         xbuf[:, slot * TOK + tk * 512: slot * TOK + (tk + 1) * 512],
                         start=(hc == 0), stop=(hc == HC - 1)),
                     waits=(tk_s if hc == 0 and tk == 0 else []) + [t_x], inc=None)
                P.op("tensor", lambda e, qslot=qslot, tk=tk, hc=hc: e.matmul(
                         pstat_q[:, tk * 512:(tk + 1) * 512], ones_c[:, :],
                         sqb[:, qslot * TOK + tk * 512: qslot * TOK + (tk + 1) * 512],
                         start=(hc == 0), stop=(hc == HC - 1)),
                     waits=(tk_q if hc == 0 and tk == 0 else []) + [v_sq],
                     inc=(sT if tk == 1 else None))
            v_mm = (sT, sT.n)
            xrot.release(slot, v_mm)
            sqrot.release(qslot, v_mm)
        w_m = list(mrs_free)
        mrs_free.clear()
        P.op("vector", lambda e: e.tensor_scalar_mul(m_sb, pstat_s[:, :], 1.0 / HID),
             waits=[v_mm] + w_m, inc=sV)
        v_m0 = (sV, sV.n)
        give_banks([0, 1], v_m0)
        P.op("vector", lambda e: e.tensor_scalar_mul(rs_sb, pstat_q[:, :], 1.0 / HID),
             inc=sV)
        v_r0 = (sV, sV.n)
        give_banks([2, 3], v_r0)
        P.op("vector", lambda e: e.tensor_mul(mm_sb, m_sb, m_sb),
             waits=[v_m0], inc=sV)
        v_mm2 = (sV, sV.n)
        P.op("vector", lambda e: e.scalar_tensor_tensor(
                 rs_sb, rs_sb, EPS, mm_sb, ALU.add, ALU.subtract),
             waits=[v_r0, v_mm2], inc=sV)
        v_var = (sV, sV.n)
        P.op("scalar", lambda e: e.activation(rs_sb, rs_sb, AF.Ln),
             waits=[v_var], inc=sA)
        v_ln = (sA, sA.n)
        P.op("scalar", lambda e: e.activation(rs_sb, rs_sb, AF.Exp, scale=-0.5),
             waits=[v_ln], inc=sA)
        v_rs = (sA, sA.n)
        t1 = dma("gpsimd", sGm, bass.AP(mrs_s, 0, [[TOK, 1], [1, TOK]]), m_sb,
                 waits=[v_var])
        dma("gpsimd", sGm, bass.AP(mrs_s, TOK, [[TOK, 1], [1, TOK]]), rs_sb,
            waits=[v_rs])
        pair_done = (sGm, sGm.n)
        t_m = dma("sync", sMB, mB[:, :], bass.AP(mrs_s, 0, [[0, 128], [1, TOK]]),
                  waits=[pair_done] + w_m)
        t_rs = dma("sync", sRSB, rsB[:, :], bass.AP(mrs_s, TOK, [[0, 128], [1, TOK]]),
                   waits=[pair_done])
        return t_m, t_rs

    def y_build(src, t_m, t_rs):
        last = None
        for hc in range(HC):
            slot, toks = xrot.acquire()
            t_x = dma("sync", xrot.sems[slot], xbuf[:, slot * TOK:(slot + 1) * TOK],
                      x_chunk_ap(src, hc, True), waits=toks)
            tslot, ttoks = tmprot.acquire()
            P.op("vector", lambda e, slot=slot, tslot=tslot: e.tensor_sub(
                     tmpb[:, tslot * TOK:(tslot + 1) * TOK],
                     xbuf[:, slot * TOK:(slot + 1) * TOK], mB[:, :]),
                 waits=list(ttoks) + [t_x, t_m], inc=sV)
            xrot.release(slot, (sV, sV.n))
            P.op("vector", lambda e, hc=hc, tslot=tslot: e.tensor_mul(
                     y_sb[:, hc * TOK:(hc + 1) * TOK],
                     tmpb[:, tslot * TOK:(tslot + 1) * TOK], rsB[:, :]),
                 waits=[t_rs], inc=sV)
            tmprot.release(tslot, (sV, sV.n))
            last = (sV, sV.n)
        mrs_free.append(last)
        return last

    def attn_apply(t_m, t_rs):
        for hc in range(HC):
            slot, toks = xrot.acquire()
            t_x = dma("sync", xrot.sems[slot], xbuf[:, slot * TOK:(slot + 1) * TOK],
                      x_chunk_ap(x_s, hc, False), waits=toks)
            tslot, ttoks = tmprot.acquire()
            xs = xbuf[:, slot * TOK:(slot + 1) * TOK]
            ts = tmpb[:, tslot * TOK:(tslot + 1) * TOK]
            P.op("vector", lambda e, xs=xs, ts=ts: e.tensor_sub(ts, xs, mB[:, :]),
                 waits=list(ttoks) + [t_x, t_m], inc=None)
            P.op("vector", lambda e, xs=xs, ts=ts: e.tensor_mul(ts, ts, rsB[:, :]),
                 waits=[t_rs], inc=None)
            P.op("vector", lambda e, xs=xs, ts=ts: e.tensor_add(xs, xs, ts), inc=sV)
            tmprot.release(tslot, (sV, sV.n))
            v_x2 = (sV, sV.n)
            t_st = dma("gpsimd", sX[slot], x_chunk_ap(x_s, hc, False), xs,
                       waits=[v_x2])
            xrot.release(slot, t_st)
        mrs_free.append((sV, sV.n))
        return [(s, s.n) for s in sX]

    def gu_phase(dequant, t_y_done, t_w_gate):
        pg_banks = [[0, 1], [2, 3]]
        stash_ready = [None, None]
        silu_free = [t_T, t_T]
        cp_counter = [0]
        grep_gate = [grep_free]

        tk67 = take_banks([6, 7])
        for hc in range(HC):
            for tk in range(2):
                P.op("tensor", lambda e, hc=hc, tk=tk: e.matmul(
                         pys[:, tk * 512:(tk + 1) * 512],
                         g_gu[:, hc * NG:(hc + 1) * NG],
                         y_sb[:, hc * TOK + tk * 512: hc * TOK + (tk + 1) * 512],
                         start=(hc == 0), stop=(hc == HC - 1)),
                     waits=(tk67 if hc == 0 and tk == 0 else []) + [t_y_done, v_ggu_done],
                     inc=(sT if hc == HC - 1 and tk == 1 else None))
        v_ys = (sT, sT.n)
        P.op("scalar", lambda e: e.activation(ys_sb[:, :], pys[:, :], AF.Identity,
                                              scale=-1.0),
             waits=[v_ys], inc=sA)
        v_ysb = (sA, sA.n)
        P.op("vector", lambda e: e.scalar_tensor_tensor(
                 ysl_sb[:, :], pys[:, :], -1.0, ys_sb[:, :],
                 ALU.mult, ALU.subtract),
             waits=[v_ysb], inc=sV)
        v_ysb_l = (sV, sV.n)
        ysd_gate = v_ysb_l

        def wd_begin(ic):
            gslot, gtoks = gtdrot.acquire()
            tgi = dma("scalar", sGrep, grep_i[0:ND, 0:128],
                      bass.AP(gidx_d_p, ic * 128, [[0, ND], [1, 128]]),
                      waits=gtoks + [grep_gate[0]])
            P.op("vector", lambda e: e.tensor_copy(grep_f[0:ND, 0:128],
                                                   grep_i[0:ND, 0:128]),
                 waits=[tgi], inc=sV)
            v_cp_d = (sV, sV.n)
            P.op("vector", lambda e, gslot=gslot: e.tensor_scalar(
                     gtd_b[:, gslot * 128:(gslot + 1) * 128], grep_f[0:ND, 0:128],
                     iota86c[:, :], None, ALU.is_equal),
                 waits=[t_const, v_cp_d], inc=sV)
            v_gtd = (sV, sV.n)
            grep_gate[0] = v_gtd
            st = {"gslot": gslot, "v_gtd": v_gtd, "ic": ic, "last_e": None,
                  "wd_next": 0, "wd_b": {}}
            wd_load(st)
            wd_load(st)
            return st

        def wd_load(st):
            b = st["wd_next"]
            if b >= 8:
                return
            st["wd_next"] = b + 1
            qslot, qtoks = wdqrot.acquire()
            qslot += 6
            t_q = dma("sync" if qslot % 2 == 0 else "gpsimd",
                      wdqrot.sems[qslot - 6],
                      qbuf[:, qslot * 2 * GW:(qslot + 1) * 2 * GW],
                      bass.AP(qw_d, st["ic"] * (128 * 16 * GW) + (2 * b) * GW,
                              [[16 * GW, 128], [1, 2 * GW]]),
                      waits=qtoks)
            st["wd_b"][b] = (qslot, t_q)

        def wd_unit(st, jt):
            gslot = st["gslot"]
            v_gtd = st["v_gtd"]
            ic = st["ic"]
            if True:
                if jt % 2 == 0:
                    qslot, t_q = st["wd_b"].pop(jt // 2)
                    P.op("vector", lambda e, qslot=qslot: e.tensor_tensor(
                             qbuf[:, qslot * 2 * GW:(qslot + 1) * 2 * GW],
                             qbuf[:, qslot * 2 * GW:(qslot + 1) * 2 * GW],
                             shift_c[:, 0:1].broadcast_to([128, 2 * GW]),
                             ALU.bitwise_and),
                         waits=[t_q, t_const], inc=sV)
                    st["q4"] = qslot
                    st["v_and"] = (sV, sV.n)
                qslot = st["q4"]
                eslot, etoks = erot.acquire()
                P.op("tensor", lambda e, eslot=eslot, gslot=gslot, jt=jt: e.matmul(
                         pE[eslot][0:128, 0:GW],
                         gtd_b[:, gslot * 128:(gslot + 1) * 128],
                         scd_h[:, jt * GW:(jt + 1) * GW],
                         start=True, stop=True),
                     waits=etoks + [v_gtd, v_scdh], inc=sT)
                v_e = (sT, sT.n)
                st["last_e"] = v_e
                esslot, estoks = esrot.acquire()
                P.op("scalar", lambda e, eslot=eslot, esslot=esslot: e.activation(
                         esb[:, esslot * GW:(esslot + 1) * GW], pE[eslot][0:128, 0:GW],
                         AF.Identity, scale=pw2_c[:, 0:1]),
                     waits=estoks + [v_e, t_const], inc=sA)
                v_es = (sA, sA.n)
                erot.release(eslot, v_es)
                oslot, otoks = wdorot.acquire()
                P.op("vector", lambda e, qslot=qslot, jt=jt, esslot=esslot,
                              oslot=oslot:
                     e.tensor_tensor(
                         wdtile[:, oslot * GW:(oslot + 1) * GW],
                         qbuf[:, qslot * 2 * GW + (jt % 2) * GW:
                              qslot * 2 * GW + (jt % 2 + 1) * GW],
                         esb[:, esslot * GW:(esslot + 1) * GW], ALU.mult),
                     waits=otoks + [v_es, st["v_and"]], inc=sV)
                v_w = (sV, sV.n)
                esrot.release(esslot, v_w)
                if jt % 2 == 1:
                    wdqrot.release(qslot - 6, v_w)
                    wd_load(st)
                if jt % 2 == 0:
                    st["pend_o"] = oslot
                else:
                    o0 = st.pop("pend_o")
                    assert o0 % 2 == 0 and oslot == o0 + 1
                    t_wb = dma("gpsimd", sWD[(jt // 2) % 2],
                               bass.AP(wd_s, ic * (128 * HID) + (jt - 1) * GW,
                                       [[HID, 128], [1, 2 * GW]]),
                               wdtile[:, o0 * GW:(o0 + 2) * GW], waits=[v_w])
                    wdorot.release(o0, t_wb)
                    wdorot.release(oslot, t_wb)

        def wd_end(st):
            gtdrot.release(st["gslot"], st["last_e"])

        grps = []
        for j in range(INTER // GW):
            grps.append(("g", j * GW))
            grps.append(("u", INTER + j * GW))

        v_ysd_stop = None
        n_up = 0
        npairs = INTER // GW

        def dq_begin(wslot, wtoks, c0):
            sslot, stoks = scrot.acquire()
            t_sc = dma("gpsimd", scrot.sems[sslot],
                       scb[:, sslot * GW: (sslot + 1) * GW],
                       bass.AP(sc_gu, c0, [[GU_COLS, NG], [1, GW]]), waits=stoks)
            return {"wslot": wslot, "wtoks": wtoks, "c0": c0, "sslot": sslot,
                    "t_sc": t_sc, "wready": [], "last_e": None,
                    "next_load": 0, "batches": {}}

        def dq_load(st):
            b = st["next_load"]
            if b >= HC // 2:
                return
            st["next_load"] = b + 1
            qslot, qtoks = qrot.acquire()
            t_q = dma("sync" if qslot % 2 == 0 else "gpsimd",
                      qrot.sems[qslot],
                      qbuf[:, qslot * 2 * GW:(qslot + 1) * 2 * GW],
                      bass.AP(qw_gu, (st["c0"] // GW) * (128 * HC * GW)
                              + (2 * b) * GW,
                              [[HC * GW, 128], [1, 2 * GW]]),
                      waits=qtoks + (st["wtoks"] if b == 0 else []))
            st["batches"][b] = (qslot, t_q)

        def dq_unit(st, hc):
            wslot = st["wslot"]
            c0 = st["c0"]
            sslot = st["sslot"]
            t_sc = st["t_sc"]
            if hc % 2 == 0:
                qslot, t_q = st["batches"].pop(hc // 2)
                P.op("vector", lambda e, qslot=qslot: e.tensor_tensor(
                         qbuf[:, qslot * 2 * GW:(qslot + 1) * 2 * GW],
                         qbuf[:, qslot * 2 * GW:(qslot + 1) * 2 * GW],
                         shift_c[:, 0:1].broadcast_to([128, 2 * GW]),
                         ALU.bitwise_and),
                     waits=[t_q, t_const], inc=sV)
                st["q4"] = qslot
                st["v_and"] = (sV, sV.n)
            qslot = st["q4"]
            eslot, etoks = erot.acquire()
            P.op("tensor", lambda e, eslot=eslot, hc=hc, sslot=sslot:
                 e.matmul(pE[eslot][0:128, 0:GW],
                          gt_gu[:, hc * 128:(hc + 1) * 128],
                          scb[:, sslot * GW: (sslot + 1) * GW],
                          start=True, stop=True),
                 waits=etoks + [t_sc, v_ggu_done], inc=sT)
            v_e = (sT, sT.n)
            st["last_e"] = v_e
            esslot, estoks = esrot.acquire()
            P.op("scalar", lambda e, eslot=eslot, esslot=esslot: e.activation(
                     esb[:, esslot * GW:(esslot + 1) * GW],
                     pE[eslot][0:128, 0:GW],
                     AF.Identity, scale=pw2_c[:, 0:1]),
                 waits=estoks + [v_e, t_const], inc=sA)
            v_es = (sA, sA.n)
            erot.release(eslot, v_es)
            P.op("vector", lambda e, qslot=qslot, esslot=esslot, wslot=wslot,
                          hc=hc:
                 e.tensor_tensor(
                     wbuf[:, wslot * (HC * GW) + hc * GW:
                          wslot * (HC * GW) + (hc + 1) * GW],
                     qbuf[:, qslot * 2 * GW + (hc % 2) * GW:
                          qslot * 2 * GW + (hc % 2 + 1) * GW],
                     esb[:, esslot * GW:(esslot + 1) * GW], ALU.mult),
                 waits=[v_es, st["v_and"]], inc=sV)
            v_w = (sV, sV.n)
            esrot.release(esslot, v_w)
            if hc % 2 == 1:
                qrot.release(qslot, v_w)
            st["wready"].append(v_w)

        def dq_end(st):
            scrot.release(st["sslot"], st["last_e"])
            wslot = st["wslot"]
            for hb in (0, HC // 2):
                dma("gpsimd", sWG[wslot],
                    bass.AP(wg_s, (st["c0"] // GW) * (128 * HC * GW) + hb * GW,
                            [[HC * GW, 128], [1, (HC // 2) * GW]]),
                    wbuf[:, wslot * (HC * GW) + hb * GW:
                         wslot * (HC * GW) + (hb + HC // 2) * GW],
                    waits=[st["wready"][-1]])
            return st["wready"], (sWG[st["wslot"]], sWG[st["wslot"]].n)

        wd_queue = []
        wd_active = []

        def drain_wd(budget):
            while budget > 0:
                if not wd_active:
                    if not wd_queue:
                        return
                    wd_active.append([wd_begin(wd_queue.pop(0)), 0])
                st, jt = wd_active[0]
                for _ in range(2):
                    wd_unit(st, jt)
                    jt += 1
                budget -= 2
                if jt == HID // GW:
                    wd_end(st)
                    wd_active.pop(0)
                else:
                    wd_active[0][1] = jt

        pend = {}
        pend_state = None
        pend_hc = 0
        for gi, (side, c0) in enumerate(grps):
            ncp = GW // 128
            if dequant:
                if gi == 0:
                    ws0, wt0 = wrot.acquire()
                    st0 = dq_begin(ws0, wt0, grps[0][1])
                    for _ in range(4):
                        dq_load(st0)
                    for hc0 in range(HC):
                        dq_unit(st0, hc0)
                        if hc0 % 2 == 1:
                            dq_load(st0)
                    pend[0] = (ws0,) + dq_end(st0)
                if gi + 1 < len(grps):
                    wsN, wtN = wrot.acquire()
                    pend_state = dq_begin(wsN, wtN, grps[gi + 1][1])
                    pend_hc = 0
                    for _ in range(4):
                        dq_load(pend_state)
                else:
                    pend_state = None
                wslot, wready, wb_use_done = pend.pop(gi)
            else:
                wslot, wtoks = wrot.acquire()
                wb_use_done = None
                for hb in range(0, HC, 8):
                    dma("sync", wrot.sems[wslot],
                        wbuf[:, wslot * (HC * GW) + hb * GW:
                             wslot * (HC * GW) + (hb + 8) * GW],
                        bass.AP(wg_s, (c0 // GW) * (128 * HC * GW) + hb * GW,
                                [[HC * GW, 128], [1, 8 * GW]]),
                        waits=(wtoks + list(t_w_gate) if hb == 0 else []))
                wready = [(wrot.sems[wslot], wrot.sems[wslot].n)]
            tslot, ttoks = tgrot.acquire()
            dma("scalar", tgrot.sems[tslot],
                tgb[:, tslot * 2 * GW: tslot * 2 * GW + GW],
                bass.AP(tg_s, c0, [[GU_COLS, NG], [1, GW]]),
                waits=ttoks + [t_T])
            t_tg = dma("scalar", tgrot.sems[tslot],
                       tgb[:, tslot * 2 * GW + GW: (tslot + 1) * 2 * GW],
                       bass.AP(tgl_s, c0, [[GU_COLS, NG], [1, GW]]), waits=[])
            grp_last = None
            for cp in range(ncp):
                pgi = cp_counter[0] % 2
                cp_counter[0] += 1
                tk_banks = take_banks(pg_banks[pgi])
                for hc in range(HC):
                    for tk in range(2):
                        w = []
                        if hc == 0 and tk == 0:
                            w += tk_banks
                            w.append(t_y_done)
                        w.append(wready[hc] if dequant else wready[0])
                        P.op("tensor", lambda e, pgi=pgi, tk=tk, hc=hc, wslot=wslot,
                                      cp=cp:
                             e.matmul(pg[pgi][:, tk * 512:(tk + 1) * 512],
                                      wbuf[:, wslot * (HC * GW) + hc * GW + cp * 128:
                                           wslot * (HC * GW) + hc * GW + (cp + 1) * 128],
                                      y_sb[:, hc * TOK + tk * 512:
                                           hc * TOK + (tk + 1) * 512],
                                      start=(hc == 0), stop=False),
                             waits=w, inc=None)
                    if dequant:
                        step = (cp * HC + hc + 1) // ncp
                        while pend_state is not None and pend_hc < step:
                            dq_unit(pend_state, pend_hc)
                            if pend_hc % 2 == 1:
                                dq_load(pend_state)
                            pend_hc += 1
                        if hc % 4 == 3:
                            drain_wd(2)
                for tk in range(2):
                    P.op("tensor", lambda e, pgi=pgi, tk=tk, tslot=tslot, cp=cp:
                         e.matmul(pg[pgi][:, tk * 512:(tk + 1) * 512],
                                  tgb[:, tslot * 2 * GW + cp * 128:
                                       tslot * 2 * GW + (cp + 1) * 128],
                                  ys_sb[:, tk * 512:(tk + 1) * 512],
                                  start=False, stop=False),
                         waits=[t_tg, v_ysb], inc=None)
                    P.op("tensor", lambda e, pgi=pgi, tk=tk, tslot=tslot, cp=cp:
                         e.matmul(pg[pgi][:, tk * 512:(tk + 1) * 512],
                                  tgb[:, tslot * 2 * GW + cp * 128:
                                       tslot * 2 * GW + (cp + 1) * 128],
                                  ysl_sb[:, tk * 512:(tk + 1) * 512],
                                  start=False, stop=False),
                         waits=[v_ysb_l], inc=None)
                    P.op("tensor", lambda e, pgi=pgi, tk=tk, tslot=tslot, cp=cp:
                         e.matmul(pg[pgi][:, tk * 512:(tk + 1) * 512],
                                  tgb[:, tslot * 2 * GW + GW + cp * 128:
                                       tslot * 2 * GW + GW + (cp + 1) * 128],
                                  ys_sb[:, tk * 512:(tk + 1) * 512],
                                  start=False, stop=True),
                         waits=[], inc=sT)
                v_cp = (sT, sT.n)
                grp_last = v_cp
                if side == "g":
                    P.op("scalar", lambda e, pgi=pgi, cp=cp: e.copy(
                             stash[:, cp * TOK:(cp + 1) * TOK], pg[pgi][:, :]),
                         waits=[v_cp], inc=sA)
                    give_banks(pg_banks[pgi], (sA, sA.n))
                    stash_ready[cp] = (sA, sA.n)
                else:
                    ic = (c0 - INTER) // 128 + cp
                    sl = n_up % 2
                    n_up += 1
                    P.op("scalar", lambda e, cp=cp, sl=sl: e.activation(
                             silu_b[:, sl * TOK:(sl + 1) * TOK],
                             stash[:, cp * TOK:(cp + 1) * TOK], AF.Silu),
                         waits=[stash_ready[cp], silu_free[sl]], inc=sA)
                    v_silu = (sA, sA.n)
                    hslot, htoks = hrot.acquire()
                    P.op("vector", lambda e, pgi=pgi, hslot=hslot, sl=sl:
                         e.tensor_tensor(
                             hbuf[:, hslot * TOK:(hslot + 1) * TOK],
                             silu_b[:, sl * TOK:(sl + 1) * TOK],
                             pg[pgi][:, :], ALU.mult),
                         waits=htoks + [v_silu, v_cp], inc=sV)
                    v_h = (sV, sV.n)
                    silu_free[sl] = v_h
                    give_banks(pg_banks[pgi], v_h)
                    gdslot, gdtoks = gdrot.acquire()
                    P.op("vector", lambda e, gdslot=gdslot, ic=ic: e.tensor_scalar(
                             gd_b[:, gdslot * ND:(gdslot + 1) * ND], iota86r[:, :],
                             gtd_f[:, ic:ic + 1], None, ALU.is_equal),
                         waits=gdtoks + [t_const, v_gtab], inc=sV)
                    v_gd = (sV, sV.n)
                    for tk in range(2):
                        P.op("tensor", lambda e, gdslot=gdslot, hslot=hslot, tk=tk,
                                      ic=ic:
                             e.matmul(pysd[:, tk * 512:(tk + 1) * 512],
                                      gd_b[:, gdslot * ND:(gdslot + 1) * ND],
                                      hbuf[:, hslot * TOK + tk * 512:
                                           hslot * TOK + (tk + 1) * 512],
                                      start=(ic == 0), stop=(ic == ICN - 1)),
                             waits=[v_gd, v_h] + ([ysd_gate] if ic == 0 and tk == 0
                                                  else []),
                             inc=(sT if tk == 1 else None))
                    v_ysd_mm = (sT, sT.n)
                    gdrot.release(gdslot, v_ysd_mm)
                    hrot.release(hslot, v_ysd_mm)
                    inter0 = c0 - INTER + cp * 128
                    t_h = dma("gpsimd", sH[hslot],
                              bass.AP(h_s, inter0 * TOK,
                                      [[TOK, 16], [16 * TOK, 8], [1, TOK]]),
                              hbuf[:, hslot * TOK:(hslot + 1) * TOK], waits=[v_h])
                    hrot.release(hslot, t_h)
                    if ic == ICN - 1:
                        v_ysd_stop = v_ysd_mm
            tgrot.release(tslot, grp_last)
            wrot.release(wslot, grp_last)
            wrot.release(wslot, wb_use_done)
            if dequant and pend_state is not None:
                while pend_hc < HC:
                    dq_unit(pend_state, pend_hc)
                    if pend_hc % 2 == 1:
                        dq_load(pend_state)
                    pend_hc += 1
                pend[gi + 1] = (pend_state["wslot"],) + dq_end(pend_state)
                pend_state = None
            if dequant and side == "u":
                k = gi // 2
                lo = (k * ICN) // npairs
                hi = ((k + 1) * ICN) // npairs
                wd_queue.extend(range(lo, hi))
        if dequant:
            drain_wd(10 ** 9)
        P.op("scalar", lambda e: e.activation(ysd_sb[:, :], pysd[:, :], AF.Identity,
                                              scale=-0.5),
             waits=[v_ysd_stop], inc=sA)
        v_ysd_h = (sA, sA.n)
        P.op("vector", lambda e: e.scalar_tensor_tensor(
                 ysdl_sb[:, :], pysd[:, :], -0.5, ysd_sb[:, :],
                 ALU.mult, ALU.subtract),
             waits=[v_ysd_h], inc=sV)
        v_ysd_sb = (sV, sV.n)
        give_banks([6, 7], v_ysd_sb)
        for i, toks in enumerate(erot.tok):
            for t in toks:
                bank_tok[4 + i].append(t)
            erot.tok[i] = []
        erot.i = 0
        h_gate = [(s, s.n) for s in sH]
        w_gate = [(s, s.n) for s in sWG] + [(s, s.n) for s in sWD]
        return v_ysd_sb, h_gate, w_gate

    def down_phase(v_ysd_sb, h_gate, w_gate, src, dst):
        pd_banks = [[0, 1], [2, 3], [4, 5], [6, 7]]
        for hg in range(8):
            tslot, ttoks = tdrot.acquire()
            dma("scalar", tdrot.sems[tslot], tdb[:, tslot * 1024: tslot * 1024 + 512],
                bass.AP(td_s, hg * 512, [[HID, ND], [1, 512]]),
                waits=ttoks + [t_T])
            t_tdsl = dma("scalar", tdrot.sems[tslot],
                         tdb[:, tslot * 1024 + 512: (tslot + 1) * 1024],
                         bass.AP(tdl_s, hg * 512, [[HID, ND], [1, 512]]), waits=[])
            for ic in range(ICN):
                hslot, htoks = hin_rot.acquire()
                t_hin = dma("sync", hin_rot.sems[hslot],
                            hinb[:, hslot * TOK:(hslot + 1) * TOK],
                            bass.AP(h_s, ic * 128 * TOK,
                                    [[TOK, 128], [1, TOK]]),
                            waits=htoks + (h_gate if hg == 0 and ic == 0 else []))
                wslot, wtoks = wdin_rot.acquire()
                t_wd = dma("sync", wdin_rot.sems[wslot],
                           wdb[:, wslot * 512:(wslot + 1) * 512],
                           bass.AP(wd_s, (ic * 128) * HID + hg * 512,
                                   [[HID, 128], [1, 512]]),
                           waits=wtoks + (w_gate if hg == 0 and ic == 0 else []))
                for c in range(4):
                    for tk in range(2):
                        w = [t_hin, t_wd]
                        if ic == 0 and tk == 0:
                            w += take_banks(pd_banks[c])
                        P.op("tensor", lambda e, c=c, tk=tk, wslot=wslot, hslot=hslot,
                                      ic=ic:
                             e.matmul(pd[c][:, tk * 512:(tk + 1) * 512],
                                      wdb[:, wslot * 512 + c * 128:
                                           wslot * 512 + (c + 1) * 128],
                                      hinb[:, hslot * TOK + tk * 512:
                                           hslot * TOK + (tk + 1) * 512],
                                      start=(ic == 0), stop=False),
                             waits=w, inc=(sT if c == 3 and tk == 1 else None))
                v_ic = (sT, sT.n)
                hin_rot.release(hslot, v_ic)
                wdin_rot.release(wslot, v_ic)
            v_c = None
            for c in range(4):
                for tk in range(2):
                    P.op("tensor", lambda e, c=c, tk=tk, tslot=tslot: e.matmul(
                             pd[c][:, tk * 512:(tk + 1) * 512],
                             tdb[:, tslot * 1024 + c * 128:
                                  tslot * 1024 + (c + 1) * 128],
                             ysd_sb[:, tk * 512:(tk + 1) * 512],
                             start=False, stop=False),
                         waits=[t_tdsl, v_ysd_sb], inc=None)
                    P.op("tensor", lambda e, c=c, tk=tk, tslot=tslot: e.matmul(
                             pd[c][:, tk * 512:(tk + 1) * 512],
                             tdb[:, tslot * 1024 + c * 128:
                                  tslot * 1024 + (c + 1) * 128],
                             ysdl_sb[:, tk * 512:(tk + 1) * 512],
                             start=False, stop=False),
                         waits=[], inc=None)
                    P.op("tensor", lambda e, c=c, tk=tk, tslot=tslot: e.matmul(
                             pd[c][:, tk * 512:(tk + 1) * 512],
                             tdb[:, tslot * 1024 + 512 + c * 128:
                                  tslot * 1024 + 512 + (c + 1) * 128],
                             ysd_sb[:, tk * 512:(tk + 1) * 512],
                             start=False, stop=True),
                         waits=[], inc=(sT if tk == 1 else None))
                v_c = (sT, sT.n)
                hc = hg * 4 + c
                slot, toks = xrot.acquire()
                t_x = dma("sync", xrot.sems[slot], xbuf[:, slot * TOK:(slot + 1) * TOK],
                          x_chunk_ap(src, hc, False), waits=toks)
                P.op("vector", lambda e, c=c, slot=slot: e.tensor_tensor(
                         xbuf[:, slot * TOK:(slot + 1) * TOK],
                         pd[c][:, :], xbuf[:, slot * TOK:(slot + 1) * TOK], ALU.add),
                     waits=[v_c, t_x], inc=sV)
                v_add = (sV, sV.n)
                give_banks(pd_banks[c], v_add)
                t_st = dma("gpsimd", sX[slot], x_chunk_ap(dst, hc, False),
                           xbuf[:, slot * TOK:(slot + 1) * TOK], waits=[v_add])
                xrot.release(slot, t_st)
            tdrot.release(tslot, v_c)
        return [(s, s.n) for s in sX]

    t_m1, t_rs1 = ln_stats(xT)
    t_y1 = y_build(xT, t_m1, t_rs1)
    v_ysd1, h_gate1, w_gate1 = gu_phase(True, t_y1, None)
    x_gate1 = down_phase(v_ysd1, h_gate1, w_gate1, xT, x_s)

    t_m2, t_rs2 = ln_stats(x_s, extra_waits=x_gate1)
    x_gate2 = attn_apply(t_m2, t_rs2)

    t_m3, t_rs3 = ln_stats(x_s, extra_waits=x_gate2)
    t_y3 = y_build(x_s, t_m3, t_rs3)
    v_ysd2, h_gate2, _ = gu_phase(False, t_y3, w_gate1)
    down_phase(v_ysd2, h_gate2, w_gate1, x_s, xoutT)

    final_waits = ([(sGt, sGt.n), (sGm, sGm.n)] +
                   [(s, s.n) for s in sX] + [(s, s.n) for s in sH] +
                   [(s, s.n) for s in sWG] + [(s, s.n) for s in sWD])
    P.op("gpsimd", lambda e: e.nop(), waits=final_waits, inc=None)

    with nc.Block() as block:
        @block.sync
        def _(eng):
            P.emit("sync", eng)

        @block.scalar
        def _(eng):
            P.emit("scalar", eng)

        @block.vector
        def _(eng):
            P.emit("vector", eng)

        @block.tensor
        def _(eng):
            P.emit("tensor", eng)

        @block.gpsimd
        def _(eng):
            P.emit("gpsimd", eng)

    return nc


def kernel(L0_attn_x, L0_ffn_qweight_gu, L0_ffn_qzeros_gu, L0_ffn_scales_gu,
           L0_ffn_g_idx_gu, L0_ffn_qweight_down, L0_ffn_qzeros_down,
           L0_ffn_scales_down, L0_ffn_g_idx_down):
    x = np.asarray(L0_attn_x, dtype=np.float32).reshape(NCORES * TOK, HID)
    _v = np.asarray(L0_ffn_qweight_gu, dtype=np.uint32).reshape(32, 16, 86, 256)
    _v8 = np.repeat(_v.view(np.uint8).reshape(32, 16, 86, 256, 4), 2, axis=4)
    qw_gu_a = np.ascontiguousarray(_v8.transpose(2, 4, 1, 0, 3)).reshape(-1)
    qz_gu_a = np.ascontiguousarray(np.asarray(L0_ffn_qzeros_gu, dtype=np.uint32))
    sc_gu_a = np.ascontiguousarray(np.asarray(L0_ffn_scales_gu, dtype=np.float32))
    _w = np.asarray(L0_ffn_qweight_down, dtype=np.uint32).reshape(86, 16, 16, 256)
    _w8 = np.repeat(_w.view(np.uint8).reshape(86, 16, 16, 256, 4), 2, axis=4)
    qw_d_a = np.ascontiguousarray(_w8.transpose(0, 4, 1, 2, 3)).reshape(-1)
    qz_d_a = np.ascontiguousarray(np.asarray(L0_ffn_qzeros_down, dtype=np.uint32))
    sc_d_a = np.ascontiguousarray(np.asarray(L0_ffn_scales_down, dtype=np.float32))
    gidx_gu = np.asarray(L0_ffn_g_idx_gu, dtype=np.int32)
    gidx_d = np.asarray(L0_ffn_g_idx_down, dtype=np.int32)

    gidx_gu_p = np.ascontiguousarray(gidx_gu.reshape(HC, 128)[:, PERM].reshape(-1))
    gidx_d_p = np.ascontiguousarray(gidx_d.reshape(ICN, 128)[:, PERM].reshape(-1))
    gidx_gu_pt = np.ascontiguousarray(gidx_gu_p.reshape(HC, 128).T)
    gidx_d_pt = np.ascontiguousarray(gidx_d_p.reshape(ICN, 128).T)

    _nib = (np.arange(128) // 16) % 2
    c_shift = np.where(_nib == 0, 0x0F, 0xF0).astype(np.uint8)
    c_pw2 = np.ldexp(np.float32(1.0), -4 * _nib.astype(np.int32)).astype(np.float32)
    c_iota32 = np.arange(NG, dtype=np.float32)
    c_iota86 = np.arange(ND, dtype=np.float32)
    c_ones = np.ones(128, dtype=np.float32)

    nc = build_program()
    in_maps = []
    for k in range(NCORES):
        xk = np.ascontiguousarray(x[k * TOK:(k + 1) * TOK, :].T)
        in_maps.append({
            "xT": xk, "qw_gu": qw_gu_a, "qz_gu": qz_gu_a, "sc_gu": sc_gu_a,
            "qw_d": qw_d_a, "qz_d": qz_d_a, "sc_d": sc_d_a,
            "gidx_gu_p": gidx_gu_p, "gidx_d_p": gidx_d_p,
            "gidx_gu_pt": gidx_gu_pt, "gidx_d_pt": gidx_d_pt,
            "c_shift": c_shift, "c_pw2": c_pw2,
            "c_iota32": c_iota32, "c_iota86": c_iota86,
            "c_ones": c_ones,
        })
    res = run_bass_kernel_spmd(nc, in_maps, core_ids=list(range(NCORES)))
    out = np.empty((NCORES * TOK, HID), dtype=np.float32)
    for k in range(NCORES):
        out[k * TOK:(k + 1) * TOK, :] = np.asarray(res.results[k]["xoutT"]).T
    return out.reshape(2, HID, HID)
